# version 15 (final)
# baseline (speedup 1.0000x reference)
import numpy as np
import ml_dtypes
import concourse.bass as bass
import concourse.mybir as mybir
from concourse.bass_utils import run_bass_kernel_spmd

F32 = mybir.dt.float32
BF16 = mybir.dt.bfloat16
AF = mybir.ActivationFunctionType
OP = mybir.AluOpType
AX = mybir.AxisListType
BF = ml_dtypes.bfloat16

H, P, N, G, KCV = 64, 64, 128, 8, 4
D = 4096
I = 4096
CONV = 6144
QC = 128
EPS = 1e-5
NKC = 32
NHBC = 48
NCORES = 8
NHB = 16
SCALING = 2.0

STREAM_OF = {"sync": "sync", "scalar": "scalar", "vector": "vector",
             "tensor": "tensor", "gpsimd": "gpsimd", "scalq": "scalar"}
ENG_SEM = {"sync": "LDS", "scalq": "LDA", "tensor": "TEN", "vector": "VEC",
           "scalar": "SCL", "gpsimd": "POO"}
DMA_SEMS = ("LDS", "LDA", "LDG")


class Sched:
    def __init__(self):
        self.ops = []
        self.last_write = {}
        self.readers = {}

    def add(self, eng, fn, reads=(), writes=(), sem=None, amt=None):
        idx = len(self.ops)
        deps = set()
        for r in reads:
            w = self.last_write.get(r)
            if w is not None:
                deps.add(w)
        for wres in writes:
            w = self.last_write.get(wres)
            if w is not None:
                deps.add(w)
            for rd in self.readers.get(wres, ()):
                deps.add(rd)
        if sem is None:
            sem = ENG_SEM[eng]
        if amt is None:
            amt = 16 if sem in DMA_SEMS else 1
        self.ops.append({"i": idx, "eng": eng, "fn": fn, "deps": deps,
                         "sem": sem, "amt": amt, "waited": False})
        for wres in writes:
            self.last_write[wres] = idx
            self.readers[wres] = []
        for r in reads:
            self.readers.setdefault(r, []).append(idx)
        return idx

    def finalize(self, block, sems):
        ops = self.ops
        NL = 8
        lane_cnt = {}
        lane_prev = {}
        for op in ops:
            if op["sem"] in DMA_SEMS:
                q = op["sem"]
                i = lane_cnt.get(q, 0)
                lane_cnt[q] = i + 1
                lane = f"{q}{i % NL}"
                op["sem"] = lane
                prev = lane_prev.get(lane)
                if prev is not None:
                    op["lane_dep"] = prev
                lane_prev[lane] = op["i"]
        for op in ops:
            pruned = set()
            for d in op["deps"]:
                dop = ops[d]
                if dop["eng"] == "tensor" and op["eng"] == "tensor":
                    continue
                pruned.add(d)
            op["deps"] = pruned
            if "lane_dep" in op:
                op["deps"].add(op["lane_dep"])
        for op in ops:
            for d in op["deps"]:
                ops[d]["waited"] = True
            if op["sem"][:3] in DMA_SEMS or op["sem"].startswith("CC"):
                op["waited"] = True
        cnt = {}
        for op in ops:
            if op["waited"]:
                cnt[op["sem"]] = cnt.get(op["sem"], 0) + op["amt"]
                op["val"] = cnt[op["sem"]]
            else:
                op["val"] = None
        streams = {e: [] for e in ["sync", "scalar", "vector", "tensor",
                                   "gpsimd"]}
        for op in ops:
            streams[STREAM_OF[op["eng"]]].append(op)
        for sname, sops in streams.items():
            hw = {}
            for op in sops:
                byv = {}
                for d in op["deps"]:
                    dop = ops[d]
                    if dop["val"] is None:
                        continue
                    s, v = dop["sem"], dop["val"]
                    if hw.get(s, 0) >= v:
                        continue
                    byv[s] = max(byv.get(s, 0), v)
                op["waits"] = sorted(byv.items())
                for s, v in byv.items():
                    hw[s] = v

        def make(sops):
            def body(e):
                for op in sops:
                    for s, v in op["waits"]:
                        e.wait_ge(sems[s], v)
                    inst = op["fn"](e)
                    if op["waited"]:
                        inst.then_inc(sems[op["sem"]], op["amt"])
            return body
        for sname in ["sync", "scalar", "vector", "tensor", "gpsimd"]:
            getattr(block, sname)(make(streams[sname]))
        return len(ops), cnt


class Arena:

    def __init__(self, sched, tensor_ap, width_f32):
        self.S = sched
        self.t = tensor_ap
        self.tb = tensor_ap.bitcast(BF16)
        self.w = width_f32
        self.off = 0
        self.marks = []
        self.used = set()
        self.fence_idx = None

    def push(self):
        self.marks.append(self.off)

    def f32(self, parts, cols, key):
        assert self.off + cols <= self.w, f"arena overflow {self.off}+{cols}"
        v = self.t[0:parts, self.off:self.off + cols]
        self.off += cols
        self._reg(key)
        return v

    def bf16(self, parts, cols, key):
        colsf = (cols + 1) // 2
        assert self.off + colsf <= self.w, f"arena overflow {self.off}+{colsf}"
        v = self.tb[0:parts, 2 * self.off:2 * self.off + cols]
        self.off += colsf
        self._reg(key)
        return v

    def pairf32(self, cols, key_top, key_bot):
        assert self.off + cols <= self.w, "arena overflow (pair)"
        top = self.t[0:64, self.off:self.off + cols]
        bot = self.t[64:128, self.off:self.off + cols]
        self.off += cols
        self._reg(key_top)
        self._reg(key_bot)
        return top, bot

    def _reg(self, key):
        keys = key if isinstance(key, (tuple, list)) else (key,)
        for k in keys:
            self.used.add(k)
            if self.fence_idx is not None and k not in self.S.last_write:
                self.S.last_write[k] = self.fence_idx

    def pop_fence(self):
        self.off = self.marks.pop()
        used = sorted(self.used)
        idx = self.S.add("vector", lambda e: e.engine_nop(), reads=(),
                         writes=used)
        self.used = set()
        self.fence_idx = idx


def _ttiles(T):
    out, t = [], 0
    while t < T:
        w = min(512, T - t)
        out.append((t, w))
        t += w
    return out


def build_program(T, dbg=False):
    NQ = T // QC
    TT = _ttiles(T)
    T16 = T // 128

    nc = bass.Bass("TRN2", target_bir_lowering=False)

    def inp(name, shape, dt=F32):
        return nc.dram_tensor(name, shape, dt, kind="ExternalInput").ap()

    x_fm = inp("x_fm", [D, T])
    w_hbcdt = inp("w_hbcdt", [49, 128, NKC, 128])
    w_gate = inp("w_gate", [NKC, 128, I])
    w_out = inp("w_out", [NKC, 128, NKC, 128])
    w_ffn = inp("w_ffn", [NKC, 128, NKC, 128])
    w_la = inp("w_la", [128, NKC, 128])
    w_lb = inp("w_lb", [128, I])
    w_rt = inp("w_rt", [128, NKC * 8])
    diagw = inp("diagw", [NHBC, KCV, 128, 128])
    cwcols = inp("cwcols", [128, NHBC * KCV])
    cbias = inp("cbias", [128, NHBC])
    dtb = inp("dtb", [H, 1])
    alog = inp("alog", [H, 1])
    dvec_row = inp("dvec_row", [1, H])
    nw_cols = inp("nw_cols", [128, NKC])
    fb_cols = inp("fb_cols", [128, NKC])
    rb_row = inp("rb_row", [1, 8])
    identity = inp("identity", [128, 128])
    tri01 = inp("tri01", [128, 128])
    ind12 = inp("ind12", [12, 512])
    e816 = inp("e816", [8, 128])
    tokmask = inp("tokmask", [1, T])
    chmh = inp("chmh", [H, 64])
    chm1mh = inp("chm1mh", [H, 64])
    bndc = inp("bndc", [4, 8])
    bmask = inp("bmask", [2, 1])
    bsel_i = inp("bsel", [128, 64])

    out_fm = nc.dram_tensor("out_fm", [D, T], F32, kind="ExternalOutput").ap()

    xbc_s = nc.dram_tensor("xbc_s", [CONV, T], BF16).ap()
    gate_s = nc.dram_tensor("gate_s", [T, I], BF16).ap()
    xtm_s = nc.dram_tensor("xtm_s", [NQ, 128, I], BF16).ap()
    x2_s = nc.dram_tensor("x2_s", [D, T], F32).ap()
    lsp3 = nc.dram_tensor("lsp3_s", [3, H, T], BF16).ap()
    nrsp3 = nc.dram_tensor("nrsp3_s", [3, H, T], BF16).ap()
    lsp_s = [lsp3[i] for i in range(3)]
    nrsp_s = [nrsp3[i] for i in range(3)]
    elend_s = nc.dram_tensor("elend_s", [NQ, H], F32).ap()
    elt_s = nc.dram_tensor("elt_s", [NQ, 128, 64], BF16)
    coefb_s = nc.dram_tensor("coefb_s", [8, H], F32).ap()
    bnd_in = nc.dram_tensor("bnd_in", [4, CONV], BF16)
    bnd_out = nc.dram_tensor("bnd_out", [32, CONV], BF16)
    st_in = nc.dram_tensor("st_in", [128, H * P], BF16)
    st_out = nc.dram_tensor("st_out", [8 * 128, H * P], BF16)
    ca_in = nc.dram_tensor("ca_in", [H, 1], F32)
    ca_out = nc.dram_tensor("ca_out", [8 * H, 1], F32)
    q_in = nc.dram_tensor("q_in", [128, 64], F32)
    q_out = nc.dram_tensor("q_out", [128, 64], F32)

    from contextlib import ExitStack
    with ExitStack() as g:
        def sbt(name, shape, dt=F32):
            return g.enter_context(nc.sbuf_tensor(name, shape, dt))[:]

        panel = sbt("panel", [128, NKC * T], BF16)
        AR_W = 12544
        S_ = Sched()
        A = Arena(S_, sbt("arena", [128, AR_W], F32), AR_W)

        ident_bf = sbt("ident_bf", [128, 128], BF16)
        ident_f = sbt("ident_f", [128, 128], F32)
        tri_sb = sbt("tri_sb", [128, 128], F32)
        ind12_sb = sbt("ind12_sb", [12, 512], BF16)
        ones1f = sbt("ones1f", [1, 128], F32)
        ones1b = sbt("ones1b", [1, 128], BF16)
        ones3b = sbt("ones3b", [3, 128], BF16)
        onescol = sbt("onescol", [128, 1], F32)
        onescol_b = sbt("onescol_b", [128, 1], BF16)
        ones64x = sbt("ones64x", [H, QC], F32)
        cb_sb = sbt("cb_sb", [128, NHBC], F32)
        cw_sb = sbt("cw_sb", [128, NHBC * KCV], F32)
        dtb_sb = sbt("dtb_sb", [H, 1], F32)
        negA = sbt("negA", [H, 1], F32)
        nw_sb = sbt("nw_sb", [128, NKC], F32)
        fb_sb = sbt("fb_sb", [128, NKC], F32)
        dvrow_sb = sbt("dvrow_sb", [1, H], F32)
        rb_sb = sbt("rb_sb", [1, 8], F32)
        e816_sb = sbt("e816_sb", [8, 128], F32)
        bm_sb = sbt("bm_sb", [2, 1], F32)
        Sst = sbt("Sst", [128, H * P], F32)
        S_bf = sbt("S_bf", [128, H * P], BF16)
        D_b = sbt("D_b", [128, H], F32)
        pf_sb = sbt("pf_sb", [128, NHBC * 3], BF16)
        pxbc = sbt("pxbc", [128, NHBC * 3], BF16)
        qacc = sbt("qacc", [128, NKC], F32)

        PF = [g.enter_context(nc.psum_tensor(f"PF{i}", [128, 1024], F32))[:]
              for i in range(3)]
        PSA = [PF[i // 2][:, (i % 2) * 512:(i % 2) * 512 + 512]
               for i in range(6)]
        PBA = [g.enter_context(nc.psum_tensor(f"PBT{i}", [128, 1024],
                                              BF16))[:]
               for i in range(2)]

        sems = {}
        semnames = ["TEN", "VEC", "SCL", "POO", "CC1", "CC2a", "CC2b", "CC3"]
        for q in ["LDS", "LDA", "LDG"]:
            semnames += [f"{q}{i}" for i in range(8)]
        for s in semnames:
            sems[s] = g.enter_context(nc.semaphore(s))

        block = g.enter_context(nc.Block())

        def mm(out_ap, lhsT, rhs, reads, wkey, start=True, stop=True):
            S_.add("tensor",
                   lambda e, o=out_ap, l=lhsT, r=rhs, st=start, sp=stop:
                   e.matmul(o, l, r, start=st, stop=sp),
                   reads=reads, writes=[wkey])

        def tr(out_ap, in_ap, ident, reads, wkey):
            S_.add("tensor",
                   lambda e, o=out_ap, i_=in_ap, d=ident:
                   e.transpose(o, i_, d),
                   reads=reads, writes=[wkey])

        def vec(fn, reads, writes):
            S_.add("vector", fn, reads=reads, writes=writes)

        def pool(fn, reads, writes):
            S_.add("gpsimd", fn, reads=reads, writes=writes, sem="POO",
                   amt=1)

        def scl(fn, reads, writes):
            S_.add("scalar", fn, reads=reads, writes=writes)

        def dma(q, fn, reads, writes):
            sm = {"sync": "LDS", "scalq": "LDA", "gpsimd": "LDG"}[q]
            S_.add(q, fn, reads=reads, writes=writes, sem=sm, amt=16)

        def coll(name_sem, kind, op_, tin, tout, reads, writes):
            S_.add("gpsimd", lambda e, k=kind, o=op_, i_=tin, u=tout:
                   e.collective_compute(
                       k, o, replica_groups=[list(range(NCORES))],
                       ins=[i_.ap().opt()], outs=[u.ap().opt()]),
                   reads=reads, writes=writes, sem=name_sem, amt=1)

        ALLPAN = [f"pan{k}" for k in range(NKC)]
        ALLSG = [f"Sg{gg}" for gg in range(8)]

        for dst, src, rk in [(tri_sb, tri01, "tri"), (ident_f, identity, "idf"),
                             (cb_sb, cbias, "cb"), (cw_sb, cwcols, "cw"),
                             (dtb_sb, dtb, "dtbk"), (nw_sb, nw_cols, "nw"),
                             (fb_sb, fb_cols, "fb"),
                             (dvrow_sb, dvec_row, "dvr"),
                             (rb_sb, rb_row, "rbk"), (e816_sb, e816, "e816"),
                             (bm_sb, bmask, "bm"), (negA, alog, "negA")]:
            dma("sync", lambda e, d=dst, s=src: e.dma_start(out=d, in_=s),
                [], [rk])
        for dst, src, rk in [(ident_bf, identity, "idb"),
                             (ind12_sb, ind12, "ind12")]:
            dma("gpsimd", lambda e, d=dst, s=src: e.dma_start(out=d, in_=s),
                [], [rk])
        for tile, rk in [(ones1f, "o1f"), (ones1b, "o1b"), (ones3b, "o3b"),
                         (onescol, "oc"), (ones64x, "o64"),
                         (onescol_b, "ocb")]:
            vec(lambda e, t=tile: e.memset(t, 1.0), [], [rk])
        vec(lambda e: e.memset(Sst, 0.0), [], ALLSG)
        scl(lambda e: e.activation(out=negA, in_=negA, func=AF.Exp),
            ["negA"], ["negA"])
        vec(lambda e: e.tensor_scalar(out=negA, in0=negA, scalar1=-1.0,
                                      scalar2=None, op0=OP.mult),
            ["negA"], ["negA"])
        mm(PSA[4][:, 0:H], ones1f, dvrow_sb, ["o1f", "dvr"], "PS4")
        vec(lambda e: e.tensor_copy(D_b, PSA[4][:, 0:H]), ["PS4"], ["D_b"])

        def phase_ln(src, ph, router=False):
            A.push()
            a_b = A.bf16(128, T, f"ab{ph}")
            b_b = A.bf16(128, T, f"bb{ph}")
            xsq = [A.bf16(128, 512, f"xsq{i}{ph}") for i in range(2)]
            tmp_ln = [A.bf16(128, T, f"tmpln{i}{ph}") for i in range(2)]
            arow = A.f32(1, T, f"arow{ph}")
            brow = A.f32(1, T, f"brow{ph}")
            moff = A.off
            mrow = A.f32(1, T, f"mrow{ph}")
            for kc in range(NKC):
                if src == "x":
                    dma("gpsimd", lambda e, k=kc:
                        e.dma_start(out=panel[:, k * T:(k + 1) * T],
                                    in_=x_fm[k * 128:(k + 1) * 128, :]),
                        [], [f"pan{kc}"])
                else:
                    dma("gpsimd", lambda e, k=kc:
                        e.dma_start(out=panel[:, k * T:(k + 1) * T],
                                    in_=x2_s[k * 128:(k + 1) * 128, :]),
                        [f"x2d{kc}"], [f"pan{kc}"])
            for it, (t0, tw) in enumerate(TT):
                for kc in range(NKC):
                    ps = panel[:, kc * T + t0:kc * T + t0 + tw]
                    xb = kc % 2
                    scl(lambda e, o=xsq[xb][:, 0:tw], i_=ps:
                        e.activation(out=o, in_=i_, func=AF.Square),
                        [f"pan{kc}"], [f"xsq{xb}{ph}"])
                    mm(PSA[0][0:1, 0:tw], onescol_b, ps,
                       ["ocb", f"pan{kc}"], "PS0",
                       start=(kc == 0), stop=(kc == NKC - 1))
                    mm(PSA[1][0:1, 0:tw], onescol_b, xsq[xb][:, 0:tw],
                       ["ocb", f"xsq{xb}{ph}"], "PS1",
                       start=(kc == 0), stop=(kc == NKC - 1))
                vec(lambda e, t0=t0, tw=tw:
                    e.tensor_copy(mrow[:, t0:t0 + tw], PSA[0][0:1, 0:tw]),
                    ["PS0"], [f"mrow{ph}"])
                vec(lambda e, t0=t0, tw=tw:
                    e.tensor_copy(arow[:, t0:t0 + tw], PSA[1][0:1, 0:tw]),
                    ["PS1"], [f"arow{ph}"])
            vec(lambda e: e.tensor_scalar(out=mrow, in0=mrow,
                                          scalar1=1.0 / D, scalar2=None,
                                          op0=OP.mult),
                [f"mrow{ph}"], [f"mrow{ph}"])
            vec(lambda e: e.tensor_scalar(out=arow, in0=arow,
                                          scalar1=1.0 / D, scalar2=None,
                                          op0=OP.mult),
                [f"arow{ph}"], [f"arow{ph}"])
            vec(lambda e: e.scalar_tensor_tensor(out=brow, in0=mrow,
                                                 scalar=-1.0, in1=mrow,
                                                 op0=OP.mult, op1=OP.mult),
                [f"mrow{ph}"], [f"brow{ph}"])
            vec(lambda e: e.tensor_tensor(out=arow, in0=arow, in1=brow,
                                          op=OP.add),
                [f"arow{ph}", f"brow{ph}"], [f"arow{ph}"])
            vec(lambda e: e.tensor_scalar(out=arow, in0=arow, scalar1=EPS,
                                          scalar2=None, op0=OP.add),
                [f"arow{ph}"], [f"arow{ph}"])
            scl(lambda e: e.activation(out=arow, in_=arow, func=AF.Ln),
                [f"arow{ph}"], [f"arow{ph}"])
            scl(lambda e: e.activation(out=arow, in_=arow, func=AF.Exp,
                                       scale=-0.5),
                [f"arow{ph}"], [f"arow{ph}"])
            vec(lambda e: e.scalar_tensor_tensor(out=brow, in0=mrow,
                                                 scalar=-1.0, in1=arow,
                                                 op0=OP.mult, op1=OP.mult),
                [f"mrow{ph}", f"arow{ph}"], [f"brow{ph}"])
            for it, (t0, tw) in enumerate(TT):
                mm(PSA[2][:, 0:tw], ones1f, arow[:, t0:t0 + tw],
                   ["o1f", f"arow{ph}"], "PS2")
                mm(PSA[3][:, 0:tw], ones1f, brow[:, t0:t0 + tw],
                   ["o1f", f"brow{ph}"], "PS3")
                scl(lambda e, t0=t0, tw=tw:
                    e.copy(out=a_b[:, t0:t0 + tw], in_=PSA[2][:, 0:tw]),
                    ["PS2"], [f"ab{ph}"])
                scl(lambda e, t0=t0, tw=tw:
                    e.copy(out=b_b[:, t0:t0 + tw], in_=PSA[3][:, 0:tw]),
                    ["PS3"], [f"bb{ph}"])
            if router:
                tokm_sb = A.tb[0:1, 2 * moff:2 * moff + T]
                tkkey = f"mrow{ph}"
                mask_b = tmp_ln[1]
                mkkey = f"tmpln1{ph}"
                mab = A.bf16(128, T, "mab")
                sqs4 = tmp_ln[0]
                sqkey = f"tmpln0{ph}"
                qc_sb = A.f32(128, 64, "qck")
                bsel_sb = A.f32(128, 64, "bselk")
                mb1 = A.f32(1, 1, "mb1")
                mbcol = A.f32(128, 1, "mbcol")
                dma("gpsimd", lambda e: e.dma_start(out=tokm_sb, in_=tokmask),
                    [tkkey], [tkkey])
                dma("sync", lambda e: e.dma_start(out=bsel_sb, in_=bsel_i),
                    [], ["bselk"])
                for it, (t0, tw) in enumerate(TT):
                    mm(PSA[4][:, 0:tw], ones1b, tokm_sb[:, t0:t0 + tw],
                       ["o1b", tkkey], "PS4")
                    scl(lambda e, t0=t0, tw=tw:
                        e.copy(out=mask_b[:, t0:t0 + tw],
                               in_=PSA[4][:, 0:tw]),
                        ["PS4"], [mkkey])
                vec(lambda e: e.tensor_tensor(out=mab, in0=mask_b, in1=a_b,
                                              op=OP.mult),
                    [mkkey, f"ab{ph}"], ["mab"])
                for kc in range(NKC):
                    vec(lambda e, k=kc: e.scalar_tensor_tensor(
                        out=sqs4, in0=panel[:, k * T:(k + 1) * T],
                        scalar=1.0, in1=mab, op0=OP.mult, op1=OP.mult,
                        accum_out=qacc[:, k:k + 1]),
                        [f"pan{kc}", "mab"], [sqkey, f"qacc{kc}"])
                vec(lambda e: e.scalar_tensor_tensor(
                    out=mab[0:1, :], in0=brow, scalar=1.0, in1=tokm_sb,
                    op0=OP.mult, op1=OP.mult, accum_out=mb1),
                    [f"brow{ph}", tkkey], ["mab", "mb1"])
                mm(PSA[4][:, 0:1], ones1f, mb1, ["o1f", "mb1"], "PS4")
                vec(lambda e: e.tensor_copy(mbcol, PSA[4][:, 0:1]),
                    ["PS4"], ["mbcol"])
                QACCK = [f"qacc{k}" for k in range(NKC)]
                vec(lambda e: e.tensor_scalar(out=qacc, in0=qacc,
                                              scalar1=mbcol[:, 0:1],
                                              scalar2=None, op0=OP.add),
                    QACCK + ["mbcol"], QACCK)
                vec(lambda e: e.tensor_tensor(
                    out=qc_sb.rearrange("p (b k) -> p b k", b=2),
                    in0=qacc[:, None, :].broadcast_to([128, 2, NKC]),
                    in1=bsel_sb.rearrange("p (b k) -> p b k", b=2),
                    op=OP.mult),
                    QACCK + ["bselk"], ["qck"])
                dma("sync", lambda e: e.dma_start(out=q_in.ap(), in_=qc_sb),
                    ["qck"], ["q_in"])
                coll("CC3", "AllReduce", OP.add, q_in, q_out,
                     ["q_in"], ["q_out"])
            for kc in range(NKC):
                pr = kc % 2
                vec(lambda e, k=kc, p=pr:
                    e.tensor_tensor(out=tmp_ln[p],
                                    in0=panel[:, k * T:(k + 1) * T],
                                    in1=a_b, op=OP.mult),
                    [f"pan{kc}", f"ab{ph}"], [f"tmpln{pr}{ph}"])
                vec(lambda e, k=kc, p=pr:
                    e.tensor_tensor(out=panel[:, k * T:(k + 1) * T],
                                    in0=tmp_ln[p], in1=b_b, op=OP.add),
                    [f"tmpln{pr}{ph}", f"bb{ph}"], [f"pan{kc}"])
            A.pop_fence()

        phase_ln("x", "L1")

        A.push()
        dt_raw_full = A.f32(128, T, ("dt_raw", "dt_pad"))
        dt_raw = dt_raw_full[0:H, :]
        A.push()
        wsl = [A.bf16(128, NKC * 128, f"wsl{i}") for i in range(2)]
        dsl = [A.bf16(128, KCV * 128, f"dsl{i}") for i in range(2)]
        raw = [A.bf16(128, T + 4, f"raw{i}") for i in range(2)]
        xbct = [A.bf16(128, T, f"xbct{i}") for i in range(2)]
        bndb = A.bf16(4, 128, "bndb")
        NDT = 8
        TQ = T // NDT
        CPP = TQ // QC
        slotA = A.f32(64, TQ, "dZa")
        slotBq = A.f32(64, TQ, "dZb")
        sCq = A.f32(64, TQ, "dCq")
        hspB = A.bf16(64, TQ, "dHsp")
        eltwb = A.bf16(64, TQ, "eltw")
        eltb = A.bf16(128, CPP * 64, "eltb")
        eLendF = A.f32(64, NQ, "eLF")
        elendt = A.f32(NQ, H, "elendtk")
        caH = A.f32(64, NDT, "caH")

        vec(lambda e: e.memset(raw[0][:, 0:4], 0.0), [], ["raw0"])
        vec(lambda e: e.memset(raw[1][:, 0:4], 0.0), [], ["raw1"])
        vec(lambda e: e.memset(bndb, 0.0), [], ["bndb"])

        def emit_dt_half(half):
            hc = half * TQ
            X = dt_raw_full[0:64, hc:hc + TQ]
            sC = sCq
            slotB = slotBq
            vec(lambda e: e.tensor_scalar(out=slotA, in0=X,
                                          scalar1=dtb_sb[:, 0:1],
                                          scalar2=None, op0=OP.add),
                ["dt_raw", "dtbk"], ["dZa"])
            vec(lambda e: e.scalar_tensor_tensor(out=slotB, in0=slotA,
                                                 scalar=-1.0, in1=slotA,
                                                 op0=OP.mult, op1=OP.max),
                ["dZa"], ["dZb"])
            scl(lambda e: e.activation(out=slotB, in_=slotB, func=AF.Exp,
                                       scale=-1.0), ["dZb"], ["dZb"])
            scl(lambda e: e.activation(out=slotB, in_=slotB, func=AF.Ln,
                                       bias=onescol[0:64, 0:1]),
                ["dZb", "oc"], ["dZb"])
            vec(lambda e: e.tensor_scalar(out=sC, in0=slotA, scalar1=0.0,
                                          scalar2=None, op0=OP.max),
                ["dZa"], ["dCq"])
            vec(lambda e: e.tensor_tensor(out=sC, in0=sC, in1=slotB,
                                          op=OP.add),
                ["dCq", "dZb"], ["dCq"])
            vec(lambda e: e.tensor_scalar(out=X, in0=sC,
                                          scalar1=negA[:, 0:1],
                                          scalar2=None, op0=OP.mult),
                ["dCq", "negA", "dt_raw"], ["dt_raw"])
            for qq in range(CPP):
                vec(lambda e, qq=qq: e.tensor_tensor_scan(
                    out=slotA[:, qq * QC:(qq + 1) * QC], data0=ones64x,
                    data1=X[:, qq * QC:(qq + 1) * QC], initial=0.0,
                    op0=OP.mult, op1=OP.add), ["dt_raw", "o64"], ["dZa"])
            vec(lambda e, half=half:
                e.reduce_sum(caH[:, half:half + 1], X, axis=AX.X),
                ["dt_raw"], ["caH"])
            scl(lambda e, half=half: e.activation(
                out=eLendF[:, half * CPP:(half + 1) * CPP],
                in_=slotA[:, QC - 1::QC], func=AF.Exp), ["dZa"], ["eLF"])
            scl(lambda e: e.activation(out=eltwb, in_=slotA, func=AF.Exp),
                ["dZa"], ["eltw"])
            scl(lambda e: e.activation(out=slotB, in_=sC, func=AF.Ln),
                ["dCq"], ["dZb"])
            vec(lambda e: e.tensor_tensor(out=sC, in0=slotB, in1=slotA,
                                          op=OP.subtract),
                ["dZa", "dZb"], ["dCq"])
            for ti, (curv, ck, outs) in enumerate(
                    [(slotA, "dZa", lsp_s), (sCq, "dCq", nrsp_s)]):
                for si in range(3):
                    vec(lambda e, c=curv: e.tensor_copy(hspB, c),
                        [ck], ["dHsp"])
                    dma("sync", lambda e, o=outs[si], hc=hc:
                        e.dma_start(out=o[:, hc:hc + TQ], in_=hspB),
                        ["dHsp"], [f"sp{ti}{si}h{half}"])
                    if si < 2:
                        vec(lambda e, c=curv:
                            e.tensor_tensor(out=c, in0=c, in1=hspB,
                                            op=OP.subtract),
                            [ck, "dHsp"], [ck])

        def emit_dt_trans(half):
            for i in range(CPP):
                tr(PBA[1][:, i * 64:(i + 1) * 64],
                   eltwb[:, i * 128:(i + 1) * 128],
                   ident_bf[0:64, 0:64],
                   ["eltw", "idb"], "PBT1")
            vec(lambda e: e.tensor_copy(eltb, PBA[1][:, 0:CPP * 64]),
                ["PBT1"], ["eltb"])
            dma("sync", lambda e, half=half: e.dma_start(
                out=elt_s.ap()[half * CPP:(half + 1) * CPP]
                .rearrange("q p h -> p q h"),
                in_=eltb.rearrange("p (q h) -> p q h", q=CPP)),
                ["eltb"], [f"eltd{half}"])

        def emit_dt_fin():
            tr(PSA[5][0:NQ, 0:H], eLendF, ident_f[0:H, 0:H],
               ["eLF", "idf"], "PS5")
            vec(lambda e: e.tensor_copy(elendt, PSA[5][0:NQ, 0:H]),
                ["PS5"], ["elendtk"])
            dma("sync", lambda e: e.dma_start(out=elend_s, in_=elendt),
                ["elendtk"], ["elend_d"])
            vec(lambda e: e.tensor_tensor(out=caH[:, 0:4],
                                          in0=caH[:, 0:4],
                                          in1=caH[:, 4:8], op=OP.add),
                ["caH"], ["caH"])
            vec(lambda e: e.tensor_tensor(out=caH[:, 0:2],
                                          in0=caH[:, 0:2],
                                          in1=caH[:, 2:4], op=OP.add),
                ["caH"], ["caH"])
            vec(lambda e: e.tensor_tensor(out=caH[:, 0:1],
                                          in0=caH[:, 0:1],
                                          in1=caH[:, 1:2], op=OP.add),
                ["caH"], ["caH"])
            scl(lambda e: e.activation(out=caH[:, 0:1],
                                       in_=caH[:, 0:1], func=AF.Exp),
                ["caH"], ["caH"])
            dma("sync", lambda e: e.dma_start(out=ca_in.ap(),
                                              in_=caH[:, 0:1]),
                ["caH"], ["ca_in"])

        gidx = 0
        cidx = 0
        for oc in range(49):
            wb = oc % 2
            dma("gpsimd", lambda e, o=oc, w=wb:
                e.dma_start(out=wsl[w],
                            in_=w_hbcdt[o].rearrange("p kc m -> p (kc m)")),
                [], [f"wsl{wb}"])
            if oc >= 1:
                dma("gpsimd", lambda e, o=oc, w=wb:
                    e.dma_start(out=dsl[w].rearrange("p (k m) -> p k m",
                                                     k=KCV),
                                in_=diagw[o - 1]
                                .rearrange("k p m -> p k m")),
                    [], [f"dsl{wb}"])
            for it, (t0, tw) in enumerate(TT):
                bank = gidx % 4
                for kc in range(NKC):
                    mm(PSA[bank][:, 0:tw],
                       wsl[wb][:, kc * 128:(kc + 1) * 128],
                       panel[:, kc * T + t0:kc * T + t0 + tw],
                       [f"wsl{wb}", f"pan{kc}"], f"PS{bank}",
                       start=(kc == 0), stop=(kc == NKC - 1))
                scl(lambda e, w=wb, t0=t0, tw=tw, b=bank:
                    e.copy(out=raw[w][:, 4 + t0:4 + t0 + tw],
                           in_=PSA[b][:, 0:tw]),
                    [f"PS{bank}"], [f"raw{wb}"])
                gidx += 1
            if oc == 0:
                vec(lambda e: e.tensor_copy(dt_raw, raw[0][0:H, 4:4 + T]),
                    ["raw0"], ["dt_raw"])
            else:
                for it, (t0, tw) in enumerate(TT):
                    bank = 4 + (cidx % 2)
                    for k in range(KCV):
                        mm(PSA[bank][:, 0:tw],
                           dsl[wb][:, k * 128:(k + 1) * 128],
                           raw[wb][:, t0 + k + 1:t0 + k + 1 + tw],
                           [f"dsl{wb}", f"raw{wb}"], f"PS{bank}",
                           start=(k == 0), stop=(k == KCV - 1))
                    scl(lambda e, w=wb, t0=t0, tw=tw, b=bank, o=oc:
                        e.activation(out=xbct[w][:, t0:t0 + tw],
                                     in_=PSA[b][:, 0:tw], func=AF.Silu,
                                     bias=cb_sb[:, o - 1:o]),
                        [f"PS{bank}", "cb"], [f"xbct{wb}"])
                    cidx += 1
                vec(lambda e, w=wb, o=oc:
                    e.tensor_copy(pf_sb[:, (o - 1) * 3:(o - 1) * 3 + 3],
                                  raw[w][:, 4:7]),
                    [f"raw{wb}"], ["pf"])
                tr(PBA[0][0:3, 0:128], raw[wb][:, T + 1:T + 4], ident_bf,
                   [f"raw{wb}", "idb"], "PBT0")
                vec(lambda e: e.tensor_copy(bndb[0:3, :], PBA[0][0:3, 0:128]),
                    ["PBT0"], ["bndb"])
                dma("sync", lambda e, o=oc:
                    e.dma_start(out=bnd_in.ap()[:, (o - 1) * 128:o * 128],
                                in_=bndb),
                    ["bndb"], ["bnd_in"])
                dma("sync", lambda e, w=wb, o=oc:
                    e.dma_start(out=xbc_s[(o - 1) * 128:o * 128, :],
                                in_=xbct[w]),
                    [f"xbct{wb}"], [f"xbcd{oc - 1}"])
            if 2 <= oc <= 9:
                emit_dt_half(oc - 2)
            elif 11 <= oc <= 18:
                emit_dt_trans(oc - 11)
            elif oc == 20:
                emit_dt_fin()
        coll("CC1", "AllGather", OP.bypass, bnd_in, bnd_out,
             ["bnd_in"], ["bnd_out"])
        A.pop_fence()

        A.pop_fence()
        A.push()
        bnd_tm = A.bf16(4, CONV, "bnd_tm")
        rbuf = [A.bf16(4, CONV, f"rbuf{i}") for i in range(2)]
        bndc_sb = A.f32(4, 8, "bndck")
        rawx = A.bf16(128, 8, "rawx")
        pacc = A.f32(128, 3, "pacc")
        dma("sync", lambda e: e.dma_start(out=bndc_sb, in_=bndc),
            [], ["bndck"])
        vec(lambda e: e.memset(bnd_tm, 0.0), [], ["bnd_tm"])
        for r in range(8):
            rb_ = r % 2
            dma("scalq", lambda e, r=r, w=rb_:
                e.dma_start(out=rbuf[w],
                            in_=bnd_out.ap()[r * 4:(r + 1) * 4, :]),
                ["bnd_out"], [f"rbuf{rb_}"])
            vec(lambda e, r=r, w=rb_: e.scalar_tensor_tensor(
                out=bnd_tm, in0=rbuf[w], scalar=bndc_sb[:, r:r + 1],
                in1=bnd_tm, op0=OP.mult, op1=OP.add),
                [f"rbuf{rb_}", "bnd_tm", "bndck"], ["bnd_tm"])
        for c in range(NHBC):
            tr(PBA[1][:, 0:4], bnd_tm[0:4, c * 128:(c + 1) * 128],
               ident_bf[0:4, 0:4], ["bnd_tm", "idb"], "PBT1")
            vec(lambda e: e.tensor_copy(rawx[:, 0:3], PBA[1][:, 0:3]),
                ["PBT1"], ["rawx"])
            vec(lambda e, c=c: e.tensor_copy(rawx[:, 3:6],
                                             pf_sb[:, c * 3:c * 3 + 3]),
                ["pf"], ["rawx"])
            vec(lambda e, c=c: e.tensor_scalar(
                out=pacc, in0=rawx[:, 0:3],
                scalar1=cw_sb[:, c * KCV:c * KCV + 1], scalar2=None,
                op0=OP.mult), ["rawx", "cw"], ["pacc"])
            for k in range(1, KCV):
                vec(lambda e, c=c, k=k: e.scalar_tensor_tensor(
                    out=pacc, in0=rawx[:, k:k + 3],
                    scalar=cw_sb[:, c * KCV + k:c * KCV + k + 1],
                    in1=pacc, op0=OP.mult, op1=OP.add),
                    ["rawx", "cw", "pacc"], ["pacc"])
            scl(lambda e, c=c: e.activation(out=pxbc[:, c * 3:c * 3 + 3],
                                            in_=pacc, func=AF.Silu,
                                            bias=cb_sb[:, c:c + 1]),
                ["pacc", "cb"], ["pxbc"])
        A.pop_fence()

        def load_nr(NRq, q, tag):
            for si in range(3):
                dma("scalq", lambda e, si=si, q=q:
                    e.dma_start(out=NRq[si * 4:(si + 1) * 4, :]
                                .rearrange("k (hb t) -> k hb t", hb=NHB),
                                in_=nrsp_s[si]
                                .rearrange("(hb k) t -> k hb t", k=4)
                                [:, :, q * QC:(q + 1) * QC]),
                    [f"sp1{si}h{q // 2}"], [tag])

        def load_l3(Lr3, q, quarter, tag, nh=16):
            for si in range(3):
                dma("scalq", lambda e, si=si, q=q, hf=quarter, nh=nh:
                    e.dma_start(out=Lr3[si:si + 1]
                                .rearrange("p (h t) -> p h t", h=nh),
                                in_=lsp_s[si][hf * nh:(hf + 1) * nh,
                                              q * QC:(q + 1) * QC]
                                .unsqueeze(0)),
                    [f"sp0{si}h{q // 2}"], [tag])

        def load_elrow(dst, q, tag):
            dma("sync", lambda e, q=q:
                e.dma_start(out=dst, in_=elend_s[q:q + 1, :]),
                ["elend_d"], [tag])

        def overlay(buf, chunks, tag):
            for li, c in enumerate(chunks):
                vec(lambda e, li=li, c=c:
                    e.tensor_copy(buf[:, li * 128:li * 128 + 3],
                                  pxbc[:, c * 3:c * 3 + 3]),
                    ["pxbc", tag], [tag])

        A.push()
        xfm_q = A.bf16(128, I, "xfmq")
        xtm_q = A.bf16(128, I, "xtmq")
        bq = A.bf16(128, G * 128, "bq")
        btm = A.bf16(128, G * 128, "btm")
        NRq = A.bf16(12, NHB * QC, "NRqA")
        Lr3 = A.bf16(3, 32 * QC, "Lr3A")
        elr = A.f32(1, H, "elrA")
        cumAq = A.f32(128, H, "cumAqA")
        wcl = A.f32(128, H, "wclA")
        wcols = A.f32(128, H, "wcolsA")
        bw = A.bf16(128, 1024, "bwA")
        sttb = A.f32(128, 512, "sttA")

        for q in range(NQ):
            load_nr(NRq, q, "NRqA")
            load_elrow(elr, q, "elrA")
            dma("scalq", lambda e, q=q:
                e.dma_start(out=xfm_q.rearrange("p (c t) -> p c t", c=32),
                            in_=xbc_s[0:I, q * QC:(q + 1) * QC]
                            .rearrange("(c p) t -> p c t", p=128)),
                [f"xbcd{c}" for c in range(32)], ["xfmq"])
            dma("scalq", lambda e, q=q:
                e.dma_start(out=bq.rearrange("p (c t) -> p c t", c=8),
                            in_=xbc_s[I:I + G * 128, q * QC:(q + 1) * QC]
                            .rearrange("(c p) t -> p c t", p=128)),
                [f"xbcd{32 + c}" for c in range(8)], ["bq"])
            if q == 0:
                overlay(xfm_q, range(32), "xfmq")
                overlay(bq, range(32, 40), "bq")
            mm(PSA[5][:, 0:H], ones1f, elr, ["o1f", "elrA"], "PS5")
            vec(lambda e: e.tensor_copy(cumAq, PSA[5][:, 0:H]),
                ["PS5"], ["cumAqA"])
            for c8 in range(0, 32, 8):
                for c in range(c8, c8 + 8):
                    tr(PBA[0][:, (c - c8) * 128:(c - c8 + 1) * 128],
                       xfm_q[:, c * 128:(c + 1) * 128], ident_bf,
                       ["xfmq", "idb"], "PBT0")
                vec(lambda e, c8=c8:
                    e.tensor_copy(xtm_q[:, c8 * 128:(c8 + 8) * 128],
                                  PBA[0]),
                    ["PBT0"], ["xtmq"])
            dma("sync", lambda e, q=q: e.dma_start(out=xtm_s[q], in_=xtm_q),
                ["xtmq"], [f"xtmd{q}"])
            for gx in range(8):
                tr(PBA[1][:, gx * 128:(gx + 1) * 128],
                   bq[:, gx * 128:(gx + 1) * 128], ident_bf,
                   ["bq", "idb"], "PBT1")
            vec(lambda e: e.tensor_copy(btm, PBA[1]), ["PBT1"], ["btm"])
            for half in range(2):
                load_l3(Lr3, q, half, "Lr3A", nh=32)
                for hbl in range(8):
                    hb = half * 8 + hbl
                    mm(PSA[4][:, hb * 4:(hb + 1) * 4],
                       NRq[:, hb * QC:(hb + 1) * QC],
                       ind12_sb[:, QC - 1::QC],
                       ["NRqA", "ind12"], "PS4", start=True, stop=False)
                    mm(PSA[4][:, hb * 4:(hb + 1) * 4],
                       ones3b,
                       Lr3[:, hbl * 4 * QC + QC - 1:(hbl + 1) * 4 * QC:QC],
                       ["o3b", "Lr3A"], "PS4", start=False, stop=True)
            vec(lambda e: e.tensor_scalar(out=wcl, in0=PSA[4][:, 0:H],
                                          scalar1=30.0, scalar2=None,
                                          op0=OP.min), ["PS4"], ["wclA"])
            scl(lambda e: e.activation(out=wcols, in_=wcl, func=AF.Exp),
                ["wclA"], ["wcolsA"])
            for gx in range(8):
                ub = 2 + (gx % 2)
                vec(lambda e, gx=gx: e.tensor_tensor(
                    out=bw.rearrange("p (h n) -> p h n", h=8),
                    in0=btm[:, gx * 128:(gx + 1) * 128][:, None, :]
                    .broadcast_to([128, 8, 128]),
                    in1=wcols[:, gx * 8:(gx + 1) * 8][:, :, None]
                    .broadcast_to([128, 8, 128]),
                    op=OP.mult), ["btm", "wcolsA"], ["bwA"])
                for hl8 in range(8):
                    h = gx * 8 + hl8
                    mm(PSA[ub][:, hl8 * 64:(hl8 + 1) * 64],
                       bw[:, hl8 * 128:(hl8 + 1) * 128],
                       xtm_q[:, h * 64:(h + 1) * 64],
                       ["bwA", "xtmq"], f"PS{ub}")
                vec(lambda e, gx=gx: e.tensor_tensor(
                    out=sttb.rearrange("p (h x) -> p h x", h=8),
                    in0=Sst[:, gx * 512:(gx + 1) * 512]
                    .rearrange("p (h x) -> p h x", h=8),
                    in1=cumAq[:, gx * 8:(gx + 1) * 8][:, :, None]
                    .broadcast_to([128, 8, 64]),
                    op=OP.mult), [f"Sg{gx}", "cumAqA"], ["sttA"])
                vec(lambda e, gx=gx, b=ub: e.tensor_tensor(
                    out=Sst[:, gx * 512:(gx + 1) * 512], in0=sttb,
                    in1=PSA[b], op=OP.add),
                    ["sttA", f"PS{ub}"], [f"Sg{gx}"])
        dma("gpsimd", lambda e: e.dma_start(out=st_in.ap(), in_=Sst),
            ALLSG, ["st_in"])
        coll("CC2a", "AllGather", OP.bypass, st_in, st_out,
             ["st_in"], ["st_out"])
        coll("CC2b", "AllGather", OP.bypass, ca_in, ca_out,
             ["ca_in"], ["ca_out"])
        A.pop_fence()

        A.push()
        GS = 256
        NGS = I // GS
        gsl = [A.bf16(128, NKC * GS, f"gsl{i}") for i in range(2)]
        gout = [A.bf16(128, GS, f"gout{i}") for i in range(2)]
        gg = 0
        for s in range(NGS):
            wb2 = s % 2
            dma("gpsimd", lambda e, s=s, w=wb2:
                e.dma_start(out=gsl[w].rearrange("p (kc m) -> p kc m",
                                                 kc=NKC),
                            in_=w_gate[:, :, s * GS:(s + 1) * GS]
                            .rearrange("kc p m -> p kc m")),
                [], [f"gsl{wb2}"])
            for tt in range(T16):
                bank = gg % 4
                for kc in range(NKC):
                    mm(PSA[bank][:, 0:GS],
                       panel[:, kc * T + tt * 128:kc * T + (tt + 1) * 128],
                       gsl[wb2][:, kc * GS:(kc + 1) * GS],
                       [f"gsl{wb2}", f"pan{kc}"], f"PS{bank}",
                       start=(kc == 0), stop=(kc == NKC - 1))
                ob = gg % 2
                scl(lambda e, b=bank, o=ob:
                    e.activation(out=gout[o], in_=PSA[b][:, 0:GS],
                                 func=AF.Silu),
                    [f"PS{bank}"], [f"gout{ob}"])
                dma("sync", lambda e, s=s, tt=tt, o=ob:
                    e.dma_start(out=gate_s[tt * 128:(tt + 1) * 128,
                                           s * GS:(s + 1) * GS],
                                in_=gout[o]),
                    [f"gout{ob}"], [f"gated{s}_{tt}"])
                gg += 1
        A.pop_fence()

        A.push()
        cumAg = A.f32(H, 8, "cumAg")
        chm_sb = A.f32(H, 64, "chmk")
        chm1_sb = A.f32(H, 64, "chm1k")
        coefs = A.f32(H, 8, "coefs")
        ct1 = A.f32(H, 8, "ct1")
        ct2 = A.f32(H, 4, "ct2")
        ct3 = A.f32(H, 2, "ct3")
        coefT = A.f32(8, H, "coefT")
        coefrow = A.f32(1, 512, "coefrow")
        Gr = A.bf16(128, H * P, "Gr")
        cfb = A.f32(128, H, "cfb")
        tbuf = A.f32(128, H * P, "tbuf")
        dma("sync", lambda e: e.dma_start(out=chm_sb, in_=chmh), [], ["chmk"])
        dma("sync", lambda e: e.dma_start(out=chm1_sb, in_=chm1mh),
            [], ["chm1k"])
        dma("sync", lambda e: e.dma_start(
            out=cumAg,
            in_=ca_out.ap().rearrange("(r h) x -> h (r x)", r=8),
            allow_slow_non_contiguous=True), ["ca_out"], ["cumAg"])
        for r in range(8):
            vec(lambda e, r=r: e.tensor_tensor(
                out=ct1, in0=cumAg, in1=chm_sb[:, r * 8:(r + 1) * 8],
                op=OP.mult), ["cumAg", "chmk"], ["ct1"])
            vec(lambda e, r=r: e.tensor_tensor(
                out=ct1, in0=ct1, in1=chm1_sb[:, r * 8:(r + 1) * 8],
                op=OP.add), ["ct1", "chm1k"], ["ct1"])
            vec(lambda e: e.tensor_tensor(out=ct2, in0=ct1[:, 0:4],
                                          in1=ct1[:, 4:8], op=OP.mult),
                ["ct1"], ["ct2"])
            vec(lambda e: e.tensor_tensor(out=ct3, in0=ct2[:, 0:2],
                                          in1=ct2[:, 2:4], op=OP.mult),
                ["ct2"], ["ct3"])
            vec(lambda e, r=r: e.tensor_tensor(out=coefs[:, r:r + 1],
                                               in0=ct3[:, 0:1],
                                               in1=ct3[:, 1:2], op=OP.mult),
                ["ct3"], ["coefs"])
        tr(PSA[5][0:8, 0:H], coefs, ident_f[0:H, 0:H], ["coefs", "idf"], "PS5")
        vec(lambda e: e.tensor_copy(coefT, PSA[5][0:8, 0:H]),
            ["PS5"], ["coefT"])
        dma("sync", lambda e: e.dma_start(out=coefb_s, in_=coefT),
            ["coefT"], ["coefb_d"])
        dma("sync", lambda e: e.dma_start(
            out=coefrow.rearrange("p (r h) -> p r h", r=8),
            in_=coefb_s.unsqueeze(0)), ["coefb_d"], ["coefrow"])
        vec(lambda e: e.memset(Sst, 0.0), [], ALLSG)
        for r in range(8):
            dma("scalq", lambda e, r=r:
                e.dma_start(out=Gr,
                            in_=st_out.ap()[r * 128:(r + 1) * 128, :]),
                ["st_out"], ["Gr"])
            mm(PSA[4][:, 0:H], ones1f, coefrow[:, r * H:(r + 1) * H],
               ["o1f", "coefrow"], "PS4")
            vec(lambda e: e.tensor_copy(cfb, PSA[4][:, 0:H]),
                ["PS4"], ["cfb"])
            vec(lambda e: e.tensor_tensor(
                out=tbuf.rearrange("p (h x) -> p h x", h=H),
                in0=Gr.rearrange("p (h x) -> p h x", h=H),
                in1=cfb[:, :, None].broadcast_to([128, H, P]),
                op=OP.mult), ["Gr", "cfb"], ["tbuf"])
            for gx in range(8):
                vec(lambda e, gx=gx: e.tensor_tensor(
                    out=Sst[:, gx * 512:(gx + 1) * 512],
                    in0=tbuf[:, gx * 512:(gx + 1) * 512],
                    in1=Sst[:, gx * 512:(gx + 1) * 512], op=OP.add),
                    ["tbuf", f"Sg{gx}"], [f"Sg{gx}"])
        for gx in range(8):
            scl(lambda e, gx=gx: e.copy(
                out=S_bf[:, gx * 512:(gx + 1) * 512],
                in_=Sst[:, gx * 512:(gx + 1) * 512]),
                [f"Sg{gx}"], [f"Sbf{gx}"])
        A.pop_fence()

        A.push()
        xtmu = [A.bf16(128, 512, f"xtmu{i}") for i in range(2)]
        bq2x = [A.bf16(128, G * 128, f"bq2x{i}") for i in range(2)]
        cq2x = [A.bf16(128, G * 128, f"cq2x{i}") for i in range(2)]
        btm2 = A.bf16(128, G * 128, "btm2")
        Gm = A.bf16(128, G * 128, "Gm")
        NRL15 = [A.bf16(15, 4 * QC, f"NRq2x{i}") for i in range(2)]
        ILx = [A.bf16(15, 8 * QC, f"Lr32x{i}") for i in range(2)]
        elrx = [A.f32(1, H, f"elrx{i}") for i in range(2)]
        cumAq2 = [A.f32(128, H, f"cumAq2_{i}") for i in range(2)]
        dmy = A.f32(128, 1024, "dmy")
        wd = [A.bf16(128, 1024, f"wd{i}") for i in range(2)]
        wt = [A.bf16(128, 1024, (f"wtA{i}", f"wtB{i}")) for i in range(2)]
        xw = [A.bf16(128, 512, f"xw{i}") for i in range(2)]
        ysd = A.bf16(128, 512, "ysd")
        sdec = A.f32(128, 512, "sdec")
        eltq = [A.bf16(128, 64, f"eltq{i}") for i in range(3)]
        ysb = [A.f32(128, 512, f"ysb{i}") for i in range(2)]
        sqscr = A.bf16(128, 512, "sqx")
        ssq = [A.f32(128, 1, f"ssq{i}") for i in range(2)]
        rstd = [A.f32(128, 1, f"rstd{i}") for i in range(2)]
        yng = A.bf16(128, 512, "yng")
        ggt = [A.bf16(128, 512, f"ggt{i}") for i in range(2)]
        b30 = A.f32(128, 1, "b30")
        vec(lambda e: e.memset(b30, 30.0), [], ["b30"])
        E30 = float(np.exp(30.0))

        NU = NQ * 8

        def load_nrl(tu):
            q, gx = tu // 8, tu % 8
            b = tu % 2
            dma("sync", lambda e, q=q, gx=gx, b=b:
                e.dma_start(
                    out=ILx[b][12:15, :]
                    .rearrange("si (h t) -> si h t", h=8),
                    in_=lsp3[:, gx * 8:(gx + 1) * 8,
                             q * QC:(q + 1) * QC]),
                [f"sp0{si}h{q // 2}" for si in range(3)], [f"Lr32x{b}"])

        def load_nr2(g):
            tu = 2 * g
            q, gx = tu // 8, tu % 8
            b = g % 2
            for si in range(3):
                dma("sync", lambda e, si=si, q=q, gx=gx, b=b:
                    e.dma_start(
                        out=NRL15[b][si * 4:(si + 1) * 4, :]
                        .rearrange("k (hb t) -> k hb t", hb=4),
                        in_=nrsp3[si]
                        .rearrange("(hb k) t -> k hb t", k=4)
                        [:, gx * 2:gx * 2 + 4, q * QC:(q + 1) * QC]),
                    [f"sp1{si}h{q // 2}"], [f"NRq2x{b}"])

        def load_qgrp(q):
            b = q % 2
            dma("sync", lambda e, q=q, b=b:
                e.dma_start(out=bq2x[b].rearrange("p (c t) -> p c t", c=8),
                            in_=xbc_s[I:I + G * 128,
                                      q * QC:(q + 1) * QC]
                            .rearrange("(c p) t -> p c t", p=128)),
                [f"xbcd{32 + c}" for c in range(8)], [f"bq2x{b}"])
            dma("sync", lambda e, q=q, b=b:
                e.dma_start(out=cq2x[b].rearrange("p (c t) -> p c t", c=8),
                            in_=xbc_s[I + G * 128:CONV,
                                      q * QC:(q + 1) * QC]
                            .rearrange("(c p) t -> p c t", p=128)),
                [f"xbcd{40 + c}" for c in range(8)], [f"cq2x{b}"])
            dma("sync", lambda e, q=q, b3=q % 3:
                e.dma_start(out=eltq[b3], in_=elt_s.ap()[q]),
                [f"eltd{q // 2}"], [f"eltq{q % 3}"])
            dma("sync", lambda e, q=q, b=b:
                e.dma_start(out=elrx[b], in_=elend_s[q:q + 1, :]),
                ["elend_d"], [f"elrx{b}"])

        def b_front_d(ui):
            a2 = ui % 2
            g2 = (ui // 2) % 2
            aks = [f"PS{2 * a2}", f"PS{2 * a2 + 1}"]
            for sub in range(2):
                hbl = (ui % 2) * 2 + sub
                mm(PF[a2][:, sub * 512:sub * 512 + 512],
                   NRL15[g2][:, hbl * QC:(hbl + 1) * QC],
                   ILx[a2][:, sub * 512:(sub + 1) * 512],
                   [f"NRq2x{g2}", f"Lr32x{a2}"], aks[sub])

        def b_front(ui):
            q, gx = ui // 8, ui % 8
            u = ui % 2
            a2 = ui % 2
            qp = q % 2
            aks = [f"PS{2 * a2}", f"PS{2 * a2 + 1}"]
            if gx == 0:
                if q + 1 < NQ:
                    load_qgrp(q + 1)
                mm(PSA[4][:, 0:H], ones1f, elrx[qp],
                   ["o1f", f"elrx{qp}"], "PS4")
                vec(lambda e, qp=qp:
                    e.tensor_copy(cumAq2[qp], PSA[4][:, 0:H]),
                    ["PS4"], [f"cumAq2_{qp}"])
                for g2 in range(8):
                    half = g2 // 4
                    mm(PF[a2][:, half * 512 + (g2 % 4) * 128:
                       half * 512 + (g2 % 4 + 1) * 128],
                       bq2x[qp][:, g2 * 128:(g2 + 1) * 128],
                       cq2x[qp][:, g2 * 128:(g2 + 1) * 128],
                       [f"bq2x{qp}", f"cq2x{qp}"], aks[half])
                for half in range(2):
                    vec(lambda e, b=half, a_=a2: e.tensor_tensor(
                        out=Gm[:, b * 512:(b + 1) * 512]
                        .rearrange("p (g t) -> p g t", g=4),
                        in0=PF[a_][:, b * 512:(b + 1) * 512]
                        .rearrange("p (g t) -> p g t", g=4),
                        in1=tri_sb[:, None, :].broadcast_to([128, 4, 128]),
                        op=OP.mult), [aks[half], "tri"], ["Gm"])
                for g2 in range(8):
                    tr(PBA[1][:, g2 * 128:(g2 + 1) * 128],
                       bq2x[qp][:, g2 * 128:(g2 + 1) * 128], ident_bf,
                       [f"bq2x{qp}", "idb"], "PBT1")
                scl(lambda e: e.activation(out=btm2, in_=PBA[1],
                                           func=AF.Identity,
                                           scale=E30),
                    ["PBT1"], ["btm2"])
            if ui + 1 < NU:
                load_nrl(ui + 1)
            if ui % 2 == 1 and ui + 1 < NU:
                load_nr2((ui + 1) // 2)
            dma("sync", lambda e, q=q, gx=gx, u=u:
                e.dma_start(out=xtmu[u],
                            in_=xtm_s[q][:, gx * 512:(gx + 1) * 512]),
                [f"xtmd{q}"], [f"xtmu{u}"])

        def b_scal(ui):
            u = ui % 2
            a2 = ui % 2
            aks = [f"PS{2 * a2}", f"PS{2 * a2 + 1}"]
            scl(lambda e, a_=a2: e.activation(
                out=dmy, in_=PF[a_], func=AF.Relu, scale=-1.0,
                bias=b30[:, 0:1]), aks + ["b30"], ["dmyA"])
            scl(lambda e, u=u: e.activation(out=wd[u], in_=dmy,
                                            func=AF.Exp, scale=-1.0),
                ["dmyA"], [f"wdA{u}", f"wdB{u}"])

        def b_cde(ui):
            q, gx = ui // 8, ui % 8
            u = ui % 2
            a2 = ui % 2
            ib = ui % 2
            qp = q % 2
            aks = [f"PS{2 * a2}", f"PS{2 * a2 + 1}"]
            vec(lambda e, u=u: e.tensor_tensor(
                out=xw[u].rearrange("p (h x) -> p h x", h=8),
                in0=xtmu[u].rearrange("p (h x) -> p h x", h=8),
                in1=wd[u][:, QC - 1::QC][:, :, None]
                .broadcast_to([128, 8, 64]), op=OP.mult),
                [f"wdA{u}", f"wdB{u}", f"xtmu{u}"], [f"xw{u}"])
            pool(lambda e, u=u: e.tensor_tensor(
                out=ysd.rearrange("p (h x) -> p h x", h=8),
                in0=xtmu[u].rearrange("p (h x) -> p h x", h=8),
                in1=D_b[:, (ui % 8) * 8:(ui % 8) * 8 + 8][:, :, None]
                .broadcast_to([128, 8, 64]), op=OP.mult),
                [f"xtmu{u}", "D_b"], ["ysd"])
            vec(lambda e, gx=gx, u=u: e.tensor_tensor(
                out=wt[u].rearrange("p (h t) -> p h t", h=8)[:, 0:4, :],
                in0=wd[u].rearrange("p (h t) -> p h t", h=8)[:, 0:4, :],
                in1=Gm[:, gx * 128:(gx + 1) * 128][:, None, :]
                .broadcast_to([128, 4, 128]),
                op=OP.mult),
                [f"wdA{u}", "Gm"], [f"wtA{u}"])
            pool(lambda e, gx=gx, u=u: e.tensor_tensor(
                out=wt[u].rearrange("p (h t) -> p h t", h=8)[:, 4:8, :],
                in0=wd[u].rearrange("p (h t) -> p h t", h=8)[:, 4:8, :],
                in1=Gm[:, gx * 128:(gx + 1) * 128][:, None, :]
                .broadcast_to([128, 4, 128]),
                op=OP.mult),
                [f"wdB{u}", "Gm"], [f"wtB{u}"])
            pool(lambda e, gx=gx, qp=qp: e.tensor_tensor(
                out=sdec.rearrange("p (h x) -> p h x", h=8),
                in0=Sst[:, gx * 512:(gx + 1) * 512]
                .rearrange("p (h x) -> p h x", h=8),
                in1=cumAq2[qp][:, gx * 8:(gx + 1) * 8][:, :, None]
                .broadcast_to([128, 8, 64]), op=OP.mult),
                [f"Sg{gx}", f"cumAq2_{qp}"], ["sdec"])
            for hl8 in range(8):
                h = gx * 8 + hl8
                mm(PF[2][:, ib * 512 + hl8 * 64:
                   ib * 512 + (hl8 + 1) * 64],
                   cq2x[qp][:, gx * 128:(gx + 1) * 128],
                   S_bf[:, h * 64:(h + 1) * 64],
                   [f"cq2x{qp}", f"Sbf{gx}"], f"PS{4 + ib}")
            for hl8 in range(8):
                mm(PF[a2][:, 512 + hl8 * 64:512 + (hl8 + 1) * 64],
                   btm2[:, gx * 128:(gx + 1) * 128],
                   xw[u][:, hl8 * 64:(hl8 + 1) * 64],
                   ["btm2", f"xw{u}"], aks[1])
            for hl8 in range(8):
                wk = f"wtA{u}" if hl8 < 4 else f"wtB{u}"
                mm(PF[a2][:, hl8 * 64:(hl8 + 1) * 64],
                   wt[u][:, hl8 * 128:(hl8 + 1) * 128],
                   xtmu[u][:, hl8 * 64:(hl8 + 1) * 64],
                   [wk, f"xtmu{u}"], aks[0])

        def b_asm(ui):
            q, gx = ui // 8, ui % 8
            u = ui % 2
            a2 = ui % 2
            ib = ui % 2
            qp = q % 2
            aks = [f"PS{2 * a2}", f"PS{2 * a2 + 1}"]
            dma("sync", lambda e, q=q, gx=gx, u=u:
                e.dma_start(out=ggt[u],
                            in_=gate_s[q * QC:(q + 1) * QC,
                                       gx * 512:(gx + 1) * 512]),
                [f"gated{2 * gx}_{q}", f"gated{2 * gx + 1}_{q}"],
                [f"ggt{u}"])
            q3 = q % 3
            vec(lambda e, gx=gx, u=u, ib=ib, q3=q3: e.tensor_tensor(
                out=ysb[u].rearrange("p (h x) -> p h x", h=8),
                in0=PF[2][:, ib * 512:ib * 512 + 512]
                .rearrange("p (h x) -> p h x", h=8),
                in1=eltq[q3][:, gx * 8:(gx + 1) * 8][:, :, None]
                .broadcast_to([128, 8, 64]), op=OP.mult),
                [f"PS{4 + ib}", f"eltq{q3}"], [f"ysb{u}"])
            vec(lambda e, u=u, a_=a2: e.tensor_tensor(
                out=ysb[u], in0=ysb[u], in1=PF[a_][:, 0:512],
                op=OP.add), [f"ysb{u}", aks[0]], [f"ysb{u}"])
            pool(lambda e, u=u: e.tensor_tensor(
                out=ysb[u], in0=ysb[u], in1=ysd,
                op=OP.add), [f"ysb{u}", "ysd"], [f"ysb{u}"])
            vec(lambda e, gx=gx, a_=a2: e.tensor_tensor(
                out=Sst[:, gx * 512:(gx + 1) * 512],
                in0=sdec, in1=PF[a_][:, 512:1024],
                op=OP.add),
                ["sdec", aks[1]], [f"Sg{gx}"])

        def b_fin(ui):
            q, gx = ui // 8, ui % 8
            u = ui % 2
            scl(lambda e, u=u: e.activation(out=sqscr, in_=ysb[u],
                                            func=AF.Square,
                                            accum_out=ssq[u]),
                [f"ysb{u}"], ["sqx", f"ssq{u}"])
            vec(lambda e, u=u: e.tensor_scalar(
                out=rstd[u], in0=ssq[u], scalar1=1.0 / 512.0,
                scalar2=EPS, op0=OP.mult, op1=OP.add),
                [f"ssq{u}"], [f"rstd{u}"])
            scl(lambda e, u=u: e.activation(out=rstd[u], in_=rstd[u],
                                            func=AF.Ln),
                [f"rstd{u}"], [f"rstd{u}"])
            scl(lambda e, u=u: e.activation(out=rstd[u], in_=rstd[u],
                                            func=AF.Exp, scale=-0.5),
                [f"rstd{u}"], [f"rstd{u}"])
            vec(lambda e, u=u: e.scalar_tensor_tensor(
                out=yng, in0=ysb[u], scalar=rstd[u][:, 0:1],
                in1=ggt[u], op0=OP.mult, op1=OP.mult),
                [f"ysb{u}", f"rstd{u}", f"ggt{u}"], ["yng"])
            dma("gpsimd", lambda e, gx=gx: e.dma_start(
                out=S_bf[:, gx * 512:(gx + 1) * 512],
                in_=Sst[:, gx * 512:(gx + 1) * 512]),
                [f"Sg{gx}"], [f"Sbf{gx}"])
            for c4 in range(4):
                tr(PBA[1][:, 512 + c4 * 128:512 + (c4 + 1) * 128],
                   yng[:, c4 * 128:(c4 + 1) * 128],
                   ident_bf, ["yng", "idb"], "PBT1")
            vec(lambda e, gx=gx, q=q: e.tensor_copy(
                panel.rearrange("p (c t) -> p c t", c=NKC)
                [:, gx * 4:gx * 4 + 4, q * QC:(q + 1) * QC],
                PBA[1][:, 512:1024]
                .rearrange("p (c t) -> p c t", c=4)),
                ["PBT1"],
                [f"pan{gx * 4 + i}" for i in range(4)])

        for b in range(2):
            vec(lambda e, b=b: e.memset(NRL15[b], 1.0), [],
                [f"NRq2x{b}"])
            for hh in range(2):
                dma("gpsimd", lambda e, b=b, hh=hh:
                    e.dma_start(out=ILx[b][0:12, hh * 512:(hh + 1) * 512],
                                in_=ind12),
                    [f"Lr32x{b}"], [f"Lr32x{b}"])
        load_qgrp(0)
        load_nrl(0)
        load_nr2(0)
        overlay(bq2x[0], range(32, 40), "bq2x0")
        overlay(cq2x[0], range(40, 48), "cq2x0")

        for k in range(NU + 3):
            if 2 <= k <= NU + 1:
                b_asm(k - 2)
            if k < NU:
                b_front_d(k)
                b_scal(k)
            if 1 <= k <= NU:
                b_cde(k - 1)
            if k < NU:
                b_front(k)
            if 3 <= k <= NU + 2:
                b_fin(k - 3)
        A.pop_fence()

        if dbg:
            dbg1 = nc.dram_tensor("dbg1", [128, NKC * T], BF16).ap()
            for kc in range(NKC):
                dma("sync", lambda e, k=kc: e.dma_start(
                    out=dbg1[:, k * T:(k + 1) * T],
                    in_=panel[:, k * T:(k + 1) * T]),
                    [f"pan{kc}"], [f"dbg1_{kc}"])

        A.push()
        wsl3 = [A.bf16(128, NKC * 128, f"wsl3{i}") for i in range(2)]
        xr3 = [A.f32(128, T, f"xr3{i}") for i in range(2)]
        x2b = [A.f32(128, T, f"x2b{i}") for i in range(2)]
        gi3 = 0
        for oc in range(NKC):
            wb3 = oc % 2
            dma("gpsimd", lambda e, o=oc, w=wb3:
                e.dma_start(out=wsl3[w],
                            in_=w_out[o].rearrange("p kc m -> p (kc m)")),
                [], [f"wsl3{wb3}"])
            dma("scalq", lambda e, o=oc, w=wb3:
                e.dma_start(out=xr3[w],
                            in_=x_fm[o * 128:(o + 1) * 128, :]),
                [], [f"xr3{wb3}"])
            for it, (t0, tw) in enumerate(TT):
                bank = gi3 % 4
                for kc in range(NKC):
                    mm(PSA[bank][:, 0:tw],
                       wsl3[wb3][:, kc * 128:(kc + 1) * 128],
                       panel[:, kc * T + t0:kc * T + t0 + tw],
                       [f"wsl3{wb3}", f"pan{kc}"], f"PS{bank}",
                       start=(kc == 0), stop=(kc == NKC - 1))
                vec(lambda e, w=wb3, t0=t0, tw=tw, b=bank:
                    e.tensor_tensor(out=x2b[w][:, t0:t0 + tw],
                                    in0=PSA[b][:, 0:tw],
                                    in1=xr3[w][:, t0:t0 + tw], op=OP.add),
                    [f"PS{bank}", f"xr3{wb3}"], [f"x2b{wb3}"])
                gi3 += 1
            dma("sync", lambda e, o=oc, w=wb3:
                e.dma_start(out=x2_s[o * 128:(o + 1) * 128, :],
                            in_=x2b[w]),
                [f"x2b{wb3}"], [f"x2d{oc}"])
        A.pop_fence()

        phase_ln("x2", "L2", router=True)

        A.push()
        wslB = [A.bf16(128, NKC * 128, f"wslB{i}") for i in range(2)]
        blora = A.bf16(128, I, "blorak")
        wrt_sb = A.f32(128, NKC * 8, "wrtk")
        qred = A.f32(128, 64, "qredk")
        tbf = A.bf16(128, T, "tbf")
        xrB = A.f32(128, T, "xrB")
        obuf = A.f32(128, T, "obuf")
        ot1 = A.f32(128, 512, "ot1")
        lgt = A.f32(2, 8, ("lgt", "lgto"))
        mx2 = A.f32(2, 1, "mx2")
        rt8 = A.f32(8, 1, "rt8")
        rteff = A.f32(128, 1, "rteffk")

        dma("gpsimd", lambda e:
            e.dma_start(out=wslB[0],
                        in_=w_la.rearrange("p kc m -> p (kc m)")),
            [], ["wslB0"])
        for it, (t0, tw) in enumerate(TT):
            bank = it % 2
            for kc in range(NKC):
                mm(PSA[bank][:, 0:tw],
                   wslB[0][:, kc * 128:(kc + 1) * 128],
                   panel[:, kc * T + t0:kc * T + t0 + tw],
                   ["wslB0", f"pan{kc}"], f"PS{bank}",
                   start=(kc == 0), stop=(kc == NKC - 1))
            vec(lambda e, t0=t0, tw=tw, b=bank: e.tensor_copy(
                tbf[:, t0:t0 + tw], PSA[b][:, 0:tw]),
                [f"PS{bank}"], ["tbf"])
        dma("gpsimd", lambda e: e.dma_start(out=blora, in_=w_lb),
            [], ["blorak"])
        dma("sync", lambda e: e.dma_start(out=wrt_sb, in_=w_rt),
            [], ["wrtk"])
        dma("sync", lambda e: e.dma_start(out=qred, in_=q_out.ap()),
            ["q_out"], ["qredk"])
        for kc in range(NKC):
            mm(PSA[2][0:2, 0:8],
               qred[:, kc:64:32], wrt_sb[:, kc * 8:(kc + 1) * 8],
               ["qredk", "wrtk"], "PS2",
               start=(kc == 0), stop=False)
        mm(PSA[2][0:2, 0:8], ones1f[:, 0:2], rb_sb,
           ["o1f", "rbk"], "PS2", start=False, stop=True)
        vec(lambda e: e.tensor_copy(lgt, PSA[2][0:2, 0:8]), ["PS2"], ["lgt"])
        vec(lambda e: e.reduce_max(mx2, lgt, axis=AX.X), ["lgt"], ["mx2"])
        vec(lambda e: e.tensor_scalar(out=lgt, in0=lgt,
                                      scalar1=mx2[:, 0:1], scalar2=None,
                                      op0=OP.subtract), ["lgt", "mx2"], ["lgt"])
        scl(lambda e: e.activation(out=lgt, in_=lgt, func=AF.Exp),
            ["lgt"], ["lgt"])
        vec(lambda e: e.reduce_sum(mx2, lgt, axis=AX.X), ["lgt"], ["mx2"])
        vec(lambda e: e.reciprocal(mx2, mx2), ["mx2"], ["mx2"])
        vec(lambda e: e.tensor_scalar(out=lgt, in0=lgt,
                                      scalar1=mx2[:, 0:1], scalar2=None,
                                      op0=OP.mult), ["lgt", "mx2"], ["lgt"])
        mm(PSA[3][0:1, 0:8], bm_sb, lgt, ["bm", "lgt"], "PS3")
        vec(lambda e: e.tensor_copy(lgt[0:1, 0:8], PSA[3][0:1, 0:8]),
            ["PS3"], ["lgto"])
        tr(PSA[2][0:8, 0:1], lgt[0:1, 0:8], ident_f[0:1, 0:1],
           ["lgto", "idf"], "PS2")
        vec(lambda e: e.tensor_copy(rt8, PSA[2][0:8, 0:1]), ["PS2"], ["rt8"])
        mm(PSA[3][:, 0:1], e816_sb, rt8, ["e816", "rt8"], "PS3")
        vec(lambda e: e.tensor_copy(rteff, PSA[3][:, 0:1]), ["PS3"], ["rteffk"])

        blsc = [A.bf16(128, 128, f"blsc{i}") for i in range(2)]
        giB = 0
        for oc in range(NKC):
            wbB = oc % 2
            vec(lambda e, o=oc, w=wbB: e.tensor_scalar(
                out=blsc[w], in0=blora[:, o * 128:(o + 1) * 128],
                scalar1=rteff[:, 0:1], scalar2=None, op0=OP.mult),
                ["blorak", "rteffk"], [f"blsc{wbB}"])
            dma("gpsimd", lambda e, o=oc, w=wbB:
                e.dma_start(out=wslB[w],
                            in_=w_ffn[o].rearrange("p kc m -> p (kc m)")),
                [], [f"wslB{wbB}"])
            dma("scalq", lambda e, o=oc:
                e.dma_start(out=xrB, in_=x2_s[o * 128:(o + 1) * 128, :]),
                [f"x2d{oc}"], ["xrB"])
            for it, (t0, tw) in enumerate(TT):
                bank = giB % 4
                for kc in range(NKC):
                    mm(PSA[bank][:, 0:tw],
                       wslB[wbB][:, kc * 128:(kc + 1) * 128],
                       panel[:, kc * T + t0:kc * T + t0 + tw],
                       [f"wslB{wbB}", f"pan{kc}"], f"PS{bank}",
                       start=(kc == 0), stop=False)
                mm(PSA[bank][:, 0:tw],
                   blsc[wbB],
                   tbf[:, t0:t0 + tw],
                   [f"blsc{wbB}", "tbf"], f"PS{bank}", start=False, stop=True)
                scl(lambda e, t0=t0, tw=tw, b=bank, o=oc:
                    e.activation(out=ot1[:, 0:tw], in_=PSA[b][:, 0:tw],
                                 func=AF.Identity, bias=fb_sb[:, o:o + 1]),
                    [f"PS{bank}", "fb"], ["ot1"])
                vec(lambda e, t0=t0, tw=tw:
                    e.tensor_tensor(out=obuf[:, t0:t0 + tw],
                                    in0=ot1[:, 0:tw],
                                    in1=xrB[:, t0:t0 + tw], op=OP.add),
                    ["ot1", "xrB"], ["obuf"])
                giB += 1
            dma("sync", lambda e, o=oc:
                e.dma_start(out=out_fm[o * 128:(o + 1) * 128, :],
                            in_=obuf),
                ["obuf"], [f"outd{oc}"])
        A.pop_fence()

        S_.add("gpsimd", lambda e: e.memset(pxbc[0:1, 0:1], 0.0),
               reads=[f"outd{o}" for o in range(NKC)], writes=[],
               sem="POO", amt=1)

        nops, cnts = S_.finalize(block, sems)
    return nc, nops


def _prep_weights(inputs, T):
    ip = np.asarray(inputs["L0_attn_in_proj_w"])
    W_inT = np.ascontiguousarray(ip.T.astype(np.float32))
    slabs = np.zeros((49, 128, NKC, 128), np.float32)
    dtslab = np.zeros((D, 128), np.float32)
    dtslab[:, 0:H] = W_inT[:, I + CONV:I + CONV + H]
    slabs[0] = dtslab.reshape(NKC, 128, 128).transpose(1, 0, 2)
    for c in range(NHBC):
        blk = W_inT[:, I + c * 128:I + (c + 1) * 128]
        slabs[c + 1] = blk.reshape(NKC, 128, 128).transpose(1, 0, 2)
    w_hbcdt = np.ascontiguousarray(slabs)

    w_gate = np.ascontiguousarray(W_inT[:, 0:I].reshape(NKC, 128, I))

    op_ = np.asarray(inputs["L0_attn_out_proj_w"])
    nwv = np.asarray(inputs["L0_attn_norm_weight"]).astype(np.float32)
    OT = np.ascontiguousarray(op_.T.astype(np.float32)
                              * nwv[:, None])
    w_out = np.zeros((NKC, 128, NKC, 128), np.float32)
    for o in range(NKC):
        blk = OT[:, o * 128:(o + 1) * 128]
        w_out[o] = blk.reshape(NKC, 128, 128).transpose(1, 0, 2)

    Wf = np.asarray(inputs["L0_ffn_W"])
    FT = np.ascontiguousarray(Wf.T.astype(np.float32))
    w_ffn = np.zeros((NKC, 128, NKC, 128), np.float32)
    for o in range(NKC):
        blk = FT[:, o * 128:(o + 1) * 128]
        w_ffn[o] = blk.reshape(NKC, 128, 128).transpose(1, 0, 2)

    Am = np.asarray(inputs["L0_ffn_A"]).reshape(128, D)
    AT = Am.T
    w_la = np.ascontiguousarray(
        AT.reshape(NKC, 128, 128).transpose(1, 0, 2))
    Bm = np.asarray(inputs["L0_ffn_B"])
    w_lb = np.ascontiguousarray(
        Bm.transpose(0, 2, 1).reshape(128, I))
    rw = np.asarray(inputs["L0_ffn_router_W"])
    w_rt = np.ascontiguousarray(
        rw.T.reshape(NKC, 128, 8).transpose(1, 0, 2).reshape(128, NKC * 8))

    cw = np.asarray(inputs["L0_attn_conv_weight"])
    diag = np.zeros((NHBC, KCV, 128, 128), np.float32)
    idx = np.arange(128)
    for c in range(NHBC):
        for k in range(KCV):
            diag[c, k, idx, idx] = cw[c * 128:(c + 1) * 128, k]
    cwc = np.ascontiguousarray(
        cw.reshape(NHBC, 128, KCV).transpose(1, 0, 2).reshape(128,
                                                              NHBC * KCV))
    cb = np.asarray(inputs["L0_attn_conv_bias"])
    cbias = np.ascontiguousarray(cb.reshape(NHBC, 128).T)

    nw = np.asarray(inputs["L0_attn_norm_weight"])
    nwc = np.ascontiguousarray(nw.reshape(NKC, 128).T)
    fbv = np.asarray(inputs["L0_ffn_b"])
    fbc = np.ascontiguousarray(fbv.reshape(NKC, 128).T)

    return dict(
        w_hbcdt=w_hbcdt, w_gate=w_gate, w_out=w_out, w_ffn=w_ffn,
        w_la=w_la, w_lb=w_lb, w_rt=w_rt, diagw=diag, cwcols=cwc,
        cbias=cbias, nw_cols=nwc, fb_cols=fbc,
        dtb=np.asarray(inputs["L0_attn_dt_bias"]).reshape(H, 1),
        alog=np.asarray(inputs["L0_attn_A_log"]).reshape(H, 1),
        dvec_row=np.asarray(inputs["L0_attn_D"]).reshape(1, H),
        rb_row=np.asarray(inputs["L0_ffn_router_b"]).reshape(1, 8),
    )


def _consts(T, qcount):
    identity = np.eye(128, dtype=np.float32)
    tri = np.triu(np.full((128, 128), np.exp(30.0), np.float32))
    ind12 = np.zeros((12, 512), np.float32)
    for blk in range(3):
        for k in range(4):
            ind12[blk * 4 + k, k * 128:(k + 1) * 128] = 1.0
    e816m = np.zeros((8, 128), np.float32)
    for e in range(8):
        e816m[e, e * 16:(e + 1) * 16] = SCALING
    return identity, tri, ind12, e816m


def kernel(**inputs):
    x = np.asarray(inputs["L0_attn_hidden_states"])
    Bb, L, _ = x.shape
    T = (Bb * L) // NCORES
    QS, QE = 611, L - 1
    qcount = QE - QS

    shared = _prep_weights(inputs, T)
    identity, tri, ind12, e816m = _consts(T, qcount)
    shared.update(identity=identity, tri01=tri, ind12=ind12, e816=e816m)

    nc, nops = build_program(T)

    in_maps = []
    for c in range(NCORES):
        b = 0 if c < NCORES // 2 else 1
        ci = c % (NCORES // 2)
        t0, t1 = ci * T, (ci + 1) * T
        m = dict(shared)
        m["x_fm"] = np.ascontiguousarray(x[b, t0:t1, :].T)
        tm = np.zeros((1, T), np.float32)
        lo, hi = max(QS, t0), min(QE, t1)
        if hi > lo:
            tm[0, lo - t0:hi - t0] = 1.0 / qcount
        m["tokmask"] = tm
        chmv = np.zeros((H, 64), np.float32)
        chm1v = np.zeros((H, 64), np.float32)
        lo_r, hi_r = (0, 4) if b == 0 else (4, 8)
        for r in range(8):
            contributes = (lo_r <= r < hi_r) and (r < c)
            if contributes:
                for i in range(8):
                    mid = (r < i < c)
                    chmv[:, r * 8 + i] = 1.0 if mid else 0.0
                    chm1v[:, r * 8 + i] = 0.0 if mid else 1.0
        m["chmh"] = chmv
        m["chm1mh"] = chm1v
        bc = np.zeros((4, 8), np.float32)
        if ci > 0:
            bc[:, c - 1] = 1.0
        m["bndc"] = bc
        bmv = np.zeros((2, 1), np.float32)
        bmv[b, 0] = 1.0
        m["bmask"] = bmv
        bs = np.zeros((128, 64), np.float32)
        bs[:, b * 32:(b + 1) * 32] = 1.0
        m["bsel"] = bs
        in_maps.append(m)

    res = run_bass_kernel_spmd(nc, in_maps, core_ids=list(range(NCORES)))
    out = np.empty((Bb, L, D), np.float32)
    for c in range(NCORES):
        b = 0 if c < NCORES // 2 else 1
        ci = c % (NCORES // 2)
        out[b, ci * T:(ci + 1) * T, :] = res.results[c]["out_fm"].T
    return out


# version 16
# speedup vs baseline: 1.0152x; 1.0152x over previous
import numpy as np
import ml_dtypes
import concourse.bass as bass
import concourse.mybir as mybir
from concourse.bass_utils import run_bass_kernel_spmd

F32 = mybir.dt.float32
BF16 = mybir.dt.bfloat16
AF = mybir.ActivationFunctionType
OP = mybir.AluOpType
AX = mybir.AxisListType
BF = ml_dtypes.bfloat16

H, P, N, G, KCV = 64, 64, 128, 8, 4
D = 4096
I = 4096
CONV = 6144
QC = 128
EPS = 1e-5
NKC = 32
NHBC = 48
NCORES = 8
NHB = 16
SCALING = 2.0

STREAM_OF = {"sync": "sync", "scalar": "scalar", "vector": "vector",
             "tensor": "tensor", "gpsimd": "gpsimd", "scalq": "scalar"}
ENG_SEM = {"sync": "LDS", "scalq": "LDA", "tensor": "TEN", "vector": "VEC",
           "scalar": "SCL", "gpsimd": "POO"}
DMA_SEMS = ("LDS", "LDA", "LDG")


class Sched:
    def __init__(self):
        self.ops = []
        self.last_write = {}
        self.readers = {}

    def add(self, eng, fn, reads=(), writes=(), sem=None, amt=None):
        idx = len(self.ops)
        deps = set()
        for r in reads:
            w = self.last_write.get(r)
            if w is not None:
                deps.add(w)
        for wres in writes:
            w = self.last_write.get(wres)
            if w is not None:
                deps.add(w)
            for rd in self.readers.get(wres, ()):
                deps.add(rd)
        if sem is None:
            sem = ENG_SEM[eng]
        if amt is None:
            amt = 16 if sem in DMA_SEMS else 1
        self.ops.append({"i": idx, "eng": eng, "fn": fn, "deps": deps,
                         "sem": sem, "amt": amt, "waited": False})
        for wres in writes:
            self.last_write[wres] = idx
            self.readers[wres] = []
        for r in reads:
            self.readers.setdefault(r, []).append(idx)
        return idx

    def finalize(self, block, sems):
        ops = self.ops
        NL = 8
        lane_cnt = {}
        lane_prev = {}
        for op in ops:
            if op["sem"] in DMA_SEMS:
                q = op["sem"]
                i = lane_cnt.get(q, 0)
                lane_cnt[q] = i + 1
                lane = f"{q}{i % NL}"
                op["sem"] = lane
                prev = lane_prev.get(lane)
                if prev is not None:
                    op["lane_dep"] = prev
                lane_prev[lane] = op["i"]
        for op in ops:
            pruned = set()
            for d in op["deps"]:
                dop = ops[d]
                if dop["eng"] == "tensor" and op["eng"] == "tensor":
                    continue
                pruned.add(d)
            op["deps"] = pruned
            if "lane_dep" in op:
                op["deps"].add(op["lane_dep"])
        for op in ops:
            for d in op["deps"]:
                ops[d]["waited"] = True
            if op["sem"][:3] in DMA_SEMS or op["sem"].startswith("CC"):
                op["waited"] = True
        cnt = {}
        for op in ops:
            if op["waited"]:
                cnt[op["sem"]] = cnt.get(op["sem"], 0) + op["amt"]
                op["val"] = cnt[op["sem"]]
            else:
                op["val"] = None
        streams = {e: [] for e in ["sync", "scalar", "vector", "tensor",
                                   "gpsimd"]}
        for op in ops:
            streams[STREAM_OF[op["eng"]]].append(op)
        for sname, sops in streams.items():
            hw = {}
            for op in sops:
                byv = {}
                for d in op["deps"]:
                    dop = ops[d]
                    if dop["val"] is None:
                        continue
                    s, v = dop["sem"], dop["val"]
                    if hw.get(s, 0) >= v:
                        continue
                    byv[s] = max(byv.get(s, 0), v)
                op["waits"] = sorted(byv.items())
                for s, v in byv.items():
                    hw[s] = v

        def make(sops):
            def body(e):
                for op in sops:
                    for s, v in op["waits"]:
                        e.wait_ge(sems[s], v)
                    inst = op["fn"](e)
                    if op["waited"]:
                        inst.then_inc(sems[op["sem"]], op["amt"])
            return body
        for sname in ["sync", "scalar", "vector", "tensor", "gpsimd"]:
            getattr(block, sname)(make(streams[sname]))
        return len(ops), cnt


class Arena:

    def __init__(self, sched, tensor_ap, width_f32):
        self.S = sched
        self.t = tensor_ap
        self.tb = tensor_ap.bitcast(BF16)
        self.w = width_f32
        self.off = 0
        self.marks = []
        self.used = set()
        self.fence_idx = None

    def push(self):
        self.marks.append(self.off)

    def f32(self, parts, cols, key):
        assert self.off + cols <= self.w, f"arena overflow {self.off}+{cols}"
        v = self.t[0:parts, self.off:self.off + cols]
        self.off += cols
        self._reg(key)
        return v

    def bf16(self, parts, cols, key):
        colsf = (cols + 1) // 2
        assert self.off + colsf <= self.w, f"arena overflow {self.off}+{colsf}"
        v = self.tb[0:parts, 2 * self.off:2 * self.off + cols]
        self.off += colsf
        self._reg(key)
        return v

    def pairf32(self, cols, key_top, key_bot):
        assert self.off + cols <= self.w, "arena overflow (pair)"
        top = self.t[0:64, self.off:self.off + cols]
        bot = self.t[64:128, self.off:self.off + cols]
        self.off += cols
        self._reg(key_top)
        self._reg(key_bot)
        return top, bot

    def _reg(self, key):
        keys = key if isinstance(key, (tuple, list)) else (key,)
        for k in keys:
            self.used.add(k)
            if self.fence_idx is not None and k not in self.S.last_write:
                self.S.last_write[k] = self.fence_idx

    def pop_fence(self):
        self.off = self.marks.pop()
        used = sorted(self.used)
        idx = self.S.add("vector", lambda e: e.engine_nop(), reads=(),
                         writes=used)
        self.used = set()
        self.fence_idx = idx


def _ttiles(T):
    out, t = [], 0
    while t < T:
        w = min(512, T - t)
        out.append((t, w))
        t += w
    return out


def build_program(T, dbg=False):
    NQ = T // QC
    TT = _ttiles(T)
    T16 = T // 128

    nc = bass.Bass("TRN2", target_bir_lowering=False)

    def inp(name, shape, dt=F32):
        return nc.dram_tensor(name, shape, dt, kind="ExternalInput").ap()

    x_fm = inp("x_fm", [D, T])
    w_hbcdt = inp("w_hbcdt", [49, 128, NKC, 128])
    w_gate = inp("w_gate", [NKC, 128, I])
    w_out = inp("w_out", [NKC, 128, NKC, 128])
    w_ffn = inp("w_ffn", [NKC, 128, NKC, 128])
    w_la = inp("w_la", [128, NKC, 128])
    w_lb = inp("w_lb", [128, I])
    w_rt = inp("w_rt", [128, NKC * 8])
    diagw = inp("diagw", [NHBC, KCV, 128, 128])
    cwcols = inp("cwcols", [128, NHBC * KCV])
    cbias = inp("cbias", [128, NHBC])
    dtb = inp("dtb", [H, 1])
    alog = inp("alog", [H, 1])
    dvec_row = inp("dvec_row", [1, H])
    nw_cols = inp("nw_cols", [128, NKC])
    fb_cols = inp("fb_cols", [128, NKC])
    rb_row = inp("rb_row", [1, 8])
    identity = inp("identity", [128, 128])
    tri01 = inp("tri01", [128, 128])
    ind12 = inp("ind12", [12, 512])
    e816 = inp("e816", [8, 128])
    tokmask = inp("tokmask", [1, T])
    chmh = inp("chmh", [H, 64])
    chm1mh = inp("chm1mh", [H, 64])
    bndc = inp("bndc", [4, 8])
    bmask = inp("bmask", [2, 1])
    bsel_i = inp("bsel", [128, 64])

    out_fm = nc.dram_tensor("out_fm", [D, T], F32, kind="ExternalOutput").ap()

    xbc_s = nc.dram_tensor("xbc_s", [CONV, T], BF16).ap()
    gate_s = nc.dram_tensor("gate_s", [T, I], BF16).ap()
    xtm_s = nc.dram_tensor("xtm_s", [NQ, 128, I], BF16).ap()
    x2_s = nc.dram_tensor("x2_s", [D, T], F32).ap()
    lsp3 = nc.dram_tensor("lsp3_s", [3, H, T], BF16).ap()
    nrsp3 = nc.dram_tensor("nrsp3_s", [3, H, T], BF16).ap()
    lsp_s = [lsp3[i] for i in range(3)]
    nrsp_s = [nrsp3[i] for i in range(3)]
    elend_s = nc.dram_tensor("elend_s", [NQ, H], F32).ap()
    elt_s = nc.dram_tensor("elt_s", [NQ, 128, 64], BF16)
    coefb_s = nc.dram_tensor("coefb_s", [8, H], F32).ap()
    bnd_in = nc.dram_tensor("bnd_in", [4, CONV], BF16)
    bnd_out = nc.dram_tensor("bnd_out", [32, CONV], BF16)
    st_in = nc.dram_tensor("st_in", [128, H * P], BF16)
    st_out = nc.dram_tensor("st_out", [8 * 128, H * P], BF16)
    ca_in = nc.dram_tensor("ca_in", [H, 1], F32)
    ca_out = nc.dram_tensor("ca_out", [8 * H, 1], F32)
    q_in = nc.dram_tensor("q_in", [128, 64], F32)
    q_out = nc.dram_tensor("q_out", [128, 64], F32)

    from contextlib import ExitStack
    with ExitStack() as g:
        def sbt(name, shape, dt=F32):
            return g.enter_context(nc.sbuf_tensor(name, shape, dt))[:]

        panel = sbt("panel", [128, NKC * T], BF16)
        AR_W = 12544
        S_ = Sched()
        A = Arena(S_, sbt("arena", [128, AR_W], F32), AR_W)

        ident_bf = sbt("ident_bf", [128, 128], BF16)
        ident_f = sbt("ident_f", [128, 128], F32)
        tri_sb = sbt("tri_sb", [128, 128], F32)
        ind12_sb = sbt("ind12_sb", [12, 512], BF16)
        ones1f = sbt("ones1f", [1, 128], F32)
        ones1b = sbt("ones1b", [1, 128], BF16)
        ones3b = sbt("ones3b", [3, 128], BF16)
        onescol = sbt("onescol", [128, 1], F32)
        onescol_b = sbt("onescol_b", [128, 1], BF16)
        ones64x = sbt("ones64x", [H, QC], F32)
        cb_sb = sbt("cb_sb", [128, NHBC], F32)
        cw_sb = sbt("cw_sb", [128, NHBC * KCV], F32)
        dtb_sb = sbt("dtb_sb", [H, 1], F32)
        negA = sbt("negA", [H, 1], F32)
        nw_sb = sbt("nw_sb", [128, NKC], F32)
        fb_sb = sbt("fb_sb", [128, NKC], F32)
        dvrow_sb = sbt("dvrow_sb", [1, H], F32)
        rb_sb = sbt("rb_sb", [1, 8], F32)
        e816_sb = sbt("e816_sb", [8, 128], F32)
        bm_sb = sbt("bm_sb", [2, 1], F32)
        Sst = sbt("Sst", [128, H * P], F32)
        S_bf = sbt("S_bf", [128, H * P], BF16)
        D_b = sbt("D_b", [128, H], F32)
        pf_sb = sbt("pf_sb", [128, NHBC * 3], BF16)
        pxbc = sbt("pxbc", [128, NHBC * 3], BF16)
        qacc = sbt("qacc", [128, NKC], F32)

        PF = [g.enter_context(nc.psum_tensor(f"PF{i}", [128, 1024], F32))[:]
              for i in range(3)]
        PSA = [PF[i // 2][:, (i % 2) * 512:(i % 2) * 512 + 512]
               for i in range(6)]
        PBA = [g.enter_context(nc.psum_tensor(f"PBT{i}", [128, 1024],
                                              BF16))[:]
               for i in range(2)]

        sems = {}
        semnames = ["TEN", "VEC", "SCL", "POO", "CC1", "CC2a", "CC2b", "CC3"]
        for q in ["LDS", "LDA", "LDG"]:
            semnames += [f"{q}{i}" for i in range(8)]
        for s in semnames:
            sems[s] = g.enter_context(nc.semaphore(s))

        block = g.enter_context(nc.Block())

        def mm(out_ap, lhsT, rhs, reads, wkey, start=True, stop=True):
            S_.add("tensor",
                   lambda e, o=out_ap, l=lhsT, r=rhs, st=start, sp=stop:
                   e.matmul(o, l, r, start=st, stop=sp),
                   reads=reads, writes=[wkey])

        def tr(out_ap, in_ap, ident, reads, wkey):
            S_.add("tensor",
                   lambda e, o=out_ap, i_=in_ap, d=ident:
                   e.transpose(o, i_, d),
                   reads=reads, writes=[wkey])

        def vec(fn, reads, writes):
            S_.add("vector", fn, reads=reads, writes=writes)

        def pool(fn, reads, writes):
            S_.add("gpsimd", fn, reads=reads, writes=writes, sem="POO",
                   amt=1)

        def scl(fn, reads, writes):
            S_.add("scalar", fn, reads=reads, writes=writes)

        def dma(q, fn, reads, writes):
            sm = {"sync": "LDS", "scalq": "LDA", "gpsimd": "LDG"}[q]
            S_.add(q, fn, reads=reads, writes=writes, sem=sm, amt=16)

        def coll(name_sem, kind, op_, tin, tout, reads, writes):
            S_.add("gpsimd", lambda e, k=kind, o=op_, i_=tin, u=tout:
                   e.collective_compute(
                       k, o, replica_groups=[list(range(NCORES))],
                       ins=[i_.ap().opt()], outs=[u.ap().opt()]),
                   reads=reads, writes=writes, sem=name_sem, amt=1)

        ALLPAN = [f"pan{k}" for k in range(NKC)]
        ALLSG = [f"Sg{gg}" for gg in range(8)]

        for dst, src, rk in [(tri_sb, tri01, "tri"), (ident_f, identity, "idf"),
                             (cb_sb, cbias, "cb"), (cw_sb, cwcols, "cw"),
                             (dtb_sb, dtb, "dtbk"), (nw_sb, nw_cols, "nw"),
                             (fb_sb, fb_cols, "fb"),
                             (dvrow_sb, dvec_row, "dvr"),
                             (rb_sb, rb_row, "rbk"), (e816_sb, e816, "e816"),
                             (bm_sb, bmask, "bm"), (negA, alog, "negA")]:
            dma("sync", lambda e, d=dst, s=src: e.dma_start(out=d, in_=s),
                [], [rk])
        for dst, src, rk in [(ident_bf, identity, "idb"),
                             (ind12_sb, ind12, "ind12")]:
            dma("gpsimd", lambda e, d=dst, s=src: e.dma_start(out=d, in_=s),
                [], [rk])
        for tile, rk in [(ones1f, "o1f"), (ones1b, "o1b"), (ones3b, "o3b"),
                         (onescol, "oc"), (ones64x, "o64"),
                         (onescol_b, "ocb")]:
            vec(lambda e, t=tile: e.memset(t, 1.0), [], [rk])
        vec(lambda e: e.memset(Sst, 0.0), [], ALLSG)
        scl(lambda e: e.activation(out=negA, in_=negA, func=AF.Exp),
            ["negA"], ["negA"])
        vec(lambda e: e.tensor_scalar(out=negA, in0=negA, scalar1=-1.0,
                                      scalar2=None, op0=OP.mult),
            ["negA"], ["negA"])
        mm(PSA[4][:, 0:H], ones1f, dvrow_sb, ["o1f", "dvr"], "PS4")
        vec(lambda e: e.tensor_copy(D_b, PSA[4][:, 0:H]), ["PS4"], ["D_b"])

        def phase_ln(src, ph, router=False):
            A.push()
            a_b = A.bf16(128, T, f"ab{ph}")
            b_b = A.bf16(128, T, f"bb{ph}")
            xsq = [A.bf16(128, 512, f"xsq{i}{ph}") for i in range(2)]
            tmp_ln = [A.bf16(128, T, f"tmpln{i}{ph}") for i in range(2)]
            arow = A.f32(1, T, f"arow{ph}")
            brow = A.f32(1, T, f"brow{ph}")
            moff = A.off
            mrow = A.f32(1, T, f"mrow{ph}")
            for kc in range(NKC):
                if src == "x":
                    dma("gpsimd", lambda e, k=kc:
                        e.dma_start(out=panel[:, k * T:(k + 1) * T],
                                    in_=x_fm[k * 128:(k + 1) * 128, :]),
                        [], [f"pan{kc}"])
                else:
                    dma("gpsimd", lambda e, k=kc:
                        e.dma_start(out=panel[:, k * T:(k + 1) * T],
                                    in_=x2_s[k * 128:(k + 1) * 128, :]),
                        [f"x2d{kc}"], [f"pan{kc}"])
            for it, (t0, tw) in enumerate(TT):
                for kc in range(NKC):
                    ps = panel[:, kc * T + t0:kc * T + t0 + tw]
                    xb = kc % 2
                    scl(lambda e, o=xsq[xb][:, 0:tw], i_=ps:
                        e.activation(out=o, in_=i_, func=AF.Square),
                        [f"pan{kc}"], [f"xsq{xb}{ph}"])
                    mm(PSA[0][0:1, 0:tw], onescol_b, ps,
                       ["ocb", f"pan{kc}"], "PS0",
                       start=(kc == 0), stop=(kc == NKC - 1))
                    mm(PSA[1][0:1, 0:tw], onescol_b, xsq[xb][:, 0:tw],
                       ["ocb", f"xsq{xb}{ph}"], "PS1",
                       start=(kc == 0), stop=(kc == NKC - 1))
                vec(lambda e, t0=t0, tw=tw:
                    e.tensor_copy(mrow[:, t0:t0 + tw], PSA[0][0:1, 0:tw]),
                    ["PS0"], [f"mrow{ph}"])
                vec(lambda e, t0=t0, tw=tw:
                    e.tensor_copy(arow[:, t0:t0 + tw], PSA[1][0:1, 0:tw]),
                    ["PS1"], [f"arow{ph}"])
            vec(lambda e: e.tensor_scalar(out=mrow, in0=mrow,
                                          scalar1=1.0 / D, scalar2=None,
                                          op0=OP.mult),
                [f"mrow{ph}"], [f"mrow{ph}"])
            vec(lambda e: e.tensor_scalar(out=arow, in0=arow,
                                          scalar1=1.0 / D, scalar2=None,
                                          op0=OP.mult),
                [f"arow{ph}"], [f"arow{ph}"])
            vec(lambda e: e.scalar_tensor_tensor(out=brow, in0=mrow,
                                                 scalar=-1.0, in1=mrow,
                                                 op0=OP.mult, op1=OP.mult),
                [f"mrow{ph}"], [f"brow{ph}"])
            vec(lambda e: e.tensor_tensor(out=arow, in0=arow, in1=brow,
                                          op=OP.add),
                [f"arow{ph}", f"brow{ph}"], [f"arow{ph}"])
            vec(lambda e: e.tensor_scalar(out=arow, in0=arow, scalar1=EPS,
                                          scalar2=None, op0=OP.add),
                [f"arow{ph}"], [f"arow{ph}"])
            scl(lambda e: e.activation(out=arow, in_=arow, func=AF.Ln),
                [f"arow{ph}"], [f"arow{ph}"])
            scl(lambda e: e.activation(out=arow, in_=arow, func=AF.Exp,
                                       scale=-0.5),
                [f"arow{ph}"], [f"arow{ph}"])
            vec(lambda e: e.scalar_tensor_tensor(out=brow, in0=mrow,
                                                 scalar=-1.0, in1=arow,
                                                 op0=OP.mult, op1=OP.mult),
                [f"mrow{ph}", f"arow{ph}"], [f"brow{ph}"])
            for it, (t0, tw) in enumerate(TT):
                mm(PSA[2][:, 0:tw], ones1f, arow[:, t0:t0 + tw],
                   ["o1f", f"arow{ph}"], "PS2")
                mm(PSA[3][:, 0:tw], ones1f, brow[:, t0:t0 + tw],
                   ["o1f", f"brow{ph}"], "PS3")
                scl(lambda e, t0=t0, tw=tw:
                    e.copy(out=a_b[:, t0:t0 + tw], in_=PSA[2][:, 0:tw]),
                    ["PS2"], [f"ab{ph}"])
                scl(lambda e, t0=t0, tw=tw:
                    e.copy(out=b_b[:, t0:t0 + tw], in_=PSA[3][:, 0:tw]),
                    ["PS3"], [f"bb{ph}"])
            if router:
                tokm_sb = A.tb[0:1, 2 * moff:2 * moff + T]
                tkkey = f"mrow{ph}"
                mask_b = tmp_ln[1]
                mkkey = f"tmpln1{ph}"
                mab = A.bf16(128, T, "mab")
                sqs4 = tmp_ln[0]
                sqkey = f"tmpln0{ph}"
                qc_sb = A.f32(128, 64, "qck")
                bsel_sb = A.f32(128, 64, "bselk")
                mb1 = A.f32(1, 1, "mb1")
                mbcol = A.f32(128, 1, "mbcol")
                dma("gpsimd", lambda e: e.dma_start(out=tokm_sb, in_=tokmask),
                    [tkkey], [tkkey])
                dma("sync", lambda e: e.dma_start(out=bsel_sb, in_=bsel_i),
                    [], ["bselk"])
                for it, (t0, tw) in enumerate(TT):
                    mm(PSA[4][:, 0:tw], ones1b, tokm_sb[:, t0:t0 + tw],
                       ["o1b", tkkey], "PS4")
                    scl(lambda e, t0=t0, tw=tw:
                        e.copy(out=mask_b[:, t0:t0 + tw],
                               in_=PSA[4][:, 0:tw]),
                        ["PS4"], [mkkey])
                vec(lambda e: e.tensor_tensor(out=mab, in0=mask_b, in1=a_b,
                                              op=OP.mult),
                    [mkkey, f"ab{ph}"], ["mab"])
                for kc in range(NKC):
                    vec(lambda e, k=kc: e.scalar_tensor_tensor(
                        out=sqs4, in0=panel[:, k * T:(k + 1) * T],
                        scalar=1.0, in1=mab, op0=OP.mult, op1=OP.mult,
                        accum_out=qacc[:, k:k + 1]),
                        [f"pan{kc}", "mab"], [sqkey, f"qacc{kc}"])
                vec(lambda e: e.scalar_tensor_tensor(
                    out=mab[0:1, :], in0=brow, scalar=1.0, in1=tokm_sb,
                    op0=OP.mult, op1=OP.mult, accum_out=mb1),
                    [f"brow{ph}", tkkey], ["mab", "mb1"])
                mm(PSA[4][:, 0:1], ones1f, mb1, ["o1f", "mb1"], "PS4")
                vec(lambda e: e.tensor_copy(mbcol, PSA[4][:, 0:1]),
                    ["PS4"], ["mbcol"])
                QACCK = [f"qacc{k}" for k in range(NKC)]
                vec(lambda e: e.tensor_scalar(out=qacc, in0=qacc,
                                              scalar1=mbcol[:, 0:1],
                                              scalar2=None, op0=OP.add),
                    QACCK + ["mbcol"], QACCK)
                vec(lambda e: e.tensor_tensor(
                    out=qc_sb.rearrange("p (b k) -> p b k", b=2),
                    in0=qacc[:, None, :].broadcast_to([128, 2, NKC]),
                    in1=bsel_sb.rearrange("p (b k) -> p b k", b=2),
                    op=OP.mult),
                    QACCK + ["bselk"], ["qck"])
                dma("sync", lambda e: e.dma_start(out=q_in.ap(), in_=qc_sb),
                    ["qck"], ["q_in"])
                coll("CC3", "AllReduce", OP.add, q_in, q_out,
                     ["q_in"], ["q_out"])
            for kc in range(NKC):
                if kc % 5 < 2:
                    eng, pr = pool, 0
                else:
                    eng, pr = vec, 1
                eng(lambda e, k=kc, p=pr:
                    e.tensor_tensor(out=tmp_ln[p],
                                    in0=panel[:, k * T:(k + 1) * T],
                                    in1=a_b, op=OP.mult),
                    [f"pan{kc}", f"ab{ph}"], [f"tmpln{pr}{ph}"])
                eng(lambda e, k=kc, p=pr:
                    e.tensor_tensor(out=panel[:, k * T:(k + 1) * T],
                                    in0=tmp_ln[p], in1=b_b, op=OP.add),
                    [f"tmpln{pr}{ph}", f"bb{ph}"], [f"pan{kc}"])
            A.pop_fence()

        phase_ln("x", "L1")

        A.push()
        dt_raw_full = A.f32(128, T, ("dt_raw", "dt_pad"))
        dt_raw = dt_raw_full[0:H, :]
        A.push()
        wsl = [A.bf16(128, NKC * 128, f"wsl{i}") for i in range(2)]
        dsl = [A.bf16(128, KCV * 128, f"dsl{i}") for i in range(2)]
        raw = [A.bf16(128, T + 4, f"raw{i}") for i in range(2)]
        xbct = [A.bf16(128, T, f"xbct{i}") for i in range(2)]
        bndb = A.bf16(4, 128, "bndb")
        NDT = 8
        TQ = T // NDT
        CPP = TQ // QC
        slotA = A.f32(64, TQ, "dZa")
        slotBq = A.f32(64, TQ, "dZb")
        sCq = A.f32(64, TQ, "dCq")
        hspB = A.bf16(64, TQ, "dHsp")
        eltwb = A.bf16(64, TQ, "eltw")
        eltb = A.bf16(128, CPP * 64, "eltb")
        eLendF = A.f32(64, NQ, "eLF")
        elendt = A.f32(NQ, H, "elendtk")
        caH = A.f32(64, NDT, "caH")

        vec(lambda e: e.memset(raw[0][:, 0:4], 0.0), [], ["raw0"])
        vec(lambda e: e.memset(raw[1][:, 0:4], 0.0), [], ["raw1"])
        vec(lambda e: e.memset(bndb, 0.0), [], ["bndb"])

        def emit_dt_half(half):
            hc = half * TQ
            X = dt_raw_full[0:64, hc:hc + TQ]
            sC = sCq
            slotB = slotBq
            vec(lambda e: e.tensor_scalar(out=slotA, in0=X,
                                          scalar1=dtb_sb[:, 0:1],
                                          scalar2=None, op0=OP.add),
                ["dt_raw", "dtbk"], ["dZa"])
            vec(lambda e: e.scalar_tensor_tensor(out=slotB, in0=slotA,
                                                 scalar=-1.0, in1=slotA,
                                                 op0=OP.mult, op1=OP.max),
                ["dZa"], ["dZb"])
            scl(lambda e: e.activation(out=slotB, in_=slotB, func=AF.Exp,
                                       scale=-1.0), ["dZb"], ["dZb"])
            scl(lambda e: e.activation(out=slotB, in_=slotB, func=AF.Ln,
                                       bias=onescol[0:64, 0:1]),
                ["dZb", "oc"], ["dZb"])
            vec(lambda e: e.tensor_scalar(out=sC, in0=slotA, scalar1=0.0,
                                          scalar2=None, op0=OP.max),
                ["dZa"], ["dCq"])
            vec(lambda e: e.tensor_tensor(out=sC, in0=sC, in1=slotB,
                                          op=OP.add),
                ["dCq", "dZb"], ["dCq"])
            vec(lambda e: e.tensor_scalar(out=X, in0=sC,
                                          scalar1=negA[:, 0:1],
                                          scalar2=None, op0=OP.mult),
                ["dCq", "negA", "dt_raw"], ["dt_raw"])
            for qq in range(CPP):
                vec(lambda e, qq=qq: e.tensor_tensor_scan(
                    out=slotA[:, qq * QC:(qq + 1) * QC], data0=ones64x,
                    data1=X[:, qq * QC:(qq + 1) * QC], initial=0.0,
                    op0=OP.mult, op1=OP.add), ["dt_raw", "o64"], ["dZa"])
            vec(lambda e, half=half:
                e.reduce_sum(caH[:, half:half + 1], X, axis=AX.X),
                ["dt_raw"], ["caH"])
            scl(lambda e, half=half: e.activation(
                out=eLendF[:, half * CPP:(half + 1) * CPP],
                in_=slotA[:, QC - 1::QC], func=AF.Exp), ["dZa"], ["eLF"])
            scl(lambda e: e.activation(out=eltwb, in_=slotA, func=AF.Exp),
                ["dZa"], ["eltw"])
            scl(lambda e: e.activation(out=slotB, in_=sC, func=AF.Ln),
                ["dCq"], ["dZb"])
            vec(lambda e: e.tensor_tensor(out=sC, in0=slotB, in1=slotA,
                                          op=OP.subtract),
                ["dZa", "dZb"], ["dCq"])
            for ti, (curv, ck, outs) in enumerate(
                    [(slotA, "dZa", lsp_s), (sCq, "dCq", nrsp_s)]):
                for si in range(3):
                    vec(lambda e, c=curv: e.tensor_copy(hspB, c),
                        [ck], ["dHsp"])
                    dma("sync", lambda e, o=outs[si], hc=hc:
                        e.dma_start(out=o[:, hc:hc + TQ], in_=hspB),
                        ["dHsp"], [f"sp{ti}{si}h{half}"])
                    if si < 2:
                        vec(lambda e, c=curv:
                            e.tensor_tensor(out=c, in0=c, in1=hspB,
                                            op=OP.subtract),
                            [ck, "dHsp"], [ck])

        def emit_dt_trans(half):
            for i in range(CPP):
                tr(PBA[1][:, i * 64:(i + 1) * 64],
                   eltwb[:, i * 128:(i + 1) * 128],
                   ident_bf[0:64, 0:64],
                   ["eltw", "idb"], "PBT1")
            vec(lambda e: e.tensor_copy(eltb, PBA[1][:, 0:CPP * 64]),
                ["PBT1"], ["eltb"])
            dma("sync", lambda e, half=half: e.dma_start(
                out=elt_s.ap()[half * CPP:(half + 1) * CPP]
                .rearrange("q p h -> p q h"),
                in_=eltb.rearrange("p (q h) -> p q h", q=CPP)),
                ["eltb"], [f"eltd{half}"])

        def emit_dt_fin():
            tr(PSA[5][0:NQ, 0:H], eLendF, ident_f[0:H, 0:H],
               ["eLF", "idf"], "PS5")
            vec(lambda e: e.tensor_copy(elendt, PSA[5][0:NQ, 0:H]),
                ["PS5"], ["elendtk"])
            dma("sync", lambda e: e.dma_start(out=elend_s, in_=elendt),
                ["elendtk"], ["elend_d"])
            vec(lambda e: e.tensor_tensor(out=caH[:, 0:4],
                                          in0=caH[:, 0:4],
                                          in1=caH[:, 4:8], op=OP.add),
                ["caH"], ["caH"])
            vec(lambda e: e.tensor_tensor(out=caH[:, 0:2],
                                          in0=caH[:, 0:2],
                                          in1=caH[:, 2:4], op=OP.add),
                ["caH"], ["caH"])
            vec(lambda e: e.tensor_tensor(out=caH[:, 0:1],
                                          in0=caH[:, 0:1],
                                          in1=caH[:, 1:2], op=OP.add),
                ["caH"], ["caH"])
            scl(lambda e: e.activation(out=caH[:, 0:1],
                                       in_=caH[:, 0:1], func=AF.Exp),
                ["caH"], ["caH"])
            dma("sync", lambda e: e.dma_start(out=ca_in.ap(),
                                              in_=caH[:, 0:1]),
                ["caH"], ["ca_in"])

        gidx = 0
        cidx = 0
        for oc in range(49):
            wb = oc % 2
            dma("gpsimd", lambda e, o=oc, w=wb:
                e.dma_start(out=wsl[w],
                            in_=w_hbcdt[o].rearrange("p kc m -> p (kc m)")),
                [], [f"wsl{wb}"])
            if oc >= 1:
                dma("gpsimd", lambda e, o=oc, w=wb:
                    e.dma_start(out=dsl[w].rearrange("p (k m) -> p k m",
                                                     k=KCV),
                                in_=diagw[o - 1]
                                .rearrange("k p m -> p k m")),
                    [], [f"dsl{wb}"])
            for it, (t0, tw) in enumerate(TT):
                bank = gidx % 4
                for kc in range(NKC):
                    mm(PSA[bank][:, 0:tw],
                       wsl[wb][:, kc * 128:(kc + 1) * 128],
                       panel[:, kc * T + t0:kc * T + t0 + tw],
                       [f"wsl{wb}", f"pan{kc}"], f"PS{bank}",
                       start=(kc == 0), stop=(kc == NKC - 1))
                scl(lambda e, w=wb, t0=t0, tw=tw, b=bank:
                    e.copy(out=raw[w][:, 4 + t0:4 + t0 + tw],
                           in_=PSA[b][:, 0:tw]),
                    [f"PS{bank}"], [f"raw{wb}"])
                gidx += 1
            if oc == 0:
                vec(lambda e: e.tensor_copy(dt_raw, raw[0][0:H, 4:4 + T]),
                    ["raw0"], ["dt_raw"])
            else:
                for it, (t0, tw) in enumerate(TT):
                    bank = 4 + (cidx % 2)
                    for k in range(KCV):
                        mm(PSA[bank][:, 0:tw],
                           dsl[wb][:, k * 128:(k + 1) * 128],
                           raw[wb][:, t0 + k + 1:t0 + k + 1 + tw],
                           [f"dsl{wb}", f"raw{wb}"], f"PS{bank}",
                           start=(k == 0), stop=(k == KCV - 1))
                    scl(lambda e, w=wb, t0=t0, tw=tw, b=bank, o=oc:
                        e.activation(out=xbct[w][:, t0:t0 + tw],
                                     in_=PSA[b][:, 0:tw], func=AF.Silu,
                                     bias=cb_sb[:, o - 1:o]),
                        [f"PS{bank}", "cb"], [f"xbct{wb}"])
                    cidx += 1
                vec(lambda e, w=wb, o=oc:
                    e.tensor_copy(pf_sb[:, (o - 1) * 3:(o - 1) * 3 + 3],
                                  raw[w][:, 4:7]),
                    [f"raw{wb}"], ["pf"])
                tr(PBA[0][0:3, 0:128], raw[wb][:, T + 1:T + 4], ident_bf,
                   [f"raw{wb}", "idb"], "PBT0")
                vec(lambda e: e.tensor_copy(bndb[0:3, :], PBA[0][0:3, 0:128]),
                    ["PBT0"], ["bndb"])
                dma("sync", lambda e, o=oc:
                    e.dma_start(out=bnd_in.ap()[:, (o - 1) * 128:o * 128],
                                in_=bndb),
                    ["bndb"], ["bnd_in"])
                dma("sync", lambda e, w=wb, o=oc:
                    e.dma_start(out=xbc_s[(o - 1) * 128:o * 128, :],
                                in_=xbct[w]),
                    [f"xbct{wb}"], [f"xbcd{oc - 1}"])
            if 2 <= oc <= 9:
                emit_dt_half(oc - 2)
            elif 11 <= oc <= 18:
                emit_dt_trans(oc - 11)
            elif oc == 20:
                emit_dt_fin()
        coll("CC1", "AllGather", OP.bypass, bnd_in, bnd_out,
             ["bnd_in"], ["bnd_out"])
        A.pop_fence()

        A.pop_fence()
        A.push()
        bnd_tm = A.bf16(4, CONV, "bnd_tm")
        rbuf = [A.bf16(4, CONV, f"rbuf{i}") for i in range(2)]
        bndc_sb = A.f32(4, 8, "bndck")
        rawx = A.bf16(128, 8, "rawx")
        pacc = A.f32(128, 3, "pacc")
        dma("sync", lambda e: e.dma_start(out=bndc_sb, in_=bndc),
            [], ["bndck"])
        vec(lambda e: e.memset(bnd_tm, 0.0), [], ["bnd_tm"])
        for r in range(8):
            rb_ = r % 2
            dma("scalq", lambda e, r=r, w=rb_:
                e.dma_start(out=rbuf[w],
                            in_=bnd_out.ap()[r * 4:(r + 1) * 4, :]),
                ["bnd_out"], [f"rbuf{rb_}"])
            vec(lambda e, r=r, w=rb_: e.scalar_tensor_tensor(
                out=bnd_tm, in0=rbuf[w], scalar=bndc_sb[:, r:r + 1],
                in1=bnd_tm, op0=OP.mult, op1=OP.add),
                [f"rbuf{rb_}", "bnd_tm", "bndck"], ["bnd_tm"])
        for c in range(NHBC):
            tr(PBA[1][:, 0:4], bnd_tm[0:4, c * 128:(c + 1) * 128],
               ident_bf[0:4, 0:4], ["bnd_tm", "idb"], "PBT1")
            vec(lambda e: e.tensor_copy(rawx[:, 0:3], PBA[1][:, 0:3]),
                ["PBT1"], ["rawx"])
            vec(lambda e, c=c: e.tensor_copy(rawx[:, 3:6],
                                             pf_sb[:, c * 3:c * 3 + 3]),
                ["pf"], ["rawx"])
            vec(lambda e, c=c: e.tensor_scalar(
                out=pacc, in0=rawx[:, 0:3],
                scalar1=cw_sb[:, c * KCV:c * KCV + 1], scalar2=None,
                op0=OP.mult), ["rawx", "cw"], ["pacc"])
            for k in range(1, KCV):
                vec(lambda e, c=c, k=k: e.scalar_tensor_tensor(
                    out=pacc, in0=rawx[:, k:k + 3],
                    scalar=cw_sb[:, c * KCV + k:c * KCV + k + 1],
                    in1=pacc, op0=OP.mult, op1=OP.add),
                    ["rawx", "cw", "pacc"], ["pacc"])
            scl(lambda e, c=c: e.activation(out=pxbc[:, c * 3:c * 3 + 3],
                                            in_=pacc, func=AF.Silu,
                                            bias=cb_sb[:, c:c + 1]),
                ["pacc", "cb"], ["pxbc"])
        A.pop_fence()

        def load_nr(NRq, q, tag):
            for si in range(3):
                dma("sync", lambda e, si=si, q=q:
                    e.dma_start(out=NRq[si * 4:(si + 1) * 4, :]
                                .rearrange("k (hb t) -> k hb t", hb=NHB),
                                in_=nrsp_s[si]
                                .rearrange("(hb k) t -> k hb t", k=4)
                                [:, :, q * QC:(q + 1) * QC]),
                    [f"sp1{si}h{q // 2}"], [tag])

        def load_l3(Lr3, q, quarter, tag, nh=16):
            for si in range(3):
                dma("sync", lambda e, si=si, q=q, hf=quarter, nh=nh:
                    e.dma_start(out=Lr3[si:si + 1]
                                .rearrange("p (h t) -> p h t", h=nh),
                                in_=lsp_s[si][hf * nh:(hf + 1) * nh,
                                              q * QC:(q + 1) * QC]
                                .unsqueeze(0)),
                    [f"sp0{si}h{q // 2}"], [tag])

        def load_elrow(dst, q, tag):
            dma("sync", lambda e, q=q:
                e.dma_start(out=dst, in_=elend_s[q:q + 1, :]),
                ["elend_d"], [tag])

        def overlay(buf, chunks, tag):
            for li, c in enumerate(chunks):
                vec(lambda e, li=li, c=c:
                    e.tensor_copy(buf[:, li * 128:li * 128 + 3],
                                  pxbc[:, c * 3:c * 3 + 3]),
                    ["pxbc", tag], [tag])

        A.push()
        xfm_q = A.bf16(128, I, "xfmq")
        xtm_q = A.bf16(128, I, "xtmq")
        bq = A.bf16(128, G * 128, "bq")
        btm = A.bf16(128, G * 128, "btm")
        NRq = A.bf16(12, NHB * QC, "NRqA")
        Lr3 = A.bf16(3, 32 * QC, "Lr3A")
        elr = A.f32(1, H, "elrA")
        cumAq = A.f32(128, H, "cumAqA")
        wcl = A.f32(128, H, "wclA")
        wcols = A.f32(128, H, "wcolsA")
        bw = A.bf16(128, 1024, "bwA")
        sttb = A.f32(128, 512, "sttA")

        for q in range(NQ):
            load_nr(NRq, q, "NRqA")
            load_elrow(elr, q, "elrA")
            dma("sync", lambda e, q=q:
                e.dma_start(out=xfm_q.rearrange("p (c t) -> p c t", c=32),
                            in_=xbc_s[0:I, q * QC:(q + 1) * QC]
                            .rearrange("(c p) t -> p c t", p=128)),
                [f"xbcd{c}" for c in range(32)], ["xfmq"])
            dma("sync", lambda e, q=q:
                e.dma_start(out=bq.rearrange("p (c t) -> p c t", c=8),
                            in_=xbc_s[I:I + G * 128, q * QC:(q + 1) * QC]
                            .rearrange("(c p) t -> p c t", p=128)),
                [f"xbcd{32 + c}" for c in range(8)], ["bq"])
            if q == 0:
                overlay(xfm_q, range(32), "xfmq")
                overlay(bq, range(32, 40), "bq")
            mm(PSA[5][:, 0:H], ones1f, elr, ["o1f", "elrA"], "PS5")
            vec(lambda e: e.tensor_copy(cumAq, PSA[5][:, 0:H]),
                ["PS5"], ["cumAqA"])
            for c8 in range(0, 32, 8):
                for c in range(c8, c8 + 8):
                    tr(PBA[0][:, (c - c8) * 128:(c - c8 + 1) * 128],
                       xfm_q[:, c * 128:(c + 1) * 128], ident_bf,
                       ["xfmq", "idb"], "PBT0")
                scl(lambda e, c8=c8:
                    e.copy(out=xtm_q[:, c8 * 128:(c8 + 8) * 128],
                           in_=PBA[0]),
                    ["PBT0"], ["xtmq"])
            dma("sync", lambda e, q=q: e.dma_start(out=xtm_s[q], in_=xtm_q),
                ["xtmq"], [f"xtmd{q}"])
            for gx in range(8):
                tr(PBA[1][:, gx * 128:(gx + 1) * 128],
                   bq[:, gx * 128:(gx + 1) * 128], ident_bf,
                   ["bq", "idb"], "PBT1")
            scl(lambda e: e.copy(out=btm, in_=PBA[1]),
                ["PBT1"], ["btm"])
            for half in range(2):
                load_l3(Lr3, q, half, "Lr3A", nh=32)
                for hbl in range(8):
                    hb = half * 8 + hbl
                    mm(PSA[4][:, hb * 4:(hb + 1) * 4],
                       NRq[:, hb * QC:(hb + 1) * QC],
                       ind12_sb[:, QC - 1::QC],
                       ["NRqA", "ind12"], "PS4", start=True, stop=False)
                    mm(PSA[4][:, hb * 4:(hb + 1) * 4],
                       ones3b,
                       Lr3[:, hbl * 4 * QC + QC - 1:(hbl + 1) * 4 * QC:QC],
                       ["o3b", "Lr3A"], "PS4", start=False, stop=True)
            vec(lambda e: e.tensor_scalar(out=wcl, in0=PSA[4][:, 0:H],
                                          scalar1=30.0, scalar2=None,
                                          op0=OP.min), ["PS4"], ["wclA"])
            scl(lambda e: e.activation(out=wcols, in_=wcl, func=AF.Exp),
                ["wclA"], ["wcolsA"])
            for gx in range(8):
                ub = 2 + (gx % 2)
                pool(lambda e, gx=gx: e.tensor_tensor(
                    out=bw.rearrange("p (h n) -> p h n", h=8),
                    in0=btm[:, gx * 128:(gx + 1) * 128][:, None, :]
                    .broadcast_to([128, 8, 128]),
                    in1=wcols[:, gx * 8:(gx + 1) * 8][:, :, None]
                    .broadcast_to([128, 8, 128]),
                    op=OP.mult), ["btm", "wcolsA"], ["bwA"])
                for hl8 in range(8):
                    h = gx * 8 + hl8
                    mm(PSA[ub][:, hl8 * 64:(hl8 + 1) * 64],
                       bw[:, hl8 * 128:(hl8 + 1) * 128],
                       xtm_q[:, h * 64:(h + 1) * 64],
                       ["bwA", "xtmq"], f"PS{ub}")
                pool(lambda e, gx=gx: e.tensor_tensor(
                    out=sttb.rearrange("p (h x) -> p h x", h=8),
                    in0=Sst[:, gx * 512:(gx + 1) * 512]
                    .rearrange("p (h x) -> p h x", h=8),
                    in1=cumAq[:, gx * 8:(gx + 1) * 8][:, :, None]
                    .broadcast_to([128, 8, 64]),
                    op=OP.mult), [f"Sg{gx}", "cumAqA"], ["sttA"])
                vec(lambda e, gx=gx, b=ub: e.tensor_tensor(
                    out=Sst[:, gx * 512:(gx + 1) * 512], in0=sttb,
                    in1=PSA[b], op=OP.add),
                    ["sttA", f"PS{ub}"], [f"Sg{gx}"])
        dma("gpsimd", lambda e: e.dma_start(out=st_in.ap(), in_=Sst),
            ALLSG, ["st_in"])
        coll("CC2a", "AllGather", OP.bypass, st_in, st_out,
             ["st_in"], ["st_out"])
        coll("CC2b", "AllGather", OP.bypass, ca_in, ca_out,
             ["ca_in"], ["ca_out"])
        A.pop_fence()

        A.push()
        GS = 256
        NGS = I // GS
        gsl = [A.bf16(128, NKC * GS, f"gsl{i}") for i in range(2)]
        gout = [A.bf16(128, GS, f"gout{i}") for i in range(2)]
        gg = 0
        for s in range(NGS):
            wb2 = s % 2
            dma("gpsimd", lambda e, s=s, w=wb2:
                e.dma_start(out=gsl[w].rearrange("p (kc m) -> p kc m",
                                                 kc=NKC),
                            in_=w_gate[:, :, s * GS:(s + 1) * GS]
                            .rearrange("kc p m -> p kc m")),
                [], [f"gsl{wb2}"])
            for tt in range(T16):
                bank = gg % 4
                for kc in range(NKC):
                    mm(PSA[bank][:, 0:GS],
                       panel[:, kc * T + tt * 128:kc * T + (tt + 1) * 128],
                       gsl[wb2][:, kc * GS:(kc + 1) * GS],
                       [f"gsl{wb2}", f"pan{kc}"], f"PS{bank}",
                       start=(kc == 0), stop=(kc == NKC - 1))
                ob = gg % 2
                scl(lambda e, b=bank, o=ob:
                    e.activation(out=gout[o], in_=PSA[b][:, 0:GS],
                                 func=AF.Silu),
                    [f"PS{bank}"], [f"gout{ob}"])
                dma("sync", lambda e, s=s, tt=tt, o=ob:
                    e.dma_start(out=gate_s[tt * 128:(tt + 1) * 128,
                                           s * GS:(s + 1) * GS],
                                in_=gout[o]),
                    [f"gout{ob}"], [f"gated{s}_{tt}"])
                gg += 1
        A.pop_fence()

        A.push()
        cumAg = A.f32(H, 8, "cumAg")
        chm_sb = A.f32(H, 64, "chmk")
        chm1_sb = A.f32(H, 64, "chm1k")
        coefs = A.f32(H, 8, "coefs")
        ct1 = A.f32(H, 8, "ct1")
        ct2 = A.f32(H, 4, "ct2")
        ct3 = A.f32(H, 2, "ct3")
        coefT = A.f32(8, H, "coefT")
        coefrow = A.f32(1, 512, "coefrow")
        Gr = A.bf16(128, H * P, "Gr")
        cfb = A.f32(128, H, "cfb")
        tbuf = A.f32(128, H * P, "tbuf")
        dma("sync", lambda e: e.dma_start(out=chm_sb, in_=chmh), [], ["chmk"])
        dma("sync", lambda e: e.dma_start(out=chm1_sb, in_=chm1mh),
            [], ["chm1k"])
        dma("sync", lambda e: e.dma_start(
            out=cumAg,
            in_=ca_out.ap().rearrange("(r h) x -> h (r x)", r=8),
            allow_slow_non_contiguous=True), ["ca_out"], ["cumAg"])
        for r in range(8):
            vec(lambda e, r=r: e.tensor_tensor(
                out=ct1, in0=cumAg, in1=chm_sb[:, r * 8:(r + 1) * 8],
                op=OP.mult), ["cumAg", "chmk"], ["ct1"])
            vec(lambda e, r=r: e.tensor_tensor(
                out=ct1, in0=ct1, in1=chm1_sb[:, r * 8:(r + 1) * 8],
                op=OP.add), ["ct1", "chm1k"], ["ct1"])
            vec(lambda e: e.tensor_tensor(out=ct2, in0=ct1[:, 0:4],
                                          in1=ct1[:, 4:8], op=OP.mult),
                ["ct1"], ["ct2"])
            vec(lambda e: e.tensor_tensor(out=ct3, in0=ct2[:, 0:2],
                                          in1=ct2[:, 2:4], op=OP.mult),
                ["ct2"], ["ct3"])
            vec(lambda e, r=r: e.tensor_tensor(out=coefs[:, r:r + 1],
                                               in0=ct3[:, 0:1],
                                               in1=ct3[:, 1:2], op=OP.mult),
                ["ct3"], ["coefs"])
        tr(PSA[5][0:8, 0:H], coefs, ident_f[0:H, 0:H], ["coefs", "idf"], "PS5")
        vec(lambda e: e.tensor_copy(coefT, PSA[5][0:8, 0:H]),
            ["PS5"], ["coefT"])
        dma("sync", lambda e: e.dma_start(out=coefb_s, in_=coefT),
            ["coefT"], ["coefb_d"])
        dma("sync", lambda e: e.dma_start(
            out=coefrow.rearrange("p (r h) -> p r h", r=8),
            in_=coefb_s.unsqueeze(0)), ["coefb_d"], ["coefrow"])
        vec(lambda e: e.memset(Sst, 0.0), [], ALLSG)
        for r in range(8):
            dma("scalq", lambda e, r=r:
                e.dma_start(out=Gr,
                            in_=st_out.ap()[r * 128:(r + 1) * 128, :]),
                ["st_out"], ["Gr"])
            mm(PSA[4][:, 0:H], ones1f, coefrow[:, r * H:(r + 1) * H],
               ["o1f", "coefrow"], "PS4")
            vec(lambda e: e.tensor_copy(cfb, PSA[4][:, 0:H]),
                ["PS4"], ["cfb"])
            vec(lambda e: e.tensor_tensor(
                out=tbuf.rearrange("p (h x) -> p h x", h=H),
                in0=Gr.rearrange("p (h x) -> p h x", h=H),
                in1=cfb[:, :, None].broadcast_to([128, H, P]),
                op=OP.mult), ["Gr", "cfb"], ["tbuf"])
            for gx in range(8):
                vec(lambda e, gx=gx: e.tensor_tensor(
                    out=Sst[:, gx * 512:(gx + 1) * 512],
                    in0=tbuf[:, gx * 512:(gx + 1) * 512],
                    in1=Sst[:, gx * 512:(gx + 1) * 512], op=OP.add),
                    ["tbuf", f"Sg{gx}"], [f"Sg{gx}"])
        for gx in range(8):
            scl(lambda e, gx=gx: e.copy(
                out=S_bf[:, gx * 512:(gx + 1) * 512],
                in_=Sst[:, gx * 512:(gx + 1) * 512]),
                [f"Sg{gx}"], [f"Sbf{gx}"])
        A.pop_fence()

        A.push()
        xtmu = [A.bf16(128, 512, f"xtmu{i}") for i in range(2)]
        bq2x = [A.bf16(128, G * 128, f"bq2x{i}") for i in range(2)]
        cq2x = [A.bf16(128, G * 128, f"cq2x{i}") for i in range(2)]
        btm2 = A.bf16(128, G * 128, "btm2")
        Gm = A.bf16(128, G * 128, "Gm")
        NRL15 = [A.bf16(15, 4 * QC, f"NRq2x{i}") for i in range(2)]
        ILx = [A.bf16(15, 8 * QC, f"Lr32x{i}") for i in range(2)]
        elrx = [A.f32(1, H, f"elrx{i}") for i in range(2)]
        cumAq2 = [A.f32(128, H, f"cumAq2_{i}") for i in range(2)]
        dmy = A.f32(128, 1024, "dmy")
        wd = [A.bf16(128, 1024, f"wd{i}") for i in range(2)]
        wt = [A.bf16(128, 1024, (f"wtA{i}", f"wtB{i}")) for i in range(2)]
        xw = [A.bf16(128, 512, f"xw{i}") for i in range(2)]
        ysd = A.bf16(128, 512, "ysd")
        sdec = A.f32(128, 512, "sdec")
        eltq = [A.bf16(128, 64, f"eltq{i}") for i in range(3)]
        ysb = [A.f32(128, 512, f"ysb{i}") for i in range(2)]
        sqscr = A.bf16(128, 512, "sqx")
        ssq = [A.f32(128, 1, f"ssq{i}") for i in range(2)]
        rstd = [A.f32(128, 1, f"rstd{i}") for i in range(2)]
        yng = A.bf16(128, 512, "yng")
        ggt = [A.bf16(128, 512, f"ggt{i}") for i in range(2)]
        b30 = A.f32(128, 1, "b30")
        vec(lambda e: e.memset(b30, 30.0), [], ["b30"])
        E30 = float(np.exp(30.0))

        NU = NQ * 8

        def load_nrl(tu):
            q, gx = tu // 8, tu % 8
            b = tu % 2
            dma("sync", lambda e, q=q, gx=gx, b=b:
                e.dma_start(
                    out=ILx[b][12:15, :]
                    .rearrange("si (h t) -> si h t", h=8),
                    in_=lsp3[:, gx * 8:(gx + 1) * 8,
                             q * QC:(q + 1) * QC]),
                [f"sp0{si}h{q // 2}" for si in range(3)], [f"Lr32x{b}"])

        def load_nr2(g):
            tu = 2 * g
            q, gx = tu // 8, tu % 8
            b = g % 2
            for si in range(3):
                dma("sync", lambda e, si=si, q=q, gx=gx, b=b:
                    e.dma_start(
                        out=NRL15[b][si * 4:(si + 1) * 4, :]
                        .rearrange("k (hb t) -> k hb t", hb=4),
                        in_=nrsp3[si]
                        .rearrange("(hb k) t -> k hb t", k=4)
                        [:, gx * 2:gx * 2 + 4, q * QC:(q + 1) * QC]),
                    [f"sp1{si}h{q // 2}"], [f"NRq2x{b}"])

        def load_qgrp(q):
            b = q % 2
            dma("sync", lambda e, q=q, b=b:
                e.dma_start(out=bq2x[b].rearrange("p (c t) -> p c t", c=8),
                            in_=xbc_s[I:I + G * 128,
                                      q * QC:(q + 1) * QC]
                            .rearrange("(c p) t -> p c t", p=128)),
                [f"xbcd{32 + c}" for c in range(8)], [f"bq2x{b}"])
            dma("sync", lambda e, q=q, b=b:
                e.dma_start(out=cq2x[b].rearrange("p (c t) -> p c t", c=8),
                            in_=xbc_s[I + G * 128:CONV,
                                      q * QC:(q + 1) * QC]
                            .rearrange("(c p) t -> p c t", p=128)),
                [f"xbcd{40 + c}" for c in range(8)], [f"cq2x{b}"])
            dma("sync", lambda e, q=q, b3=q % 3:
                e.dma_start(out=eltq[b3], in_=elt_s.ap()[q]),
                [f"eltd{q // 2}"], [f"eltq{q % 3}"])
            dma("sync", lambda e, q=q, b=b:
                e.dma_start(out=elrx[b], in_=elend_s[q:q + 1, :]),
                ["elend_d"], [f"elrx{b}"])

        def b_front_d(ui):
            a2 = ui % 2
            g2 = (ui // 2) % 2
            aks = [f"PS{2 * a2}", f"PS{2 * a2 + 1}"]
            for sub in range(2):
                hbl = (ui % 2) * 2 + sub
                mm(PF[a2][:, sub * 512:sub * 512 + 512],
                   NRL15[g2][:, hbl * QC:(hbl + 1) * QC],
                   ILx[a2][:, sub * 512:(sub + 1) * 512],
                   [f"NRq2x{g2}", f"Lr32x{a2}"], aks[sub])

        def b_front(ui):
            q, gx = ui // 8, ui % 8
            u = ui % 2
            a2 = ui % 2
            qp = q % 2
            aks = [f"PS{2 * a2}", f"PS{2 * a2 + 1}"]
            if gx == 0:
                if q + 1 < NQ:
                    load_qgrp(q + 1)
                mm(PSA[4][:, 0:H], ones1f, elrx[qp],
                   ["o1f", f"elrx{qp}"], "PS4")
                vec(lambda e, qp=qp:
                    e.tensor_copy(cumAq2[qp], PSA[4][:, 0:H]),
                    ["PS4"], [f"cumAq2_{qp}"])
                for g2 in range(8):
                    half = g2 // 4
                    mm(PF[a2][:, half * 512 + (g2 % 4) * 128:
                       half * 512 + (g2 % 4 + 1) * 128],
                       bq2x[qp][:, g2 * 128:(g2 + 1) * 128],
                       cq2x[qp][:, g2 * 128:(g2 + 1) * 128],
                       [f"bq2x{qp}", f"cq2x{qp}"], aks[half])
                for half in range(2):
                    vec(lambda e, b=half, a_=a2: e.tensor_tensor(
                        out=Gm[:, b * 512:(b + 1) * 512]
                        .rearrange("p (g t) -> p g t", g=4),
                        in0=PF[a_][:, b * 512:(b + 1) * 512]
                        .rearrange("p (g t) -> p g t", g=4),
                        in1=tri_sb[:, None, :].broadcast_to([128, 4, 128]),
                        op=OP.mult), [aks[half], "tri"], ["Gm"])
                for g2 in range(8):
                    tr(PBA[1][:, g2 * 128:(g2 + 1) * 128],
                       bq2x[qp][:, g2 * 128:(g2 + 1) * 128], ident_bf,
                       [f"bq2x{qp}", "idb"], "PBT1")
                scl(lambda e: e.activation(out=btm2, in_=PBA[1],
                                           func=AF.Identity,
                                           scale=E30),
                    ["PBT1"], ["btm2"])
            if ui + 1 < NU:
                load_nrl(ui + 1)
            if ui % 2 == 1 and ui + 1 < NU:
                load_nr2((ui + 1) // 2)
            dma("sync", lambda e, q=q, gx=gx, u=u:
                e.dma_start(out=xtmu[u],
                            in_=xtm_s[q][:, gx * 512:(gx + 1) * 512]),
                [f"xtmd{q}"], [f"xtmu{u}"])

        def b_scal(ui):
            u = ui % 2
            a2 = ui % 2
            aks = [f"PS{2 * a2}", f"PS{2 * a2 + 1}"]
            scl(lambda e, a_=a2: e.activation(
                out=dmy, in_=PF[a_], func=AF.Relu, scale=-1.0,
                bias=b30[:, 0:1]), aks + ["b30"], ["dmyA"])
            scl(lambda e, u=u: e.activation(out=wd[u], in_=dmy,
                                            func=AF.Exp, scale=-1.0),
                ["dmyA"], [f"wdA{u}", f"wdB{u}"])

        def b_cde(ui):
            q, gx = ui // 8, ui % 8
            u = ui % 2
            a2 = ui % 2
            ib = ui % 2
            qp = q % 2
            aks = [f"PS{2 * a2}", f"PS{2 * a2 + 1}"]
            vec(lambda e, u=u: e.tensor_tensor(
                out=xw[u].rearrange("p (h x) -> p h x", h=8),
                in0=xtmu[u].rearrange("p (h x) -> p h x", h=8),
                in1=wd[u][:, QC - 1::QC][:, :, None]
                .broadcast_to([128, 8, 64]), op=OP.mult),
                [f"wdA{u}", f"wdB{u}", f"xtmu{u}"], [f"xw{u}"])
            pool(lambda e, u=u: e.tensor_tensor(
                out=ysd.rearrange("p (h x) -> p h x", h=8),
                in0=xtmu[u].rearrange("p (h x) -> p h x", h=8),
                in1=D_b[:, (ui % 8) * 8:(ui % 8) * 8 + 8][:, :, None]
                .broadcast_to([128, 8, 64]), op=OP.mult),
                [f"xtmu{u}", "D_b"], ["ysd"])
            vec(lambda e, gx=gx, u=u: e.tensor_tensor(
                out=wt[u].rearrange("p (h t) -> p h t", h=8)[:, 0:4, :],
                in0=wd[u].rearrange("p (h t) -> p h t", h=8)[:, 0:4, :],
                in1=Gm[:, gx * 128:(gx + 1) * 128][:, None, :]
                .broadcast_to([128, 4, 128]),
                op=OP.mult),
                [f"wdA{u}", "Gm"], [f"wtA{u}"])
            pool(lambda e, gx=gx, u=u: e.tensor_tensor(
                out=wt[u].rearrange("p (h t) -> p h t", h=8)[:, 4:8, :],
                in0=wd[u].rearrange("p (h t) -> p h t", h=8)[:, 4:8, :],
                in1=Gm[:, gx * 128:(gx + 1) * 128][:, None, :]
                .broadcast_to([128, 4, 128]),
                op=OP.mult),
                [f"wdB{u}", "Gm"], [f"wtB{u}"])
            pool(lambda e, gx=gx, qp=qp: e.tensor_tensor(
                out=sdec.rearrange("p (h x) -> p h x", h=8),
                in0=Sst[:, gx * 512:(gx + 1) * 512]
                .rearrange("p (h x) -> p h x", h=8),
                in1=cumAq2[qp][:, gx * 8:(gx + 1) * 8][:, :, None]
                .broadcast_to([128, 8, 64]), op=OP.mult),
                [f"Sg{gx}", f"cumAq2_{qp}"], ["sdec"])
            for hl8 in range(8):
                h = gx * 8 + hl8
                mm(PF[2][:, ib * 512 + hl8 * 64:
                   ib * 512 + (hl8 + 1) * 64],
                   cq2x[qp][:, gx * 128:(gx + 1) * 128],
                   S_bf[:, h * 64:(h + 1) * 64],
                   [f"cq2x{qp}", f"Sbf{gx}"], f"PS{4 + ib}")
            for hl8 in range(8):
                mm(PF[a2][:, 512 + hl8 * 64:512 + (hl8 + 1) * 64],
                   btm2[:, gx * 128:(gx + 1) * 128],
                   xw[u][:, hl8 * 64:(hl8 + 1) * 64],
                   ["btm2", f"xw{u}"], aks[1])
            for hl8 in range(8):
                wk = f"wtA{u}" if hl8 < 4 else f"wtB{u}"
                mm(PF[a2][:, hl8 * 64:(hl8 + 1) * 64],
                   wt[u][:, hl8 * 128:(hl8 + 1) * 128],
                   xtmu[u][:, hl8 * 64:(hl8 + 1) * 64],
                   [wk, f"xtmu{u}"], aks[0])

        def b_asm(ui):
            q, gx = ui // 8, ui % 8
            u = ui % 2
            a2 = ui % 2
            ib = ui % 2
            qp = q % 2
            aks = [f"PS{2 * a2}", f"PS{2 * a2 + 1}"]
            dma("sync", lambda e, q=q, gx=gx, u=u:
                e.dma_start(out=ggt[u],
                            in_=gate_s[q * QC:(q + 1) * QC,
                                       gx * 512:(gx + 1) * 512]),
                [f"gated{2 * gx}_{q}", f"gated{2 * gx + 1}_{q}"],
                [f"ggt{u}"])
            q3 = q % 3
            vec(lambda e, gx=gx, u=u, ib=ib, q3=q3: e.tensor_tensor(
                out=ysb[u].rearrange("p (h x) -> p h x", h=8),
                in0=PF[2][:, ib * 512:ib * 512 + 512]
                .rearrange("p (h x) -> p h x", h=8),
                in1=eltq[q3][:, gx * 8:(gx + 1) * 8][:, :, None]
                .broadcast_to([128, 8, 64]), op=OP.mult),
                [f"PS{4 + ib}", f"eltq{q3}"], [f"ysb{u}"])
            vec(lambda e, u=u, a_=a2: e.tensor_tensor(
                out=ysb[u], in0=ysb[u], in1=PF[a_][:, 0:512],
                op=OP.add), [f"ysb{u}", aks[0]], [f"ysb{u}"])
            pool(lambda e, u=u: e.tensor_tensor(
                out=ysb[u], in0=ysb[u], in1=ysd,
                op=OP.add), [f"ysb{u}", "ysd"], [f"ysb{u}"])
            vec(lambda e, gx=gx, a_=a2: e.tensor_tensor(
                out=Sst[:, gx * 512:(gx + 1) * 512],
                in0=sdec, in1=PF[a_][:, 512:1024],
                op=OP.add),
                ["sdec", aks[1]], [f"Sg{gx}"])

        def b_fin(ui):
            q, gx = ui // 8, ui % 8
            u = ui % 2
            scl(lambda e, u=u: e.activation(out=sqscr, in_=ysb[u],
                                            func=AF.Square,
                                            accum_out=ssq[u]),
                [f"ysb{u}"], ["sqx", f"ssq{u}"])
            vec(lambda e, u=u: e.tensor_scalar(
                out=rstd[u], in0=ssq[u], scalar1=1.0 / 512.0,
                scalar2=EPS, op0=OP.mult, op1=OP.add),
                [f"ssq{u}"], [f"rstd{u}"])
            scl(lambda e, u=u: e.activation(out=rstd[u], in_=rstd[u],
                                            func=AF.Ln),
                [f"rstd{u}"], [f"rstd{u}"])
            scl(lambda e, u=u: e.activation(out=rstd[u], in_=rstd[u],
                                            func=AF.Exp, scale=-0.5),
                [f"rstd{u}"], [f"rstd{u}"])
            vec(lambda e, u=u: e.scalar_tensor_tensor(
                out=yng, in0=ysb[u], scalar=rstd[u][:, 0:1],
                in1=ggt[u], op0=OP.mult, op1=OP.mult),
                [f"ysb{u}", f"rstd{u}", f"ggt{u}"], ["yng"])
            dma("gpsimd", lambda e, gx=gx: e.dma_start(
                out=S_bf[:, gx * 512:(gx + 1) * 512],
                in_=Sst[:, gx * 512:(gx + 1) * 512]),
                [f"Sg{gx}"], [f"Sbf{gx}"])
            for c4 in range(4):
                tr(PBA[1][:, 512 + c4 * 128:512 + (c4 + 1) * 128],
                   yng[:, c4 * 128:(c4 + 1) * 128],
                   ident_bf, ["yng", "idb"], "PBT1")
            vec(lambda e, gx=gx, q=q: e.tensor_copy(
                panel.rearrange("p (c t) -> p c t", c=NKC)
                [:, gx * 4:gx * 4 + 4, q * QC:(q + 1) * QC],
                PBA[1][:, 512:1024]
                .rearrange("p (c t) -> p c t", c=4)),
                ["PBT1"],
                [f"pan{gx * 4 + i}" for i in range(4)])

        for b in range(2):
            vec(lambda e, b=b: e.memset(NRL15[b], 1.0), [],
                [f"NRq2x{b}"])
            for hh in range(2):
                dma("gpsimd", lambda e, b=b, hh=hh:
                    e.dma_start(out=ILx[b][0:12, hh * 512:(hh + 1) * 512],
                                in_=ind12),
                    [f"Lr32x{b}"], [f"Lr32x{b}"])
        load_qgrp(0)
        load_nrl(0)
        load_nr2(0)
        overlay(bq2x[0], range(32, 40), "bq2x0")
        overlay(cq2x[0], range(40, 48), "cq2x0")

        for k in range(NU + 3):
            if 2 <= k <= NU + 1:
                b_asm(k - 2)
            if k < NU:
                b_front_d(k)
                b_scal(k)
            if 1 <= k <= NU:
                b_cde(k - 1)
            if k < NU:
                b_front(k)
            if 3 <= k <= NU + 2:
                b_fin(k - 3)
        A.pop_fence()

        if dbg:
            dbg1 = nc.dram_tensor("dbg1", [128, NKC * T], BF16).ap()
            for kc in range(NKC):
                dma("sync", lambda e, k=kc: e.dma_start(
                    out=dbg1[:, k * T:(k + 1) * T],
                    in_=panel[:, k * T:(k + 1) * T]),
                    [f"pan{kc}"], [f"dbg1_{kc}"])

        A.push()
        wsl3 = [A.bf16(128, NKC * 128, f"wsl3{i}") for i in range(2)]
        xr3 = [A.f32(128, T, f"xr3{i}") for i in range(2)]
        x2b = [A.f32(128, T, f"x2b{i}") for i in range(2)]
        gi3 = 0
        for oc in range(NKC):
            wb3 = oc % 2
            dma("gpsimd", lambda e, o=oc, w=wb3:
                e.dma_start(out=wsl3[w],
                            in_=w_out[o].rearrange("p kc m -> p (kc m)")),
                [], [f"wsl3{wb3}"])
            dma("scalq", lambda e, o=oc, w=wb3:
                e.dma_start(out=xr3[w],
                            in_=x_fm[o * 128:(o + 1) * 128, :]),
                [], [f"xr3{wb3}"])
            for it, (t0, tw) in enumerate(TT):
                bank = gi3 % 4
                for kc in range(NKC):
                    mm(PSA[bank][:, 0:tw],
                       wsl3[wb3][:, kc * 128:(kc + 1) * 128],
                       panel[:, kc * T + t0:kc * T + t0 + tw],
                       [f"wsl3{wb3}", f"pan{kc}"], f"PS{bank}",
                       start=(kc == 0), stop=(kc == NKC - 1))
                vec(lambda e, w=wb3, t0=t0, tw=tw, b=bank:
                    e.tensor_tensor(out=x2b[w][:, t0:t0 + tw],
                                    in0=PSA[b][:, 0:tw],
                                    in1=xr3[w][:, t0:t0 + tw], op=OP.add),
                    [f"PS{bank}", f"xr3{wb3}"], [f"x2b{wb3}"])
                gi3 += 1
            dma("sync", lambda e, o=oc, w=wb3:
                e.dma_start(out=x2_s[o * 128:(o + 1) * 128, :],
                            in_=x2b[w]),
                [f"x2b{wb3}"], [f"x2d{oc}"])
        A.pop_fence()

        phase_ln("x2", "L2", router=True)

        A.push()
        wslB = [A.bf16(128, NKC * 128, f"wslB{i}") for i in range(2)]
        blora = A.bf16(128, I, "blorak")
        wrt_sb = A.f32(128, NKC * 8, "wrtk")
        qred = A.f32(128, 64, "qredk")
        tbf = A.bf16(128, T, "tbf")
        xrB = A.f32(128, T, "xrB")
        obuf = A.f32(128, T, "obuf")
        ot1 = A.f32(128, 512, "ot1")
        lgt = A.f32(2, 8, ("lgt", "lgto"))
        mx2 = A.f32(2, 1, "mx2")
        rt8 = A.f32(8, 1, "rt8")
        rteff = A.f32(128, 1, "rteffk")

        dma("gpsimd", lambda e:
            e.dma_start(out=wslB[0],
                        in_=w_la.rearrange("p kc m -> p (kc m)")),
            [], ["wslB0"])
        for it, (t0, tw) in enumerate(TT):
            bank = it % 2
            for kc in range(NKC):
                mm(PSA[bank][:, 0:tw],
                   wslB[0][:, kc * 128:(kc + 1) * 128],
                   panel[:, kc * T + t0:kc * T + t0 + tw],
                   ["wslB0", f"pan{kc}"], f"PS{bank}",
                   start=(kc == 0), stop=(kc == NKC - 1))
            vec(lambda e, t0=t0, tw=tw, b=bank: e.tensor_copy(
                tbf[:, t0:t0 + tw], PSA[b][:, 0:tw]),
                [f"PS{bank}"], ["tbf"])
        dma("gpsimd", lambda e: e.dma_start(out=blora, in_=w_lb),
            [], ["blorak"])
        dma("sync", lambda e: e.dma_start(out=wrt_sb, in_=w_rt),
            [], ["wrtk"])
        dma("sync", lambda e: e.dma_start(out=qred, in_=q_out.ap()),
            ["q_out"], ["qredk"])
        for kc in range(NKC):
            mm(PSA[2][0:2, 0:8],
               qred[:, kc:64:32], wrt_sb[:, kc * 8:(kc + 1) * 8],
               ["qredk", "wrtk"], "PS2",
               start=(kc == 0), stop=False)
        mm(PSA[2][0:2, 0:8], ones1f[:, 0:2], rb_sb,
           ["o1f", "rbk"], "PS2", start=False, stop=True)
        vec(lambda e: e.tensor_copy(lgt, PSA[2][0:2, 0:8]), ["PS2"], ["lgt"])
        vec(lambda e: e.reduce_max(mx2, lgt, axis=AX.X), ["lgt"], ["mx2"])
        vec(lambda e: e.tensor_scalar(out=lgt, in0=lgt,
                                      scalar1=mx2[:, 0:1], scalar2=None,
                                      op0=OP.subtract), ["lgt", "mx2"], ["lgt"])
        scl(lambda e: e.activation(out=lgt, in_=lgt, func=AF.Exp),
            ["lgt"], ["lgt"])
        vec(lambda e: e.reduce_sum(mx2, lgt, axis=AX.X), ["lgt"], ["mx2"])
        vec(lambda e: e.reciprocal(mx2, mx2), ["mx2"], ["mx2"])
        vec(lambda e: e.tensor_scalar(out=lgt, in0=lgt,
                                      scalar1=mx2[:, 0:1], scalar2=None,
                                      op0=OP.mult), ["lgt", "mx2"], ["lgt"])
        mm(PSA[3][0:1, 0:8], bm_sb, lgt, ["bm", "lgt"], "PS3")
        vec(lambda e: e.tensor_copy(lgt[0:1, 0:8], PSA[3][0:1, 0:8]),
            ["PS3"], ["lgto"])
        tr(PSA[2][0:8, 0:1], lgt[0:1, 0:8], ident_f[0:1, 0:1],
           ["lgto", "idf"], "PS2")
        vec(lambda e: e.tensor_copy(rt8, PSA[2][0:8, 0:1]), ["PS2"], ["rt8"])
        mm(PSA[3][:, 0:1], e816_sb, rt8, ["e816", "rt8"], "PS3")
        vec(lambda e: e.tensor_copy(rteff, PSA[3][:, 0:1]), ["PS3"], ["rteffk"])

        blsc = [A.bf16(128, 128, f"blsc{i}") for i in range(2)]
        giB = 0
        for oc in range(NKC):
            wbB = oc % 2
            vec(lambda e, o=oc, w=wbB: e.tensor_scalar(
                out=blsc[w], in0=blora[:, o * 128:(o + 1) * 128],
                scalar1=rteff[:, 0:1], scalar2=None, op0=OP.mult),
                ["blorak", "rteffk"], [f"blsc{wbB}"])
            dma("gpsimd", lambda e, o=oc, w=wbB:
                e.dma_start(out=wslB[w],
                            in_=w_ffn[o].rearrange("p kc m -> p (kc m)")),
                [], [f"wslB{wbB}"])
            dma("scalq", lambda e, o=oc:
                e.dma_start(out=xrB, in_=x2_s[o * 128:(o + 1) * 128, :]),
                [f"x2d{oc}"], ["xrB"])
            for it, (t0, tw) in enumerate(TT):
                bank = giB % 4
                for kc in range(NKC):
                    mm(PSA[bank][:, 0:tw],
                       wslB[wbB][:, kc * 128:(kc + 1) * 128],
                       panel[:, kc * T + t0:kc * T + t0 + tw],
                       [f"wslB{wbB}", f"pan{kc}"], f"PS{bank}",
                       start=(kc == 0), stop=False)
                mm(PSA[bank][:, 0:tw],
                   blsc[wbB],
                   tbf[:, t0:t0 + tw],
                   [f"blsc{wbB}", "tbf"], f"PS{bank}", start=False, stop=True)
                scl(lambda e, t0=t0, tw=tw, b=bank, o=oc:
                    e.activation(out=ot1[:, 0:tw], in_=PSA[b][:, 0:tw],
                                 func=AF.Identity, bias=fb_sb[:, o:o + 1]),
                    [f"PS{bank}", "fb"], ["ot1"])
                vec(lambda e, t0=t0, tw=tw:
                    e.tensor_tensor(out=obuf[:, t0:t0 + tw],
                                    in0=ot1[:, 0:tw],
                                    in1=xrB[:, t0:t0 + tw], op=OP.add),
                    ["ot1", "xrB"], ["obuf"])
                giB += 1
            dma("sync", lambda e, o=oc:
                e.dma_start(out=out_fm[o * 128:(o + 1) * 128, :],
                            in_=obuf),
                ["obuf"], [f"outd{oc}"])
        A.pop_fence()

        S_.add("gpsimd", lambda e: e.memset(pxbc[0:1, 0:1], 0.0),
               reads=[f"outd{o}" for o in range(NKC)], writes=[],
               sem="POO", amt=1)

        nops, cnts = S_.finalize(block, sems)
    return nc, nops


def _prep_weights(inputs, T):
    ip = np.asarray(inputs["L0_attn_in_proj_w"])
    W_inT = np.ascontiguousarray(ip.T.astype(np.float32))
    slabs = np.zeros((49, 128, NKC, 128), np.float32)
    dtslab = np.zeros((D, 128), np.float32)
    dtslab[:, 0:H] = W_inT[:, I + CONV:I + CONV + H]
    slabs[0] = dtslab.reshape(NKC, 128, 128).transpose(1, 0, 2)
    for c in range(NHBC):
        blk = W_inT[:, I + c * 128:I + (c + 1) * 128]
        slabs[c + 1] = blk.reshape(NKC, 128, 128).transpose(1, 0, 2)
    w_hbcdt = np.ascontiguousarray(slabs)

    w_gate = np.ascontiguousarray(W_inT[:, 0:I].reshape(NKC, 128, I))

    op_ = np.asarray(inputs["L0_attn_out_proj_w"])
    nwv = np.asarray(inputs["L0_attn_norm_weight"]).astype(np.float32)
    OT = np.ascontiguousarray(op_.T.astype(np.float32)
                              * nwv[:, None])
    w_out = np.zeros((NKC, 128, NKC, 128), np.float32)
    for o in range(NKC):
        blk = OT[:, o * 128:(o + 1) * 128]
        w_out[o] = blk.reshape(NKC, 128, 128).transpose(1, 0, 2)

    Wf = np.asarray(inputs["L0_ffn_W"])
    FT = np.ascontiguousarray(Wf.T.astype(np.float32))
    w_ffn = np.zeros((NKC, 128, NKC, 128), np.float32)
    for o in range(NKC):
        blk = FT[:, o * 128:(o + 1) * 128]
        w_ffn[o] = blk.reshape(NKC, 128, 128).transpose(1, 0, 2)

    Am = np.asarray(inputs["L0_ffn_A"]).reshape(128, D)
    AT = Am.T
    w_la = np.ascontiguousarray(
        AT.reshape(NKC, 128, 128).transpose(1, 0, 2))
    Bm = np.asarray(inputs["L0_ffn_B"])
    w_lb = np.ascontiguousarray(
        Bm.transpose(0, 2, 1).reshape(128, I))
    rw = np.asarray(inputs["L0_ffn_router_W"])
    w_rt = np.ascontiguousarray(
        rw.T.reshape(NKC, 128, 8).transpose(1, 0, 2).reshape(128, NKC * 8))

    cw = np.asarray(inputs["L0_attn_conv_weight"])
    diag = np.zeros((NHBC, KCV, 128, 128), np.float32)
    idx = np.arange(128)
    for c in range(NHBC):
        for k in range(KCV):
            diag[c, k, idx, idx] = cw[c * 128:(c + 1) * 128, k]
    cwc = np.ascontiguousarray(
        cw.reshape(NHBC, 128, KCV).transpose(1, 0, 2).reshape(128,
                                                              NHBC * KCV))
    cb = np.asarray(inputs["L0_attn_conv_bias"])
    cbias = np.ascontiguousarray(cb.reshape(NHBC, 128).T)

    nw = np.asarray(inputs["L0_attn_norm_weight"])
    nwc = np.ascontiguousarray(nw.reshape(NKC, 128).T)
    fbv = np.asarray(inputs["L0_ffn_b"])
    fbc = np.ascontiguousarray(fbv.reshape(NKC, 128).T)

    return dict(
        w_hbcdt=w_hbcdt, w_gate=w_gate, w_out=w_out, w_ffn=w_ffn,
        w_la=w_la, w_lb=w_lb, w_rt=w_rt, diagw=diag, cwcols=cwc,
        cbias=cbias, nw_cols=nwc, fb_cols=fbc,
        dtb=np.asarray(inputs["L0_attn_dt_bias"]).reshape(H, 1),
        alog=np.asarray(inputs["L0_attn_A_log"]).reshape(H, 1),
        dvec_row=np.asarray(inputs["L0_attn_D"]).reshape(1, H),
        rb_row=np.asarray(inputs["L0_ffn_router_b"]).reshape(1, 8),
    )


def _consts(T, qcount):
    identity = np.eye(128, dtype=np.float32)
    tri = np.triu(np.full((128, 128), np.exp(30.0), np.float32))
    ind12 = np.zeros((12, 512), np.float32)
    for blk in range(3):
        for k in range(4):
            ind12[blk * 4 + k, k * 128:(k + 1) * 128] = 1.0
    e816m = np.zeros((8, 128), np.float32)
    for e in range(8):
        e816m[e, e * 16:(e + 1) * 16] = SCALING
    return identity, tri, ind12, e816m


def kernel(**inputs):
    x = np.asarray(inputs["L0_attn_hidden_states"])
    Bb, L, _ = x.shape
    T = (Bb * L) // NCORES
    QS, QE = 611, L - 1
    qcount = QE - QS

    shared = _prep_weights(inputs, T)
    identity, tri, ind12, e816m = _consts(T, qcount)
    shared.update(identity=identity, tri01=tri, ind12=ind12, e816=e816m)

    nc, nops = build_program(T)

    in_maps = []
    for c in range(NCORES):
        b = 0 if c < NCORES // 2 else 1
        ci = c % (NCORES // 2)
        t0, t1 = ci * T, (ci + 1) * T
        m = dict(shared)
        m["x_fm"] = np.ascontiguousarray(x[b, t0:t1, :].T)
        tm = np.zeros((1, T), np.float32)
        lo, hi = max(QS, t0), min(QE, t1)
        if hi > lo:
            tm[0, lo - t0:hi - t0] = 1.0 / qcount
        m["tokmask"] = tm
        chmv = np.zeros((H, 64), np.float32)
        chm1v = np.zeros((H, 64), np.float32)
        lo_r, hi_r = (0, 4) if b == 0 else (4, 8)
        for r in range(8):
            contributes = (lo_r <= r < hi_r) and (r < c)
            if contributes:
                for i in range(8):
                    mid = (r < i < c)
                    chmv[:, r * 8 + i] = 1.0 if mid else 0.0
                    chm1v[:, r * 8 + i] = 0.0 if mid else 1.0
        m["chmh"] = chmv
        m["chm1mh"] = chm1v
        bc = np.zeros((4, 8), np.float32)
        if ci > 0:
            bc[:, c - 1] = 1.0
        m["bndc"] = bc
        bmv = np.zeros((2, 1), np.float32)
        bmv[b, 0] = 1.0
        m["bmask"] = bmv
        bs = np.zeros((128, 64), np.float32)
        bs[:, b * 32:(b + 1) * 32] = 1.0
        m["bsel"] = bs
        in_maps.append(m)

    res = run_bass_kernel_spmd(nc, in_maps, core_ids=list(range(NCORES)))
    out = np.empty((Bb, L, D), np.float32)
    for c in range(NCORES):
        b = 0 if c < NCORES // 2 else 1
        ci = c % (NCORES // 2)
        out[b, ci * T:(ci + 1) * T, :] = res.results[c]["out_fm"].T
    return out


# version 17
# speedup vs baseline: 1.0388x; 1.0232x over previous
import numpy as np
import ml_dtypes
import concourse.bass as bass
import concourse.mybir as mybir
from concourse.bass_utils import run_bass_kernel_spmd

F32 = mybir.dt.float32
BF16 = mybir.dt.bfloat16
AF = mybir.ActivationFunctionType
OP = mybir.AluOpType
AX = mybir.AxisListType
BF = ml_dtypes.bfloat16

H, P, N, G, KCV = 64, 64, 128, 8, 4
D = 4096
I = 4096
CONV = 6144
QC = 128
EPS = 1e-5
NKC = 32
NHBC = 48
NCORES = 8
NHB = 16
SCALING = 2.0

STREAM_OF = {"sync": "sync", "scalar": "scalar", "vector": "vector",
             "tensor": "tensor", "gpsimd": "gpsimd", "scalq": "scalar"}
ENG_SEM = {"sync": "LDS", "scalq": "LDA", "tensor": "TEN", "vector": "VEC",
           "scalar": "SCL", "gpsimd": "POO"}
DMA_SEMS = ("LDS", "LDA", "LDG")


class Sched:
    def __init__(self):
        self.ops = []
        self.last_write = {}
        self.readers = {}

    def add(self, eng, fn, reads=(), writes=(), sem=None, amt=None):
        idx = len(self.ops)
        deps = set()
        for r in reads:
            w = self.last_write.get(r)
            if w is not None:
                deps.add(w)
        for wres in writes:
            w = self.last_write.get(wres)
            if w is not None:
                deps.add(w)
            for rd in self.readers.get(wres, ()):
                deps.add(rd)
        if sem is None:
            sem = ENG_SEM[eng]
        if amt is None:
            amt = 16 if sem in DMA_SEMS else 1
        self.ops.append({"i": idx, "eng": eng, "fn": fn, "deps": deps,
                         "sem": sem, "amt": amt, "waited": False})
        for wres in writes:
            self.last_write[wres] = idx
            self.readers[wres] = []
        for r in reads:
            self.readers.setdefault(r, []).append(idx)
        return idx

    def finalize(self, block, sems):
        ops = self.ops
        NL = 8
        lane_cnt = {}
        lane_prev = {}
        for op in ops:
            if op["sem"] in DMA_SEMS:
                q = op["sem"]
                i = lane_cnt.get(q, 0)
                lane_cnt[q] = i + 1
                lane = f"{q}{i % NL}"
                op["sem"] = lane
                prev = lane_prev.get(lane)
                if prev is not None:
                    op["lane_dep"] = prev
                lane_prev[lane] = op["i"]
        for op in ops:
            pruned = set()
            for d in op["deps"]:
                dop = ops[d]
                if dop["eng"] == "tensor" and op["eng"] == "tensor":
                    continue
                pruned.add(d)
            op["deps"] = pruned
            if "lane_dep" in op:
                op["deps"].add(op["lane_dep"])
        for op in ops:
            for d in op["deps"]:
                ops[d]["waited"] = True
            if op["sem"][:3] in DMA_SEMS or op["sem"].startswith("CC"):
                op["waited"] = True
        cnt = {}
        for op in ops:
            if op["waited"]:
                cnt[op["sem"]] = cnt.get(op["sem"], 0) + op["amt"]
                op["val"] = cnt[op["sem"]]
            else:
                op["val"] = None
        streams = {e: [] for e in ["sync", "scalar", "vector", "tensor",
                                   "gpsimd"]}
        for op in ops:
            streams[STREAM_OF[op["eng"]]].append(op)
        for sname, sops in streams.items():
            hw = {}
            for op in sops:
                byv = {}
                for d in op["deps"]:
                    dop = ops[d]
                    if dop["val"] is None:
                        continue
                    s, v = dop["sem"], dop["val"]
                    if hw.get(s, 0) >= v:
                        continue
                    byv[s] = max(byv.get(s, 0), v)
                op["waits"] = sorted(byv.items())
                for s, v in byv.items():
                    hw[s] = v

        def make(sops):
            def body(e):
                for op in sops:
                    for s, v in op["waits"]:
                        e.wait_ge(sems[s], v)
                    inst = op["fn"](e)
                    if op["waited"]:
                        inst.then_inc(sems[op["sem"]], op["amt"])
            return body
        for sname in ["sync", "scalar", "vector", "tensor", "gpsimd"]:
            getattr(block, sname)(make(streams[sname]))
        return len(ops), cnt


class Arena:

    def __init__(self, sched, tensor_ap, width_f32):
        self.S = sched
        self.t = tensor_ap
        self.tb = tensor_ap.bitcast(BF16)
        self.w = width_f32
        self.off = 0
        self.marks = []
        self.used = set()
        self.fence_idx = None

    def push(self):
        self.marks.append(self.off)

    def f32(self, parts, cols, key):
        assert self.off + cols <= self.w, f"arena overflow {self.off}+{cols}"
        v = self.t[0:parts, self.off:self.off + cols]
        self.off += cols
        self._reg(key)
        return v

    def bf16(self, parts, cols, key):
        colsf = (cols + 1) // 2
        assert self.off + colsf <= self.w, f"arena overflow {self.off}+{colsf}"
        v = self.tb[0:parts, 2 * self.off:2 * self.off + cols]
        self.off += colsf
        self._reg(key)
        return v

    def pairf32(self, cols, key_top, key_bot):
        assert self.off + cols <= self.w, "arena overflow (pair)"
        top = self.t[0:64, self.off:self.off + cols]
        bot = self.t[64:128, self.off:self.off + cols]
        self.off += cols
        self._reg(key_top)
        self._reg(key_bot)
        return top, bot

    def _reg(self, key):
        keys = key if isinstance(key, (tuple, list)) else (key,)
        for k in keys:
            self.used.add(k)
            if self.fence_idx is not None and k not in self.S.last_write:
                self.S.last_write[k] = self.fence_idx

    def pop_fence(self):
        self.off = self.marks.pop()
        used = sorted(self.used)
        idx = self.S.add("vector", lambda e: e.engine_nop(), reads=(),
                         writes=used)
        self.used = set()
        self.fence_idx = idx


def _ttiles(T):
    out, t = [], 0
    while t < T:
        w = min(512, T - t)
        out.append((t, w))
        t += w
    return out


def build_program(T, dbg=False):
    NQ = T // QC
    TT = _ttiles(T)
    T16 = T // 128

    nc = bass.Bass("TRN2", target_bir_lowering=False)

    def inp(name, shape, dt=F32):
        return nc.dram_tensor(name, shape, dt, kind="ExternalInput").ap()

    x_fm = inp("x_fm", [D, T])
    w_hbcdt = inp("w_hbcdt", [49, 128, NKC, 128])
    w_gate = inp("w_gate", [NKC, 128, I])
    w_out = inp("w_out", [NKC, 128, NKC, 128])
    w_ffn = inp("w_ffn", [NKC, 128, NKC, 128])
    w_la = inp("w_la", [128, NKC, 128])
    w_lb = inp("w_lb", [128, I])
    w_rt = inp("w_rt", [128, NKC * 8])
    diagw = inp("diagw", [NHBC, KCV, 128, 128])
    cwcols = inp("cwcols", [128, NHBC * KCV])
    cbias = inp("cbias", [128, NHBC])
    dtb = inp("dtb", [H, 1])
    alog = inp("alog", [H, 1])
    dvec_row = inp("dvec_row", [1, H])
    nw_cols = inp("nw_cols", [128, NKC])
    fb_cols = inp("fb_cols", [128, NKC])
    rb_row = inp("rb_row", [1, 8])
    identity = inp("identity", [128, 128])
    tri01 = inp("tri01", [128, 128])
    ind12 = inp("ind12", [12, 512])
    e816 = inp("e816", [8, 128])
    tokmask = inp("tokmask", [1, T])
    chmh = inp("chmh", [H, 64])
    chm1mh = inp("chm1mh", [H, 64])
    bndc = inp("bndc", [4, 8])
    bmask = inp("bmask", [2, 1])
    bsel_i = inp("bsel", [128, 64])

    out_fm = nc.dram_tensor("out_fm", [D, T], F32, kind="ExternalOutput").ap()

    xbc_s = nc.dram_tensor("xbc_s", [CONV, T], BF16).ap()
    gate_s = nc.dram_tensor("gate_s", [T, I], BF16).ap()
    xtm_s = nc.dram_tensor("xtm_s", [NQ, 128, I], BF16).ap()
    x2_s = nc.dram_tensor("x2_s", [D, T], F32).ap()
    lsp3 = nc.dram_tensor("lsp3_s", [3, H, T], BF16).ap()
    nrsp3 = nc.dram_tensor("nrsp3_s", [3, H, T], BF16).ap()
    lsp_s = [lsp3[i] for i in range(3)]
    nrsp_s = [nrsp3[i] for i in range(3)]
    elend_s = nc.dram_tensor("elend_s", [NQ, H], F32).ap()
    elt_s = nc.dram_tensor("elt_s", [NQ, 128, 64], BF16)
    coefb_s = nc.dram_tensor("coefb_s", [8, H], F32).ap()
    bnd_in = nc.dram_tensor("bnd_in", [4, CONV], BF16)
    bnd_out = nc.dram_tensor("bnd_out", [32, CONV], BF16)
    st_in = nc.dram_tensor("st_in", [128, H * P], BF16)
    st_out = nc.dram_tensor("st_out", [8 * 128, H * P], BF16)
    ca_in = nc.dram_tensor("ca_in", [H, 1], F32)
    ca_out = nc.dram_tensor("ca_out", [8 * H, 1], F32)
    q_in = nc.dram_tensor("q_in", [128, 64], F32)
    q_out = nc.dram_tensor("q_out", [128, 64], F32)

    from contextlib import ExitStack
    with ExitStack() as g:
        def sbt(name, shape, dt=F32):
            return g.enter_context(nc.sbuf_tensor(name, shape, dt))[:]

        panel = sbt("panel", [128, NKC * T], BF16)
        AR_W = 12544
        S_ = Sched()
        A = Arena(S_, sbt("arena", [128, AR_W], F32), AR_W)

        ident_bf = sbt("ident_bf", [128, 128], BF16)
        ident_f = sbt("ident_f", [128, 128], F32)
        tri_sb = sbt("tri_sb", [128, 128], F32)
        ind12_sb = sbt("ind12_sb", [12, 512], BF16)
        ones1f = sbt("ones1f", [1, 128], F32)
        ones1b = sbt("ones1b", [1, 128], BF16)
        ones3b = sbt("ones3b", [3, 128], BF16)
        onescol = sbt("onescol", [128, 1], F32)
        onescol_b = sbt("onescol_b", [128, 1], BF16)
        ones64x = sbt("ones64x", [H, QC], F32)
        cb_sb = sbt("cb_sb", [128, NHBC], F32)
        cw_sb = sbt("cw_sb", [128, NHBC * KCV], F32)
        dtb_sb = sbt("dtb_sb", [H, 1], F32)
        negA = sbt("negA", [H, 1], F32)
        nw_sb = sbt("nw_sb", [128, NKC], F32)
        fb_sb = sbt("fb_sb", [128, NKC], F32)
        dvrow_sb = sbt("dvrow_sb", [1, H], F32)
        rb_sb = sbt("rb_sb", [1, 8], F32)
        e816_sb = sbt("e816_sb", [8, 128], F32)
        bm_sb = sbt("bm_sb", [2, 1], F32)
        Sst = sbt("Sst", [128, H * P], F32)
        S_bf = sbt("S_bf", [128, H * P], BF16)
        D_b = sbt("D_b", [128, H], F32)
        pf_sb = sbt("pf_sb", [128, NHBC * 3], BF16)
        pxbc = sbt("pxbc", [128, NHBC * 3], BF16)
        qacc = sbt("qacc", [128, NKC], F32)

        PF = [g.enter_context(nc.psum_tensor(f"PF{i}", [128, 1024], F32))[:]
              for i in range(3)]
        PSA = [PF[i // 2][:, (i % 2) * 512:(i % 2) * 512 + 512]
               for i in range(6)]
        PBA = [g.enter_context(nc.psum_tensor(f"PBT{i}", [128, 1024],
                                              BF16))[:]
               for i in range(2)]

        sems = {}
        semnames = ["TEN", "VEC", "SCL", "POO", "CC1", "CC2a", "CC2b", "CC3"]
        for q in ["LDS", "LDA", "LDG"]:
            semnames += [f"{q}{i}" for i in range(8)]
        for s in semnames:
            sems[s] = g.enter_context(nc.semaphore(s))

        block = g.enter_context(nc.Block())

        def mm(out_ap, lhsT, rhs, reads, wkey, start=True, stop=True):
            S_.add("tensor",
                   lambda e, o=out_ap, l=lhsT, r=rhs, st=start, sp=stop:
                   e.matmul(o, l, r, start=st, stop=sp),
                   reads=reads, writes=[wkey])

        def tr(out_ap, in_ap, ident, reads, wkey):
            S_.add("tensor",
                   lambda e, o=out_ap, i_=in_ap, d=ident:
                   e.transpose(o, i_, d),
                   reads=reads, writes=[wkey])

        def vec(fn, reads, writes):
            S_.add("vector", fn, reads=reads, writes=writes)

        def pool(fn, reads, writes):
            S_.add("gpsimd", fn, reads=reads, writes=writes, sem="POO",
                   amt=1)

        def scl(fn, reads, writes):
            S_.add("scalar", fn, reads=reads, writes=writes)

        def dma(q, fn, reads, writes):
            sm = {"sync": "LDS", "scalq": "LDA", "gpsimd": "LDG"}[q]
            S_.add(q, fn, reads=reads, writes=writes, sem=sm, amt=16)

        def coll(name_sem, kind, op_, tin, tout, reads, writes):
            S_.add("gpsimd", lambda e, k=kind, o=op_, i_=tin, u=tout:
                   e.collective_compute(
                       k, o, replica_groups=[list(range(NCORES))],
                       ins=[i_.ap().opt()], outs=[u.ap().opt()]),
                   reads=reads, writes=writes, sem=name_sem, amt=1)

        ALLPAN = [f"pan{k}" for k in range(NKC)]
        ALLSG = [f"Sg{gg}" for gg in range(8)]

        for dst, src, rk in [(tri_sb, tri01, "tri"), (ident_f, identity, "idf"),
                             (cb_sb, cbias, "cb"), (cw_sb, cwcols, "cw"),
                             (dtb_sb, dtb, "dtbk"), (nw_sb, nw_cols, "nw"),
                             (fb_sb, fb_cols, "fb"),
                             (dvrow_sb, dvec_row, "dvr"),
                             (rb_sb, rb_row, "rbk"), (e816_sb, e816, "e816"),
                             (bm_sb, bmask, "bm"), (negA, alog, "negA")]:
            dma("sync", lambda e, d=dst, s=src: e.dma_start(out=d, in_=s),
                [], [rk])
        for dst, src, rk in [(ident_bf, identity, "idb"),
                             (ind12_sb, ind12, "ind12")]:
            dma("gpsimd", lambda e, d=dst, s=src: e.dma_start(out=d, in_=s),
                [], [rk])
        for tile, rk in [(ones1f, "o1f"), (ones1b, "o1b"), (ones3b, "o3b"),
                         (onescol, "oc"), (ones64x, "o64"),
                         (onescol_b, "ocb")]:
            vec(lambda e, t=tile: e.memset(t, 1.0), [], [rk])
        vec(lambda e: e.memset(Sst, 0.0), [], ALLSG)
        scl(lambda e: e.activation(out=negA, in_=negA, func=AF.Exp),
            ["negA"], ["negA"])
        vec(lambda e: e.tensor_scalar(out=negA, in0=negA, scalar1=-1.0,
                                      scalar2=None, op0=OP.mult),
            ["negA"], ["negA"])
        mm(PSA[4][:, 0:H], ones1f, dvrow_sb, ["o1f", "dvr"], "PS4")
        vec(lambda e: e.tensor_copy(D_b, PSA[4][:, 0:H]), ["PS4"], ["D_b"])

        def phase_ln(src, ph, router=False):
            A.push()
            a_b = A.bf16(128, T, f"ab{ph}")
            b_b = A.bf16(128, T, f"bb{ph}")
            xsq = [A.bf16(128, 512, f"xsq{i}{ph}") for i in range(2)]
            tmp_ln = [A.bf16(128, T, f"tmpln{i}{ph}") for i in range(2)]
            arow = A.f32(1, T, f"arow{ph}")
            brow = A.f32(1, T, f"brow{ph}")
            moff = A.off
            mrow = A.f32(1, T, f"mrow{ph}")
            for kc in range(NKC):
                if src == "x":
                    dma("gpsimd", lambda e, k=kc:
                        e.dma_start(out=panel[:, k * T:(k + 1) * T],
                                    in_=x_fm[k * 128:(k + 1) * 128, :]),
                        [], [f"pan{kc}"])
                else:
                    dma("gpsimd", lambda e, k=kc:
                        e.dma_start(out=panel[:, k * T:(k + 1) * T],
                                    in_=x2_s[k * 128:(k + 1) * 128, :]),
                        [f"x2d{kc}"], [f"pan{kc}"])
            for it, (t0, tw) in enumerate(TT):
                for kc in range(NKC):
                    ps = panel[:, kc * T + t0:kc * T + t0 + tw]
                    xb = kc % 2
                    scl(lambda e, o=xsq[xb][:, 0:tw], i_=ps:
                        e.activation(out=o, in_=i_, func=AF.Square),
                        [f"pan{kc}"], [f"xsq{xb}{ph}"])
                    mm(PSA[0][0:1, 0:tw], onescol_b, ps,
                       ["ocb", f"pan{kc}"], "PS0",
                       start=(kc == 0), stop=(kc == NKC - 1))
                    mm(PSA[1][0:1, 0:tw], onescol_b, xsq[xb][:, 0:tw],
                       ["ocb", f"xsq{xb}{ph}"], "PS1",
                       start=(kc == 0), stop=(kc == NKC - 1))
                vec(lambda e, t0=t0, tw=tw:
                    e.tensor_copy(mrow[:, t0:t0 + tw], PSA[0][0:1, 0:tw]),
                    ["PS0"], [f"mrow{ph}"])
                vec(lambda e, t0=t0, tw=tw:
                    e.tensor_copy(arow[:, t0:t0 + tw], PSA[1][0:1, 0:tw]),
                    ["PS1"], [f"arow{ph}"])
            vec(lambda e: e.tensor_scalar(out=mrow, in0=mrow,
                                          scalar1=1.0 / D, scalar2=None,
                                          op0=OP.mult),
                [f"mrow{ph}"], [f"mrow{ph}"])
            vec(lambda e: e.tensor_scalar(out=arow, in0=arow,
                                          scalar1=1.0 / D, scalar2=None,
                                          op0=OP.mult),
                [f"arow{ph}"], [f"arow{ph}"])
            vec(lambda e: e.scalar_tensor_tensor(out=brow, in0=mrow,
                                                 scalar=-1.0, in1=mrow,
                                                 op0=OP.mult, op1=OP.mult),
                [f"mrow{ph}"], [f"brow{ph}"])
            vec(lambda e: e.tensor_tensor(out=arow, in0=arow, in1=brow,
                                          op=OP.add),
                [f"arow{ph}", f"brow{ph}"], [f"arow{ph}"])
            vec(lambda e: e.tensor_scalar(out=arow, in0=arow, scalar1=EPS,
                                          scalar2=None, op0=OP.add),
                [f"arow{ph}"], [f"arow{ph}"])
            scl(lambda e: e.activation(out=arow, in_=arow, func=AF.Ln),
                [f"arow{ph}"], [f"arow{ph}"])
            scl(lambda e: e.activation(out=arow, in_=arow, func=AF.Exp,
                                       scale=-0.5),
                [f"arow{ph}"], [f"arow{ph}"])
            vec(lambda e: e.scalar_tensor_tensor(out=brow, in0=mrow,
                                                 scalar=-1.0, in1=arow,
                                                 op0=OP.mult, op1=OP.mult),
                [f"mrow{ph}", f"arow{ph}"], [f"brow{ph}"])
            for it, (t0, tw) in enumerate(TT):
                mm(PSA[2][:, 0:tw], ones1f, arow[:, t0:t0 + tw],
                   ["o1f", f"arow{ph}"], "PS2")
                mm(PSA[3][:, 0:tw], ones1f, brow[:, t0:t0 + tw],
                   ["o1f", f"brow{ph}"], "PS3")
                scl(lambda e, t0=t0, tw=tw:
                    e.copy(out=a_b[:, t0:t0 + tw], in_=PSA[2][:, 0:tw]),
                    ["PS2"], [f"ab{ph}"])
                scl(lambda e, t0=t0, tw=tw:
                    e.copy(out=b_b[:, t0:t0 + tw], in_=PSA[3][:, 0:tw]),
                    ["PS3"], [f"bb{ph}"])
            if router:
                tokm_sb = A.tb[0:1, 2 * moff:2 * moff + T]
                tkkey = f"mrow{ph}"
                mask_b = tmp_ln[1]
                mkkey = f"tmpln1{ph}"
                mab = A.bf16(128, T, "mab")
                sqs4 = tmp_ln[0]
                sqkey = f"tmpln0{ph}"
                qc_sb = A.f32(128, 64, "qck")
                bsel_sb = A.f32(128, 64, "bselk")
                mb1 = A.f32(1, 1, "mb1")
                mbcol = A.f32(128, 1, "mbcol")
                dma("gpsimd", lambda e: e.dma_start(out=tokm_sb, in_=tokmask),
                    [tkkey], [tkkey])
                dma("sync", lambda e: e.dma_start(out=bsel_sb, in_=bsel_i),
                    [], ["bselk"])
                for it, (t0, tw) in enumerate(TT):
                    mm(PSA[4][:, 0:tw], ones1b, tokm_sb[:, t0:t0 + tw],
                       ["o1b", tkkey], "PS4")
                    scl(lambda e, t0=t0, tw=tw:
                        e.copy(out=mask_b[:, t0:t0 + tw],
                               in_=PSA[4][:, 0:tw]),
                        ["PS4"], [mkkey])
                vec(lambda e: e.tensor_tensor(out=mab, in0=mask_b, in1=a_b,
                                              op=OP.mult),
                    [mkkey, f"ab{ph}"], ["mab"])
                for kc in range(NKC):
                    vec(lambda e, k=kc: e.scalar_tensor_tensor(
                        out=sqs4, in0=panel[:, k * T:(k + 1) * T],
                        scalar=1.0, in1=mab, op0=OP.mult, op1=OP.mult,
                        accum_out=qacc[:, k:k + 1]),
                        [f"pan{kc}", "mab"], [sqkey, f"qacc{kc}"])
                vec(lambda e: e.scalar_tensor_tensor(
                    out=mab[0:1, :], in0=brow, scalar=1.0, in1=tokm_sb,
                    op0=OP.mult, op1=OP.mult, accum_out=mb1),
                    [f"brow{ph}", tkkey], ["mab", "mb1"])
                mm(PSA[4][:, 0:1], ones1f, mb1, ["o1f", "mb1"], "PS4")
                vec(lambda e: e.tensor_copy(mbcol, PSA[4][:, 0:1]),
                    ["PS4"], ["mbcol"])
                QACCK = [f"qacc{k}" for k in range(NKC)]
                vec(lambda e: e.tensor_scalar(out=qacc, in0=qacc,
                                              scalar1=mbcol[:, 0:1],
                                              scalar2=None, op0=OP.add),
                    QACCK + ["mbcol"], QACCK)
                vec(lambda e: e.tensor_tensor(
                    out=qc_sb.rearrange("p (b k) -> p b k", b=2),
                    in0=qacc[:, None, :].broadcast_to([128, 2, NKC]),
                    in1=bsel_sb.rearrange("p (b k) -> p b k", b=2),
                    op=OP.mult),
                    QACCK + ["bselk"], ["qck"])
                dma("sync", lambda e: e.dma_start(out=q_in.ap(), in_=qc_sb),
                    ["qck"], ["q_in"])
                coll("CC3", "AllReduce", OP.add, q_in, q_out,
                     ["q_in"], ["q_out"])
            for kc in range(NKC):
                eng, pr = vec, kc % 2
                eng(lambda e, k=kc, p=pr:
                    e.tensor_tensor(out=tmp_ln[p],
                                    in0=panel[:, k * T:(k + 1) * T],
                                    in1=a_b, op=OP.mult),
                    [f"pan{kc}", f"ab{ph}"], [f"tmpln{pr}{ph}"])
                eng(lambda e, k=kc, p=pr:
                    e.tensor_tensor(out=panel[:, k * T:(k + 1) * T],
                                    in0=tmp_ln[p], in1=b_b, op=OP.add),
                    [f"tmpln{pr}{ph}", f"bb{ph}"], [f"pan{kc}"])
            A.pop_fence()

        phase_ln("x", "L1")

        A.push()
        dt_raw_full = A.f32(128, T, ("dt_raw", "dt_pad"))
        dt_raw = dt_raw_full[0:H, :]
        A.push()
        wsl = [A.bf16(128, NKC * 128, f"wsl{i}") for i in range(2)]
        dsl = [A.bf16(128, KCV * 128, f"dsl{i}") for i in range(2)]
        raw = [A.bf16(128, T + 4, f"raw{i}") for i in range(2)]
        xbct = [A.bf16(128, T, f"xbct{i}") for i in range(2)]
        bndb = A.bf16(4, 128, "bndb")
        NDT = 8
        TQ = T // NDT
        CPP = TQ // QC
        slotA = A.f32(64, TQ, "dZa")
        slotBq = A.f32(64, TQ, "dZb")
        sCq = A.f32(64, TQ, "dCq")
        hspB = A.bf16(64, TQ, "dHsp")
        eltwb = A.bf16(64, TQ, "eltw")
        eltb = A.bf16(128, CPP * 64, "eltb")
        eLendF = A.f32(64, NQ, "eLF")
        elendt = A.f32(NQ, H, "elendtk")
        caH = A.f32(64, NDT, "caH")

        vec(lambda e: e.memset(raw[0][:, 0:4], 0.0), [], ["raw0"])
        vec(lambda e: e.memset(raw[1][:, 0:4], 0.0), [], ["raw1"])
        vec(lambda e: e.memset(bndb, 0.0), [], ["bndb"])

        def emit_dt_half(half):
            hc = half * TQ
            X = dt_raw_full[0:64, hc:hc + TQ]
            sC = sCq
            slotB = slotBq
            vec(lambda e: e.tensor_scalar(out=slotA, in0=X,
                                          scalar1=dtb_sb[:, 0:1],
                                          scalar2=None, op0=OP.add),
                ["dt_raw", "dtbk"], ["dZa"])
            vec(lambda e: e.scalar_tensor_tensor(out=slotB, in0=slotA,
                                                 scalar=-1.0, in1=slotA,
                                                 op0=OP.mult, op1=OP.max),
                ["dZa"], ["dZb"])
            scl(lambda e: e.activation(out=slotB, in_=slotB, func=AF.Exp,
                                       scale=-1.0), ["dZb"], ["dZb"])
            scl(lambda e: e.activation(out=slotB, in_=slotB, func=AF.Ln,
                                       bias=onescol[0:64, 0:1]),
                ["dZb", "oc"], ["dZb"])
            vec(lambda e: e.tensor_scalar(out=sC, in0=slotA, scalar1=0.0,
                                          scalar2=None, op0=OP.max),
                ["dZa"], ["dCq"])
            vec(lambda e: e.tensor_tensor(out=sC, in0=sC, in1=slotB,
                                          op=OP.add),
                ["dCq", "dZb"], ["dCq"])
            vec(lambda e: e.tensor_scalar(out=X, in0=sC,
                                          scalar1=negA[:, 0:1],
                                          scalar2=None, op0=OP.mult),
                ["dCq", "negA", "dt_raw"], ["dt_raw"])
            for qq in range(CPP):
                vec(lambda e, qq=qq: e.tensor_tensor_scan(
                    out=slotA[:, qq * QC:(qq + 1) * QC], data0=ones64x,
                    data1=X[:, qq * QC:(qq + 1) * QC], initial=0.0,
                    op0=OP.mult, op1=OP.add), ["dt_raw", "o64"], ["dZa"])
            vec(lambda e, half=half:
                e.reduce_sum(caH[:, half:half + 1], X, axis=AX.X),
                ["dt_raw"], ["caH"])
            scl(lambda e, half=half: e.activation(
                out=eLendF[:, half * CPP:(half + 1) * CPP],
                in_=slotA[:, QC - 1::QC], func=AF.Exp), ["dZa"], ["eLF"])
            scl(lambda e: e.activation(out=eltwb, in_=slotA, func=AF.Exp),
                ["dZa"], ["eltw"])
            scl(lambda e: e.activation(out=slotB, in_=sC, func=AF.Ln),
                ["dCq"], ["dZb"])
            vec(lambda e: e.tensor_tensor(out=sC, in0=slotB, in1=slotA,
                                          op=OP.subtract),
                ["dZa", "dZb"], ["dCq"])
            for ti, (curv, ck, outs) in enumerate(
                    [(slotA, "dZa", lsp_s), (sCq, "dCq", nrsp_s)]):
                for si in range(3):
                    vec(lambda e, c=curv: e.tensor_copy(hspB, c),
                        [ck], ["dHsp"])
                    dma("sync", lambda e, o=outs[si], hc=hc:
                        e.dma_start(out=o[:, hc:hc + TQ], in_=hspB),
                        ["dHsp"], [f"sp{ti}{si}h{half}"])
                    if si < 2:
                        vec(lambda e, c=curv:
                            e.tensor_tensor(out=c, in0=c, in1=hspB,
                                            op=OP.subtract),
                            [ck, "dHsp"], [ck])

        def emit_dt_trans(half):
            for i in range(CPP):
                tr(PBA[1][:, i * 64:(i + 1) * 64],
                   eltwb[:, i * 128:(i + 1) * 128],
                   ident_bf[0:64, 0:64],
                   ["eltw", "idb"], "PBT1")
            vec(lambda e: e.tensor_copy(eltb, PBA[1][:, 0:CPP * 64]),
                ["PBT1"], ["eltb"])
            dma("sync", lambda e, half=half: e.dma_start(
                out=elt_s.ap()[half * CPP:(half + 1) * CPP]
                .rearrange("q p h -> p q h"),
                in_=eltb.rearrange("p (q h) -> p q h", q=CPP)),
                ["eltb"], [f"eltd{half}"])

        def emit_dt_fin():
            tr(PSA[5][0:NQ, 0:H], eLendF, ident_f[0:H, 0:H],
               ["eLF", "idf"], "PS5")
            vec(lambda e: e.tensor_copy(elendt, PSA[5][0:NQ, 0:H]),
                ["PS5"], ["elendtk"])
            dma("sync", lambda e: e.dma_start(out=elend_s, in_=elendt),
                ["elendtk"], ["elend_d"])
            vec(lambda e: e.tensor_tensor(out=caH[:, 0:4],
                                          in0=caH[:, 0:4],
                                          in1=caH[:, 4:8], op=OP.add),
                ["caH"], ["caH"])
            vec(lambda e: e.tensor_tensor(out=caH[:, 0:2],
                                          in0=caH[:, 0:2],
                                          in1=caH[:, 2:4], op=OP.add),
                ["caH"], ["caH"])
            vec(lambda e: e.tensor_tensor(out=caH[:, 0:1],
                                          in0=caH[:, 0:1],
                                          in1=caH[:, 1:2], op=OP.add),
                ["caH"], ["caH"])
            scl(lambda e: e.activation(out=caH[:, 0:1],
                                       in_=caH[:, 0:1], func=AF.Exp),
                ["caH"], ["caH"])
            dma("sync", lambda e: e.dma_start(out=ca_in.ap(),
                                              in_=caH[:, 0:1]),
                ["caH"], ["ca_in"])

        gidx = 0
        cidx = 0
        for oc in range(49):
            wb = oc % 2
            dma("gpsimd", lambda e, o=oc, w=wb:
                e.dma_start(out=wsl[w],
                            in_=w_hbcdt[o].rearrange("p kc m -> p (kc m)")),
                [], [f"wsl{wb}"])
            if oc >= 1:
                dma("gpsimd", lambda e, o=oc, w=wb:
                    e.dma_start(out=dsl[w].rearrange("p (k m) -> p k m",
                                                     k=KCV),
                                in_=diagw[o - 1]
                                .rearrange("k p m -> p k m")),
                    [], [f"dsl{wb}"])
            for it, (t0, tw) in enumerate(TT):
                bank = gidx % 4
                for kc in range(NKC):
                    mm(PSA[bank][:, 0:tw],
                       wsl[wb][:, kc * 128:(kc + 1) * 128],
                       panel[:, kc * T + t0:kc * T + t0 + tw],
                       [f"wsl{wb}", f"pan{kc}"], f"PS{bank}",
                       start=(kc == 0), stop=(kc == NKC - 1))
                scl(lambda e, w=wb, t0=t0, tw=tw, b=bank:
                    e.copy(out=raw[w][:, 4 + t0:4 + t0 + tw],
                           in_=PSA[b][:, 0:tw]),
                    [f"PS{bank}"], [f"raw{wb}"])
                gidx += 1
            if oc == 0:
                vec(lambda e: e.tensor_copy(dt_raw, raw[0][0:H, 4:4 + T]),
                    ["raw0"], ["dt_raw"])
            else:
                for it, (t0, tw) in enumerate(TT):
                    bank = 4 + (cidx % 2)
                    for k in range(KCV):
                        mm(PSA[bank][:, 0:tw],
                           dsl[wb][:, k * 128:(k + 1) * 128],
                           raw[wb][:, t0 + k + 1:t0 + k + 1 + tw],
                           [f"dsl{wb}", f"raw{wb}"], f"PS{bank}",
                           start=(k == 0), stop=(k == KCV - 1))
                    scl(lambda e, w=wb, t0=t0, tw=tw, b=bank, o=oc:
                        e.activation(out=xbct[w][:, t0:t0 + tw],
                                     in_=PSA[b][:, 0:tw], func=AF.Silu,
                                     bias=cb_sb[:, o - 1:o]),
                        [f"PS{bank}", "cb"], [f"xbct{wb}"])
                    cidx += 1
                vec(lambda e, w=wb, o=oc:
                    e.tensor_copy(pf_sb[:, (o - 1) * 3:(o - 1) * 3 + 3],
                                  raw[w][:, 4:7]),
                    [f"raw{wb}"], ["pf"])
                tr(PBA[0][0:3, 0:128], raw[wb][:, T + 1:T + 4], ident_bf,
                   [f"raw{wb}", "idb"], "PBT0")
                vec(lambda e: e.tensor_copy(bndb[0:3, :], PBA[0][0:3, 0:128]),
                    ["PBT0"], ["bndb"])
                dma("sync", lambda e, o=oc:
                    e.dma_start(out=bnd_in.ap()[:, (o - 1) * 128:o * 128],
                                in_=bndb),
                    ["bndb"], ["bnd_in"])
                dma("sync", lambda e, w=wb, o=oc:
                    e.dma_start(out=xbc_s[(o - 1) * 128:o * 128, :],
                                in_=xbct[w]),
                    [f"xbct{wb}"], [f"xbcd{oc - 1}"])
            if 2 <= oc <= 9:
                emit_dt_half(oc - 2)
            elif 11 <= oc <= 18:
                emit_dt_trans(oc - 11)
            elif oc == 20:
                emit_dt_fin()
        coll("CC1", "AllGather", OP.bypass, bnd_in, bnd_out,
             ["bnd_in"], ["bnd_out"])
        A.pop_fence()

        A.pop_fence()
        A.push()
        bnd_tm = A.bf16(4, CONV, "bnd_tm")
        rbuf = [A.bf16(4, CONV, f"rbuf{i}") for i in range(2)]
        bndc_sb = A.f32(4, 8, "bndck")
        rawx = A.bf16(128, 8, "rawx")
        pacc = A.f32(128, 3, "pacc")
        dma("sync", lambda e: e.dma_start(out=bndc_sb, in_=bndc),
            [], ["bndck"])
        vec(lambda e: e.memset(bnd_tm, 0.0), [], ["bnd_tm"])
        for r in range(8):
            rb_ = r % 2
            dma("scalq", lambda e, r=r, w=rb_:
                e.dma_start(out=rbuf[w],
                            in_=bnd_out.ap()[r * 4:(r + 1) * 4, :]),
                ["bnd_out"], [f"rbuf{rb_}"])
            vec(lambda e, r=r, w=rb_: e.scalar_tensor_tensor(
                out=bnd_tm, in0=rbuf[w], scalar=bndc_sb[:, r:r + 1],
                in1=bnd_tm, op0=OP.mult, op1=OP.add),
                [f"rbuf{rb_}", "bnd_tm", "bndck"], ["bnd_tm"])
        for c in range(NHBC):
            tr(PBA[1][:, 0:4], bnd_tm[0:4, c * 128:(c + 1) * 128],
               ident_bf[0:4, 0:4], ["bnd_tm", "idb"], "PBT1")
            vec(lambda e: e.tensor_copy(rawx[:, 0:3], PBA[1][:, 0:3]),
                ["PBT1"], ["rawx"])
            vec(lambda e, c=c: e.tensor_copy(rawx[:, 3:6],
                                             pf_sb[:, c * 3:c * 3 + 3]),
                ["pf"], ["rawx"])
            vec(lambda e, c=c: e.tensor_scalar(
                out=pacc, in0=rawx[:, 0:3],
                scalar1=cw_sb[:, c * KCV:c * KCV + 1], scalar2=None,
                op0=OP.mult), ["rawx", "cw"], ["pacc"])
            for k in range(1, KCV):
                vec(lambda e, c=c, k=k: e.scalar_tensor_tensor(
                    out=pacc, in0=rawx[:, k:k + 3],
                    scalar=cw_sb[:, c * KCV + k:c * KCV + k + 1],
                    in1=pacc, op0=OP.mult, op1=OP.add),
                    ["rawx", "cw", "pacc"], ["pacc"])
            scl(lambda e, c=c: e.activation(out=pxbc[:, c * 3:c * 3 + 3],
                                            in_=pacc, func=AF.Silu,
                                            bias=cb_sb[:, c:c + 1]),
                ["pacc", "cb"], ["pxbc"])
        A.pop_fence()

        def load_nr(NRq, q, tag):
            for si in range(3):
                dma("sync", lambda e, si=si, q=q:
                    e.dma_start(out=NRq[si * 4:(si + 1) * 4, :]
                                .rearrange("k (hb t) -> k hb t", hb=NHB),
                                in_=nrsp_s[si]
                                .rearrange("(hb k) t -> k hb t", k=4)
                                [:, :, q * QC:(q + 1) * QC]),
                    [f"sp1{si}h{q // 2}"], [tag])

        def load_l3(Lr3, q, quarter, tag, nh=16):
            for si in range(3):
                dma("sync", lambda e, si=si, q=q, hf=quarter, nh=nh:
                    e.dma_start(out=Lr3[si:si + 1]
                                .rearrange("p (h t) -> p h t", h=nh),
                                in_=lsp_s[si][hf * nh:(hf + 1) * nh,
                                              q * QC:(q + 1) * QC]
                                .unsqueeze(0)),
                    [f"sp0{si}h{q // 2}"], [tag])

        def load_elrow(dst, q, tag):
            dma("sync", lambda e, q=q:
                e.dma_start(out=dst, in_=elend_s[q:q + 1, :]),
                ["elend_d"], [tag])

        def overlay(buf, chunks, tag):
            for li, c in enumerate(chunks):
                vec(lambda e, li=li, c=c:
                    e.tensor_copy(buf[:, li * 128:li * 128 + 3],
                                  pxbc[:, c * 3:c * 3 + 3]),
                    ["pxbc", tag], [tag])

        A.push()
        xfm_q = A.bf16(128, I, "xfmq")
        xtm_q = A.bf16(128, I, "xtmq")
        bq = A.bf16(128, G * 128, "bq")
        btm = A.bf16(128, G * 128, "btm")
        NRq = A.bf16(12, NHB * QC, "NRqA")
        Lr3 = A.bf16(3, 32 * QC, "Lr3A")
        elr = A.f32(1, H, "elrA")
        cumAq = A.f32(128, H, "cumAqA")
        wcl = A.f32(128, H, "wclA")
        wcols = A.f32(128, H, "wcolsA")
        bw = A.bf16(128, 1024, "bwA")
        sttb = A.f32(128, 512, "sttA")

        for q in range(NQ):
            load_nr(NRq, q, "NRqA")
            load_elrow(elr, q, "elrA")
            dma("scalq", lambda e, q=q:
                e.dma_start(out=xfm_q.rearrange("p (c t) -> p c t", c=32),
                            in_=xbc_s[0:I, q * QC:(q + 1) * QC]
                            .rearrange("(c p) t -> p c t", p=128)),
                [f"xbcd{c}" for c in range(32)], ["xfmq"])
            dma("scalq", lambda e, q=q:
                e.dma_start(out=bq.rearrange("p (c t) -> p c t", c=8),
                            in_=xbc_s[I:I + G * 128, q * QC:(q + 1) * QC]
                            .rearrange("(c p) t -> p c t", p=128)),
                [f"xbcd{32 + c}" for c in range(8)], ["bq"])
            if q == 0:
                overlay(xfm_q, range(32), "xfmq")
                overlay(bq, range(32, 40), "bq")
            mm(PSA[5][:, 0:H], ones1f, elr, ["o1f", "elrA"], "PS5")
            vec(lambda e: e.tensor_copy(cumAq, PSA[5][:, 0:H]),
                ["PS5"], ["cumAqA"])
            for c8 in range(0, 32, 8):
                for c in range(c8, c8 + 8):
                    tr(PBA[0][:, (c - c8) * 128:(c - c8 + 1) * 128],
                       xfm_q[:, c * 128:(c + 1) * 128], ident_bf,
                       ["xfmq", "idb"], "PBT0")
                scl(lambda e, c8=c8:
                    e.copy(out=xtm_q[:, c8 * 128:(c8 + 8) * 128],
                           in_=PBA[0]),
                    ["PBT0"], ["xtmq"])
            dma("sync", lambda e, q=q: e.dma_start(out=xtm_s[q], in_=xtm_q),
                ["xtmq"], [f"xtmd{q}"])
            for gx in range(8):
                tr(PBA[1][:, gx * 128:(gx + 1) * 128],
                   bq[:, gx * 128:(gx + 1) * 128], ident_bf,
                   ["bq", "idb"], "PBT1")
            scl(lambda e: e.copy(out=btm, in_=PBA[1]),
                ["PBT1"], ["btm"])
            for half in range(2):
                load_l3(Lr3, q, half, "Lr3A", nh=32)
                for hbl in range(8):
                    hb = half * 8 + hbl
                    mm(PSA[4][:, hb * 4:(hb + 1) * 4],
                       NRq[:, hb * QC:(hb + 1) * QC],
                       ind12_sb[:, QC - 1::QC],
                       ["NRqA", "ind12"], "PS4", start=True, stop=False)
                    mm(PSA[4][:, hb * 4:(hb + 1) * 4],
                       ones3b,
                       Lr3[:, hbl * 4 * QC + QC - 1:(hbl + 1) * 4 * QC:QC],
                       ["o3b", "Lr3A"], "PS4", start=False, stop=True)
            vec(lambda e: e.tensor_scalar(out=wcl, in0=PSA[4][:, 0:H],
                                          scalar1=30.0, scalar2=None,
                                          op0=OP.min), ["PS4"], ["wclA"])
            scl(lambda e: e.activation(out=wcols, in_=wcl, func=AF.Exp),
                ["wclA"], ["wcolsA"])
            for gx in range(8):
                ub = 2 + (gx % 2)
                pool(lambda e, gx=gx: e.tensor_tensor(
                    out=bw.rearrange("p (h n) -> p h n", h=8),
                    in0=btm[:, gx * 128:(gx + 1) * 128][:, None, :]
                    .broadcast_to([128, 8, 128]),
                    in1=wcols[:, gx * 8:(gx + 1) * 8][:, :, None]
                    .broadcast_to([128, 8, 128]),
                    op=OP.mult), ["btm", "wcolsA"], ["bwA"])
                for hl8 in range(8):
                    h = gx * 8 + hl8
                    mm(PSA[ub][:, hl8 * 64:(hl8 + 1) * 64],
                       bw[:, hl8 * 128:(hl8 + 1) * 128],
                       xtm_q[:, h * 64:(h + 1) * 64],
                       ["bwA", "xtmq"], f"PS{ub}")
                pool(lambda e, gx=gx: e.tensor_tensor(
                    out=sttb.rearrange("p (h x) -> p h x", h=8),
                    in0=Sst[:, gx * 512:(gx + 1) * 512]
                    .rearrange("p (h x) -> p h x", h=8),
                    in1=cumAq[:, gx * 8:(gx + 1) * 8][:, :, None]
                    .broadcast_to([128, 8, 64]),
                    op=OP.mult), [f"Sg{gx}", "cumAqA"], ["sttA"])
                vec(lambda e, gx=gx, b=ub: e.tensor_tensor(
                    out=Sst[:, gx * 512:(gx + 1) * 512], in0=sttb,
                    in1=PSA[b], op=OP.add),
                    ["sttA", f"PS{ub}"], [f"Sg{gx}"])
        dma("gpsimd", lambda e: e.dma_start(out=st_in.ap(), in_=Sst),
            ALLSG, ["st_in"])
        coll("CC2a", "AllGather", OP.bypass, st_in, st_out,
             ["st_in"], ["st_out"])
        coll("CC2b", "AllGather", OP.bypass, ca_in, ca_out,
             ["ca_in"], ["ca_out"])
        A.pop_fence()

        A.push()
        GS = 256
        NGS = I // GS
        gsl = [A.bf16(128, NKC * GS, f"gsl{i}") for i in range(2)]
        gout = [A.bf16(128, GS, f"gout{i}") for i in range(2)]
        gg = 0
        for s in range(NGS):
            wb2 = s % 2
            dma("gpsimd", lambda e, s=s, w=wb2:
                e.dma_start(out=gsl[w].rearrange("p (kc m) -> p kc m",
                                                 kc=NKC),
                            in_=w_gate[:, :, s * GS:(s + 1) * GS]
                            .rearrange("kc p m -> p kc m")),
                [], [f"gsl{wb2}"])
            for tt in range(T16):
                bank = gg % 4
                for kc in range(NKC):
                    mm(PSA[bank][:, 0:GS],
                       panel[:, kc * T + tt * 128:kc * T + (tt + 1) * 128],
                       gsl[wb2][:, kc * GS:(kc + 1) * GS],
                       [f"gsl{wb2}", f"pan{kc}"], f"PS{bank}",
                       start=(kc == 0), stop=(kc == NKC - 1))
                ob = gg % 2
                scl(lambda e, b=bank, o=ob:
                    e.activation(out=gout[o], in_=PSA[b][:, 0:GS],
                                 func=AF.Silu),
                    [f"PS{bank}"], [f"gout{ob}"])
                dma("sync", lambda e, s=s, tt=tt, o=ob:
                    e.dma_start(out=gate_s[tt * 128:(tt + 1) * 128,
                                           s * GS:(s + 1) * GS],
                                in_=gout[o]),
                    [f"gout{ob}"], [f"gated{s}_{tt}"])
                gg += 1
        A.pop_fence()

        A.push()
        cumAg = A.f32(H, 8, "cumAg")
        chm_sb = A.f32(H, 64, "chmk")
        chm1_sb = A.f32(H, 64, "chm1k")
        coefs = A.f32(H, 8, "coefs")
        ct1 = A.f32(H, 8, "ct1")
        ct2 = A.f32(H, 4, "ct2")
        ct3 = A.f32(H, 2, "ct3")
        coefT = A.f32(8, H, "coefT")
        coefrow = A.f32(1, 512, "coefrow")
        Gr = A.bf16(128, H * P, "Gr")
        cfb = A.f32(128, H, "cfb")
        tbuf = A.f32(128, H * P, "tbuf")
        dma("sync", lambda e: e.dma_start(out=chm_sb, in_=chmh), [], ["chmk"])
        dma("sync", lambda e: e.dma_start(out=chm1_sb, in_=chm1mh),
            [], ["chm1k"])
        dma("sync", lambda e: e.dma_start(
            out=cumAg,
            in_=ca_out.ap().rearrange("(r h) x -> h (r x)", r=8),
            allow_slow_non_contiguous=True), ["ca_out"], ["cumAg"])
        for r in range(8):
            vec(lambda e, r=r: e.tensor_tensor(
                out=ct1, in0=cumAg, in1=chm_sb[:, r * 8:(r + 1) * 8],
                op=OP.mult), ["cumAg", "chmk"], ["ct1"])
            vec(lambda e, r=r: e.tensor_tensor(
                out=ct1, in0=ct1, in1=chm1_sb[:, r * 8:(r + 1) * 8],
                op=OP.add), ["ct1", "chm1k"], ["ct1"])
            vec(lambda e: e.tensor_tensor(out=ct2, in0=ct1[:, 0:4],
                                          in1=ct1[:, 4:8], op=OP.mult),
                ["ct1"], ["ct2"])
            vec(lambda e: e.tensor_tensor(out=ct3, in0=ct2[:, 0:2],
                                          in1=ct2[:, 2:4], op=OP.mult),
                ["ct2"], ["ct3"])
            vec(lambda e, r=r: e.tensor_tensor(out=coefs[:, r:r + 1],
                                               in0=ct3[:, 0:1],
                                               in1=ct3[:, 1:2], op=OP.mult),
                ["ct3"], ["coefs"])
        tr(PSA[5][0:8, 0:H], coefs, ident_f[0:H, 0:H], ["coefs", "idf"], "PS5")
        vec(lambda e: e.tensor_copy(coefT, PSA[5][0:8, 0:H]),
            ["PS5"], ["coefT"])
        dma("sync", lambda e: e.dma_start(out=coefb_s, in_=coefT),
            ["coefT"], ["coefb_d"])
        dma("sync", lambda e: e.dma_start(
            out=coefrow.rearrange("p (r h) -> p r h", r=8),
            in_=coefb_s.unsqueeze(0)), ["coefb_d"], ["coefrow"])
        vec(lambda e: e.memset(Sst, 0.0), [], ALLSG)
        for r in range(8):
            dma("scalq", lambda e, r=r:
                e.dma_start(out=Gr,
                            in_=st_out.ap()[r * 128:(r + 1) * 128, :]),
                ["st_out"], ["Gr"])
            mm(PSA[4][:, 0:H], ones1f, coefrow[:, r * H:(r + 1) * H],
               ["o1f", "coefrow"], "PS4")
            vec(lambda e: e.tensor_copy(cfb, PSA[4][:, 0:H]),
                ["PS4"], ["cfb"])
            vec(lambda e: e.tensor_tensor(
                out=tbuf.rearrange("p (h x) -> p h x", h=H),
                in0=Gr.rearrange("p (h x) -> p h x", h=H),
                in1=cfb[:, :, None].broadcast_to([128, H, P]),
                op=OP.mult), ["Gr", "cfb"], ["tbuf"])
            for gx in range(8):
                vec(lambda e, gx=gx: e.tensor_tensor(
                    out=Sst[:, gx * 512:(gx + 1) * 512],
                    in0=tbuf[:, gx * 512:(gx + 1) * 512],
                    in1=Sst[:, gx * 512:(gx + 1) * 512], op=OP.add),
                    ["tbuf", f"Sg{gx}"], [f"Sg{gx}"])
        for gx in range(8):
            scl(lambda e, gx=gx: e.copy(
                out=S_bf[:, gx * 512:(gx + 1) * 512],
                in_=Sst[:, gx * 512:(gx + 1) * 512]),
                [f"Sg{gx}"], [f"Sbf{gx}"])
        A.pop_fence()

        A.push()
        xtmu = [A.bf16(128, 512, f"xtmu{i}") for i in range(2)]
        bq2x = [A.bf16(128, G * 128, f"bq2x{i}") for i in range(2)]
        cq2x = [A.bf16(128, G * 128, f"cq2x{i}") for i in range(2)]
        btm2 = A.bf16(128, G * 128, "btm2")
        Gm = A.bf16(128, G * 128, "Gm")
        NRL15 = [A.bf16(15, 4 * QC, f"NRq2x{i}") for i in range(2)]
        ILx = [A.bf16(15, 8 * QC, f"Lr32x{i}") for i in range(2)]
        elrx = [A.f32(1, H, f"elrx{i}") for i in range(2)]
        cumAq2 = [A.f32(128, H, f"cumAq2_{i}") for i in range(2)]
        dmy = A.f32(128, 1024, "dmy")
        wd = [A.bf16(128, 1024, f"wd{i}") for i in range(2)]
        wt = [A.bf16(128, 1024, (f"wtA{i}", f"wtB{i}")) for i in range(2)]
        xw = [A.bf16(128, 512, f"xw{i}") for i in range(2)]
        ysd = A.bf16(128, 512, "ysd")
        sdec = A.f32(128, 512, "sdec")
        eltq = [A.bf16(128, 64, f"eltq{i}") for i in range(3)]
        ysb = [A.f32(128, 512, f"ysb{i}") for i in range(2)]
        sqscr = A.bf16(128, 512, "sqx")
        ssq = [A.f32(128, 1, f"ssq{i}") for i in range(2)]
        rstd = [A.f32(128, 1, f"rstd{i}") for i in range(2)]
        yng = A.bf16(128, 512, "yng")
        ggt = [A.bf16(128, 512, f"ggt{i}") for i in range(2)]
        b30 = A.f32(128, 1, "b30")
        vec(lambda e: e.memset(b30, 30.0), [], ["b30"])
        E30 = float(np.exp(30.0))

        NU = NQ * 8

        def load_nrl(tu):
            q, gx = tu // 8, tu % 8
            b = tu % 2
            dma("sync", lambda e, q=q, gx=gx, b=b:
                e.dma_start(
                    out=ILx[b][12:15, :]
                    .rearrange("si (h t) -> si h t", h=8),
                    in_=lsp3[:, gx * 8:(gx + 1) * 8,
                             q * QC:(q + 1) * QC]),
                [f"sp0{si}h{q // 2}" for si in range(3)], [f"Lr32x{b}"])

        def load_nr2(g):
            tu = 2 * g
            q, gx = tu // 8, tu % 8
            b = g % 2
            for si in range(3):
                dma("sync", lambda e, si=si, q=q, gx=gx, b=b:
                    e.dma_start(
                        out=NRL15[b][si * 4:(si + 1) * 4, :]
                        .rearrange("k (hb t) -> k hb t", hb=4),
                        in_=nrsp3[si]
                        .rearrange("(hb k) t -> k hb t", k=4)
                        [:, gx * 2:gx * 2 + 4, q * QC:(q + 1) * QC]),
                    [f"sp1{si}h{q // 2}"], [f"NRq2x{b}"])

        def load_qgrp(q):
            b = q % 2
            dma("sync", lambda e, q=q, b=b:
                e.dma_start(out=bq2x[b].rearrange("p (c t) -> p c t", c=8),
                            in_=xbc_s[I:I + G * 128,
                                      q * QC:(q + 1) * QC]
                            .rearrange("(c p) t -> p c t", p=128)),
                [f"xbcd{32 + c}" for c in range(8)], [f"bq2x{b}"])
            dma("sync", lambda e, q=q, b=b:
                e.dma_start(out=cq2x[b].rearrange("p (c t) -> p c t", c=8),
                            in_=xbc_s[I + G * 128:CONV,
                                      q * QC:(q + 1) * QC]
                            .rearrange("(c p) t -> p c t", p=128)),
                [f"xbcd{40 + c}" for c in range(8)], [f"cq2x{b}"])
            dma("sync", lambda e, q=q, b3=q % 3:
                e.dma_start(out=eltq[b3], in_=elt_s.ap()[q]),
                [f"eltd{q // 2}"], [f"eltq{q % 3}"])
            dma("sync", lambda e, q=q, b=b:
                e.dma_start(out=elrx[b], in_=elend_s[q:q + 1, :]),
                ["elend_d"], [f"elrx{b}"])

        def b_front_d(ui):
            a2 = ui % 2
            g2 = (ui // 2) % 2
            aks = [f"PS{2 * a2}", f"PS{2 * a2 + 1}"]
            for sub in range(2):
                hbl = (ui % 2) * 2 + sub
                mm(PF[a2][:, sub * 512:sub * 512 + 512],
                   NRL15[g2][:, hbl * QC:(hbl + 1) * QC],
                   ILx[a2][:, sub * 512:(sub + 1) * 512],
                   [f"NRq2x{g2}", f"Lr32x{a2}"], aks[sub])

        def b_front(ui):
            q, gx = ui // 8, ui % 8
            u = ui % 2
            a2 = ui % 2
            qp = q % 2
            aks = [f"PS{2 * a2}", f"PS{2 * a2 + 1}"]
            if gx == 0:
                if q + 1 < NQ:
                    load_qgrp(q + 1)
                mm(PSA[4][:, 0:H], ones1f, elrx[qp],
                   ["o1f", f"elrx{qp}"], "PS4")
                vec(lambda e, qp=qp:
                    e.tensor_copy(cumAq2[qp], PSA[4][:, 0:H]),
                    ["PS4"], [f"cumAq2_{qp}"])
                for g2 in range(8):
                    half = g2 // 4
                    mm(PF[a2][:, half * 512 + (g2 % 4) * 128:
                       half * 512 + (g2 % 4 + 1) * 128],
                       bq2x[qp][:, g2 * 128:(g2 + 1) * 128],
                       cq2x[qp][:, g2 * 128:(g2 + 1) * 128],
                       [f"bq2x{qp}", f"cq2x{qp}"], aks[half])
                for half in range(2):
                    vec(lambda e, b=half, a_=a2: e.tensor_tensor(
                        out=Gm[:, b * 512:(b + 1) * 512]
                        .rearrange("p (g t) -> p g t", g=4),
                        in0=PF[a_][:, b * 512:(b + 1) * 512]
                        .rearrange("p (g t) -> p g t", g=4),
                        in1=tri_sb[:, None, :].broadcast_to([128, 4, 128]),
                        op=OP.mult), [aks[half], "tri"], ["Gm"])
                for g2 in range(8):
                    tr(PBA[1][:, g2 * 128:(g2 + 1) * 128],
                       bq2x[qp][:, g2 * 128:(g2 + 1) * 128], ident_bf,
                       [f"bq2x{qp}", "idb"], "PBT1")
                scl(lambda e: e.activation(out=btm2, in_=PBA[1],
                                           func=AF.Identity,
                                           scale=E30),
                    ["PBT1"], ["btm2"])
            if ui + 1 < NU:
                load_nrl(ui + 1)
            if ui % 2 == 1 and ui + 1 < NU:
                load_nr2((ui + 1) // 2)
            dma("sync", lambda e, q=q, gx=gx, u=u:
                e.dma_start(out=xtmu[u],
                            in_=xtm_s[q][:, gx * 512:(gx + 1) * 512]),
                [f"xtmd{q}"], [f"xtmu{u}"])

        def b_scal(ui):
            u = ui % 2
            a2 = ui % 2
            aks = [f"PS{2 * a2}", f"PS{2 * a2 + 1}"]
            scl(lambda e, a_=a2: e.activation(
                out=dmy, in_=PF[a_], func=AF.Relu, scale=-1.0,
                bias=b30[:, 0:1]), aks + ["b30"], ["dmyA"])
            scl(lambda e, u=u: e.activation(out=wd[u], in_=dmy,
                                            func=AF.Exp, scale=-1.0),
                ["dmyA"], [f"wdA{u}", f"wdB{u}"])

        def b_cde(ui):
            q, gx = ui // 8, ui % 8
            u = ui % 2
            a2 = ui % 2
            ib = ui % 2
            qp = q % 2
            aks = [f"PS{2 * a2}", f"PS{2 * a2 + 1}"]
            vec(lambda e, u=u: e.tensor_tensor(
                out=xw[u].rearrange("p (h x) -> p h x", h=8),
                in0=xtmu[u].rearrange("p (h x) -> p h x", h=8),
                in1=wd[u][:, QC - 1::QC][:, :, None]
                .broadcast_to([128, 8, 64]), op=OP.mult),
                [f"wdA{u}", f"wdB{u}", f"xtmu{u}"], [f"xw{u}"])
            pool(lambda e, u=u: e.tensor_tensor(
                out=ysd.rearrange("p (h x) -> p h x", h=8),
                in0=xtmu[u].rearrange("p (h x) -> p h x", h=8),
                in1=D_b[:, (ui % 8) * 8:(ui % 8) * 8 + 8][:, :, None]
                .broadcast_to([128, 8, 64]), op=OP.mult),
                [f"xtmu{u}", "D_b"], ["ysd"])
            vec(lambda e, gx=gx, u=u: e.tensor_tensor(
                out=wt[u].rearrange("p (h t) -> p h t", h=8)[:, 0:4, :],
                in0=wd[u].rearrange("p (h t) -> p h t", h=8)[:, 0:4, :],
                in1=Gm[:, gx * 128:(gx + 1) * 128][:, None, :]
                .broadcast_to([128, 4, 128]),
                op=OP.mult),
                [f"wdA{u}", "Gm"], [f"wtA{u}"])
            pool(lambda e, gx=gx, u=u: e.tensor_tensor(
                out=wt[u].rearrange("p (h t) -> p h t", h=8)[:, 4:8, :],
                in0=wd[u].rearrange("p (h t) -> p h t", h=8)[:, 4:8, :],
                in1=Gm[:, gx * 128:(gx + 1) * 128][:, None, :]
                .broadcast_to([128, 4, 128]),
                op=OP.mult),
                [f"wdB{u}", "Gm"], [f"wtB{u}"])
            pool(lambda e, gx=gx, qp=qp: e.tensor_tensor(
                out=sdec.rearrange("p (h x) -> p h x", h=8),
                in0=Sst[:, gx * 512:(gx + 1) * 512]
                .rearrange("p (h x) -> p h x", h=8),
                in1=cumAq2[qp][:, gx * 8:(gx + 1) * 8][:, :, None]
                .broadcast_to([128, 8, 64]), op=OP.mult),
                [f"Sg{gx}", f"cumAq2_{qp}"], ["sdec"])
            for hl8 in range(8):
                h = gx * 8 + hl8
                mm(PF[2][:, ib * 512 + hl8 * 64:
                   ib * 512 + (hl8 + 1) * 64],
                   cq2x[qp][:, gx * 128:(gx + 1) * 128],
                   S_bf[:, h * 64:(h + 1) * 64],
                   [f"cq2x{qp}", f"Sbf{gx}"], f"PS{4 + ib}")
            for hl8 in range(8):
                mm(PF[a2][:, 512 + hl8 * 64:512 + (hl8 + 1) * 64],
                   btm2[:, gx * 128:(gx + 1) * 128],
                   xw[u][:, hl8 * 64:(hl8 + 1) * 64],
                   ["btm2", f"xw{u}"], aks[1])
            for hl8 in range(8):
                wk = f"wtA{u}" if hl8 < 4 else f"wtB{u}"
                mm(PF[a2][:, hl8 * 64:(hl8 + 1) * 64],
                   wt[u][:, hl8 * 128:(hl8 + 1) * 128],
                   xtmu[u][:, hl8 * 64:(hl8 + 1) * 64],
                   [wk, f"xtmu{u}"], aks[0])

        def b_asm(ui):
            q, gx = ui // 8, ui % 8
            u = ui % 2
            a2 = ui % 2
            ib = ui % 2
            qp = q % 2
            aks = [f"PS{2 * a2}", f"PS{2 * a2 + 1}"]
            dma("sync", lambda e, q=q, gx=gx, u=u:
                e.dma_start(out=ggt[u],
                            in_=gate_s[q * QC:(q + 1) * QC,
                                       gx * 512:(gx + 1) * 512]),
                [f"gated{2 * gx}_{q}", f"gated{2 * gx + 1}_{q}"],
                [f"ggt{u}"])
            q3 = q % 3
            vec(lambda e, gx=gx, u=u, ib=ib, q3=q3: e.tensor_tensor(
                out=ysb[u].rearrange("p (h x) -> p h x", h=8),
                in0=PF[2][:, ib * 512:ib * 512 + 512]
                .rearrange("p (h x) -> p h x", h=8),
                in1=eltq[q3][:, gx * 8:(gx + 1) * 8][:, :, None]
                .broadcast_to([128, 8, 64]), op=OP.mult),
                [f"PS{4 + ib}", f"eltq{q3}"], [f"ysb{u}"])
            vec(lambda e, u=u, a_=a2: e.tensor_tensor(
                out=ysb[u], in0=ysb[u], in1=PF[a_][:, 0:512],
                op=OP.add), [f"ysb{u}", aks[0]], [f"ysb{u}"])
            pool(lambda e, u=u: e.tensor_tensor(
                out=ysb[u], in0=ysb[u], in1=ysd,
                op=OP.add), [f"ysb{u}", "ysd"], [f"ysb{u}"])
            vec(lambda e, gx=gx, a_=a2: e.tensor_tensor(
                out=Sst[:, gx * 512:(gx + 1) * 512],
                in0=sdec, in1=PF[a_][:, 512:1024],
                op=OP.add),
                ["sdec", aks[1]], [f"Sg{gx}"])

        def b_fin(ui):
            q, gx = ui // 8, ui % 8
            u = ui % 2
            scl(lambda e, u=u: e.activation(out=sqscr, in_=ysb[u],
                                            func=AF.Square,
                                            accum_out=ssq[u]),
                [f"ysb{u}"], ["sqx", f"ssq{u}"])
            vec(lambda e, u=u: e.tensor_scalar(
                out=rstd[u], in0=ssq[u], scalar1=1.0 / 512.0,
                scalar2=EPS, op0=OP.mult, op1=OP.add),
                [f"ssq{u}"], [f"rstd{u}"])
            scl(lambda e, u=u: e.activation(out=rstd[u], in_=rstd[u],
                                            func=AF.Ln),
                [f"rstd{u}"], [f"rstd{u}"])
            scl(lambda e, u=u: e.activation(out=rstd[u], in_=rstd[u],
                                            func=AF.Exp, scale=-0.5),
                [f"rstd{u}"], [f"rstd{u}"])
            vec(lambda e, u=u: e.scalar_tensor_tensor(
                out=yng, in0=ysb[u], scalar=rstd[u][:, 0:1],
                in1=ggt[u], op0=OP.mult, op1=OP.mult),
                [f"ysb{u}", f"rstd{u}", f"ggt{u}"], ["yng"])
            dma("gpsimd", lambda e, gx=gx: e.dma_start(
                out=S_bf[:, gx * 512:(gx + 1) * 512],
                in_=Sst[:, gx * 512:(gx + 1) * 512]),
                [f"Sg{gx}"], [f"Sbf{gx}"])
            for c4 in range(4):
                tr(PBA[1][:, 512 + c4 * 128:512 + (c4 + 1) * 128],
                   yng[:, c4 * 128:(c4 + 1) * 128],
                   ident_bf, ["yng", "idb"], "PBT1")
            vec(lambda e, gx=gx, q=q: e.tensor_copy(
                panel.rearrange("p (c t) -> p c t", c=NKC)
                [:, gx * 4:gx * 4 + 4, q * QC:(q + 1) * QC],
                PBA[1][:, 512:1024]
                .rearrange("p (c t) -> p c t", c=4)),
                ["PBT1"],
                [f"pan{gx * 4 + i}" for i in range(4)])

        for b in range(2):
            vec(lambda e, b=b: e.memset(NRL15[b], 1.0), [],
                [f"NRq2x{b}"])
            for hh in range(2):
                dma("gpsimd", lambda e, b=b, hh=hh:
                    e.dma_start(out=ILx[b][0:12, hh * 512:(hh + 1) * 512],
                                in_=ind12),
                    [f"Lr32x{b}"], [f"Lr32x{b}"])
        load_qgrp(0)
        load_nrl(0)
        load_nr2(0)
        overlay(bq2x[0], range(32, 40), "bq2x0")
        overlay(cq2x[0], range(40, 48), "cq2x0")

        for k in range(NU + 3):
            if 2 <= k <= NU + 1:
                b_asm(k - 2)
            if k < NU:
                b_front_d(k)
                b_scal(k)
            if 1 <= k <= NU:
                b_cde(k - 1)
            if k < NU:
                b_front(k)
            if 3 <= k <= NU + 2:
                b_fin(k - 3)
        A.pop_fence()

        if dbg:
            dbg1 = nc.dram_tensor("dbg1", [128, NKC * T], BF16).ap()
            for kc in range(NKC):
                dma("sync", lambda e, k=kc: e.dma_start(
                    out=dbg1[:, k * T:(k + 1) * T],
                    in_=panel[:, k * T:(k + 1) * T]),
                    [f"pan{kc}"], [f"dbg1_{kc}"])

        A.push()
        wsl3 = [A.bf16(128, NKC * 128, f"wsl3{i}") for i in range(2)]
        xr3 = [A.f32(128, T, f"xr3{i}") for i in range(2)]
        x2b = [A.f32(128, T, f"x2b{i}") for i in range(2)]
        gi3 = 0
        for oc in range(NKC):
            wb3 = oc % 2
            dma("gpsimd", lambda e, o=oc, w=wb3:
                e.dma_start(out=wsl3[w],
                            in_=w_out[o].rearrange("p kc m -> p (kc m)")),
                [], [f"wsl3{wb3}"])
            dma("scalq", lambda e, o=oc, w=wb3:
                e.dma_start(out=xr3[w],
                            in_=x_fm[o * 128:(o + 1) * 128, :]),
                [], [f"xr3{wb3}"])
            for it, (t0, tw) in enumerate(TT):
                bank = gi3 % 4
                for kc in range(NKC):
                    mm(PSA[bank][:, 0:tw],
                       wsl3[wb3][:, kc * 128:(kc + 1) * 128],
                       panel[:, kc * T + t0:kc * T + t0 + tw],
                       [f"wsl3{wb3}", f"pan{kc}"], f"PS{bank}",
                       start=(kc == 0), stop=(kc == NKC - 1))
                vec(lambda e, w=wb3, t0=t0, tw=tw, b=bank:
                    e.tensor_tensor(out=x2b[w][:, t0:t0 + tw],
                                    in0=PSA[b][:, 0:tw],
                                    in1=xr3[w][:, t0:t0 + tw], op=OP.add),
                    [f"PS{bank}", f"xr3{wb3}"], [f"x2b{wb3}"])
                gi3 += 1
            dma("sync", lambda e, o=oc, w=wb3:
                e.dma_start(out=x2_s[o * 128:(o + 1) * 128, :],
                            in_=x2b[w]),
                [f"x2b{wb3}"], [f"x2d{oc}"])
        A.pop_fence()

        phase_ln("x2", "L2", router=True)

        A.push()
        wslB = [A.bf16(128, NKC * 128, f"wslB{i}") for i in range(2)]
        blora = A.bf16(128, I, "blorak")
        wrt_sb = A.f32(128, NKC * 8, "wrtk")
        qred = A.f32(128, 64, "qredk")
        tbf = A.bf16(128, T, "tbf")
        xrB = A.f32(128, T, "xrB")
        obuf = A.f32(128, T, "obuf")
        ot1 = A.f32(128, 512, "ot1")
        lgt = A.f32(2, 8, ("lgt", "lgto"))
        mx2 = A.f32(2, 1, "mx2")
        rt8 = A.f32(8, 1, "rt8")
        rteff = A.f32(128, 1, "rteffk")

        dma("gpsimd", lambda e:
            e.dma_start(out=wslB[0],
                        in_=w_la.rearrange("p kc m -> p (kc m)")),
            [], ["wslB0"])
        for it, (t0, tw) in enumerate(TT):
            bank = it % 2
            for kc in range(NKC):
                mm(PSA[bank][:, 0:tw],
                   wslB[0][:, kc * 128:(kc + 1) * 128],
                   panel[:, kc * T + t0:kc * T + t0 + tw],
                   ["wslB0", f"pan{kc}"], f"PS{bank}",
                   start=(kc == 0), stop=(kc == NKC - 1))
            vec(lambda e, t0=t0, tw=tw, b=bank: e.tensor_copy(
                tbf[:, t0:t0 + tw], PSA[b][:, 0:tw]),
                [f"PS{bank}"], ["tbf"])
        dma("gpsimd", lambda e: e.dma_start(out=blora, in_=w_lb),
            [], ["blorak"])
        dma("sync", lambda e: e.dma_start(out=wrt_sb, in_=w_rt),
            [], ["wrtk"])
        dma("sync", lambda e: e.dma_start(out=qred, in_=q_out.ap()),
            ["q_out"], ["qredk"])
        for kc in range(NKC):
            mm(PSA[2][0:2, 0:8],
               qred[:, kc:64:32], wrt_sb[:, kc * 8:(kc + 1) * 8],
               ["qredk", "wrtk"], "PS2",
               start=(kc == 0), stop=False)
        mm(PSA[2][0:2, 0:8], ones1f[:, 0:2], rb_sb,
           ["o1f", "rbk"], "PS2", start=False, stop=True)
        vec(lambda e: e.tensor_copy(lgt, PSA[2][0:2, 0:8]), ["PS2"], ["lgt"])
        vec(lambda e: e.reduce_max(mx2, lgt, axis=AX.X), ["lgt"], ["mx2"])
        vec(lambda e: e.tensor_scalar(out=lgt, in0=lgt,
                                      scalar1=mx2[:, 0:1], scalar2=None,
                                      op0=OP.subtract), ["lgt", "mx2"], ["lgt"])
        scl(lambda e: e.activation(out=lgt, in_=lgt, func=AF.Exp),
            ["lgt"], ["lgt"])
        vec(lambda e: e.reduce_sum(mx2, lgt, axis=AX.X), ["lgt"], ["mx2"])
        vec(lambda e: e.reciprocal(mx2, mx2), ["mx2"], ["mx2"])
        vec(lambda e: e.tensor_scalar(out=lgt, in0=lgt,
                                      scalar1=mx2[:, 0:1], scalar2=None,
                                      op0=OP.mult), ["lgt", "mx2"], ["lgt"])
        mm(PSA[3][0:1, 0:8], bm_sb, lgt, ["bm", "lgt"], "PS3")
        vec(lambda e: e.tensor_copy(lgt[0:1, 0:8], PSA[3][0:1, 0:8]),
            ["PS3"], ["lgto"])
        tr(PSA[2][0:8, 0:1], lgt[0:1, 0:8], ident_f[0:1, 0:1],
           ["lgto", "idf"], "PS2")
        vec(lambda e: e.tensor_copy(rt8, PSA[2][0:8, 0:1]), ["PS2"], ["rt8"])
        mm(PSA[3][:, 0:1], e816_sb, rt8, ["e816", "rt8"], "PS3")
        vec(lambda e: e.tensor_copy(rteff, PSA[3][:, 0:1]), ["PS3"], ["rteffk"])

        blsc = [A.bf16(128, 128, f"blsc{i}") for i in range(2)]
        giB = 0
        for oc in range(NKC):
            wbB = oc % 2
            vec(lambda e, o=oc, w=wbB: e.tensor_scalar(
                out=blsc[w], in0=blora[:, o * 128:(o + 1) * 128],
                scalar1=rteff[:, 0:1], scalar2=None, op0=OP.mult),
                ["blorak", "rteffk"], [f"blsc{wbB}"])
            dma("gpsimd", lambda e, o=oc, w=wbB:
                e.dma_start(out=wslB[w],
                            in_=w_ffn[o].rearrange("p kc m -> p (kc m)")),
                [], [f"wslB{wbB}"])
            dma("scalq", lambda e, o=oc:
                e.dma_start(out=xrB, in_=x2_s[o * 128:(o + 1) * 128, :]),
                [f"x2d{oc}"], ["xrB"])
            for it, (t0, tw) in enumerate(TT):
                bank = giB % 4
                for kc in range(NKC):
                    mm(PSA[bank][:, 0:tw],
                       wslB[wbB][:, kc * 128:(kc + 1) * 128],
                       panel[:, kc * T + t0:kc * T + t0 + tw],
                       [f"wslB{wbB}", f"pan{kc}"], f"PS{bank}",
                       start=(kc == 0), stop=False)
                mm(PSA[bank][:, 0:tw],
                   blsc[wbB],
                   tbf[:, t0:t0 + tw],
                   [f"blsc{wbB}", "tbf"], f"PS{bank}", start=False, stop=True)
                scl(lambda e, t0=t0, tw=tw, b=bank, o=oc:
                    e.activation(out=ot1[:, 0:tw], in_=PSA[b][:, 0:tw],
                                 func=AF.Identity, bias=fb_sb[:, o:o + 1]),
                    [f"PS{bank}", "fb"], ["ot1"])
                vec(lambda e, t0=t0, tw=tw:
                    e.tensor_tensor(out=obuf[:, t0:t0 + tw],
                                    in0=ot1[:, 0:tw],
                                    in1=xrB[:, t0:t0 + tw], op=OP.add),
                    ["ot1", "xrB"], ["obuf"])
                giB += 1
            dma("sync", lambda e, o=oc:
                e.dma_start(out=out_fm[o * 128:(o + 1) * 128, :],
                            in_=obuf),
                ["obuf"], [f"outd{oc}"])
        A.pop_fence()

        S_.add("gpsimd", lambda e: e.memset(pxbc[0:1, 0:1], 0.0),
               reads=[f"outd{o}" for o in range(NKC)], writes=[],
               sem="POO", amt=1)

        nops, cnts = S_.finalize(block, sems)
    return nc, nops


def _prep_weights(inputs, T):
    ip = np.asarray(inputs["L0_attn_in_proj_w"])
    W_inT = np.ascontiguousarray(ip.T.astype(np.float32))
    slabs = np.zeros((49, 128, NKC, 128), np.float32)
    dtslab = np.zeros((D, 128), np.float32)
    dtslab[:, 0:H] = W_inT[:, I + CONV:I + CONV + H]
    slabs[0] = dtslab.reshape(NKC, 128, 128).transpose(1, 0, 2)
    for c in range(NHBC):
        blk = W_inT[:, I + c * 128:I + (c + 1) * 128]
        slabs[c + 1] = blk.reshape(NKC, 128, 128).transpose(1, 0, 2)
    w_hbcdt = np.ascontiguousarray(slabs)

    w_gate = np.ascontiguousarray(W_inT[:, 0:I].reshape(NKC, 128, I))

    op_ = np.asarray(inputs["L0_attn_out_proj_w"])
    nwv = np.asarray(inputs["L0_attn_norm_weight"]).astype(np.float32)
    OT = np.ascontiguousarray(op_.T.astype(np.float32)
                              * nwv[:, None])
    w_out = np.zeros((NKC, 128, NKC, 128), np.float32)
    for o in range(NKC):
        blk = OT[:, o * 128:(o + 1) * 128]
        w_out[o] = blk.reshape(NKC, 128, 128).transpose(1, 0, 2)

    Wf = np.asarray(inputs["L0_ffn_W"])
    FT = np.ascontiguousarray(Wf.T.astype(np.float32))
    w_ffn = np.zeros((NKC, 128, NKC, 128), np.float32)
    for o in range(NKC):
        blk = FT[:, o * 128:(o + 1) * 128]
        w_ffn[o] = blk.reshape(NKC, 128, 128).transpose(1, 0, 2)

    Am = np.asarray(inputs["L0_ffn_A"]).reshape(128, D)
    AT = Am.T
    w_la = np.ascontiguousarray(
        AT.reshape(NKC, 128, 128).transpose(1, 0, 2))
    Bm = np.asarray(inputs["L0_ffn_B"])
    w_lb = np.ascontiguousarray(
        Bm.transpose(0, 2, 1).reshape(128, I))
    rw = np.asarray(inputs["L0_ffn_router_W"])
    w_rt = np.ascontiguousarray(
        rw.T.reshape(NKC, 128, 8).transpose(1, 0, 2).reshape(128, NKC * 8))

    cw = np.asarray(inputs["L0_attn_conv_weight"])
    diag = np.zeros((NHBC, KCV, 128, 128), np.float32)
    idx = np.arange(128)
    for c in range(NHBC):
        for k in range(KCV):
            diag[c, k, idx, idx] = cw[c * 128:(c + 1) * 128, k]
    cwc = np.ascontiguousarray(
        cw.reshape(NHBC, 128, KCV).transpose(1, 0, 2).reshape(128,
                                                              NHBC * KCV))
    cb = np.asarray(inputs["L0_attn_conv_bias"])
    cbias = np.ascontiguousarray(cb.reshape(NHBC, 128).T)

    nw = np.asarray(inputs["L0_attn_norm_weight"])
    nwc = np.ascontiguousarray(nw.reshape(NKC, 128).T)
    fbv = np.asarray(inputs["L0_ffn_b"])
    fbc = np.ascontiguousarray(fbv.reshape(NKC, 128).T)

    return dict(
        w_hbcdt=w_hbcdt, w_gate=w_gate, w_out=w_out, w_ffn=w_ffn,
        w_la=w_la, w_lb=w_lb, w_rt=w_rt, diagw=diag, cwcols=cwc,
        cbias=cbias, nw_cols=nwc, fb_cols=fbc,
        dtb=np.asarray(inputs["L0_attn_dt_bias"]).reshape(H, 1),
        alog=np.asarray(inputs["L0_attn_A_log"]).reshape(H, 1),
        dvec_row=np.asarray(inputs["L0_attn_D"]).reshape(1, H),
        rb_row=np.asarray(inputs["L0_ffn_router_b"]).reshape(1, 8),
    )


def _consts(T, qcount):
    identity = np.eye(128, dtype=np.float32)
    tri = np.triu(np.full((128, 128), np.exp(30.0), np.float32))
    ind12 = np.zeros((12, 512), np.float32)
    for blk in range(3):
        for k in range(4):
            ind12[blk * 4 + k, k * 128:(k + 1) * 128] = 1.0
    e816m = np.zeros((8, 128), np.float32)
    for e in range(8):
        e816m[e, e * 16:(e + 1) * 16] = SCALING
    return identity, tri, ind12, e816m


def kernel(**inputs):
    x = np.asarray(inputs["L0_attn_hidden_states"])
    Bb, L, _ = x.shape
    T = (Bb * L) // NCORES
    QS, QE = 611, L - 1
    qcount = QE - QS

    shared = _prep_weights(inputs, T)
    identity, tri, ind12, e816m = _consts(T, qcount)
    shared.update(identity=identity, tri01=tri, ind12=ind12, e816=e816m)

    nc, nops = build_program(T)

    in_maps = []
    for c in range(NCORES):
        b = 0 if c < NCORES // 2 else 1
        ci = c % (NCORES // 2)
        t0, t1 = ci * T, (ci + 1) * T
        m = dict(shared)
        m["x_fm"] = np.ascontiguousarray(x[b, t0:t1, :].T)
        tm = np.zeros((1, T), np.float32)
        lo, hi = max(QS, t0), min(QE, t1)
        if hi > lo:
            tm[0, lo - t0:hi - t0] = 1.0 / qcount
        m["tokmask"] = tm
        chmv = np.zeros((H, 64), np.float32)
        chm1v = np.zeros((H, 64), np.float32)
        lo_r, hi_r = (0, 4) if b == 0 else (4, 8)
        for r in range(8):
            contributes = (lo_r <= r < hi_r) and (r < c)
            if contributes:
                for i in range(8):
                    mid = (r < i < c)
                    chmv[:, r * 8 + i] = 1.0 if mid else 0.0
                    chm1v[:, r * 8 + i] = 0.0 if mid else 1.0
        m["chmh"] = chmv
        m["chm1mh"] = chm1v
        bc = np.zeros((4, 8), np.float32)
        if ci > 0:
            bc[:, c - 1] = 1.0
        m["bndc"] = bc
        bmv = np.zeros((2, 1), np.float32)
        bmv[b, 0] = 1.0
        m["bmask"] = bmv
        bs = np.zeros((128, 64), np.float32)
        bs[:, b * 32:(b + 1) * 32] = 1.0
        m["bsel"] = bs
        in_maps.append(m)

    res = run_bass_kernel_spmd(nc, in_maps, core_ids=list(range(NCORES)))
    out = np.empty((Bb, L, D), np.float32)
    for c in range(NCORES):
        b = 0 if c < NCORES // 2 else 1
        ci = c % (NCORES // 2)
        out[b, ci * T:(ci + 1) * T, :] = res.results[c]["out_fm"].T
    return out


# version 18
# speedup vs baseline: 1.0544x; 1.0150x over previous
import numpy as np
import ml_dtypes
import concourse.bass as bass
import concourse.mybir as mybir
from concourse.bass_utils import run_bass_kernel_spmd

F32 = mybir.dt.float32
BF16 = mybir.dt.bfloat16
AF = mybir.ActivationFunctionType
OP = mybir.AluOpType
AX = mybir.AxisListType
BF = ml_dtypes.bfloat16

H, P, N, G, KCV = 64, 64, 128, 8, 4
D = 4096
I = 4096
CONV = 6144
QC = 128
EPS = 1e-5
NKC = 32
NHBC = 48
NCORES = 8
NHB = 16
SCALING = 2.0

STREAM_OF = {"sync": "sync", "scalar": "scalar", "vector": "vector",
             "tensor": "tensor", "gpsimd": "gpsimd", "scalq": "scalar"}
ENG_SEM = {"sync": "LDS", "scalq": "LDA", "tensor": "TEN", "vector": "VEC",
           "scalar": "SCL", "gpsimd": "POO"}
DMA_SEMS = ("LDS", "LDA", "LDG")


class Sched:
    def __init__(self):
        self.ops = []
        self.last_write = {}
        self.readers = {}

    def add(self, eng, fn, reads=(), writes=(), sem=None, amt=None):
        idx = len(self.ops)
        deps = set()
        for r in reads:
            w = self.last_write.get(r)
            if w is not None:
                deps.add(w)
        for wres in writes:
            w = self.last_write.get(wres)
            if w is not None:
                deps.add(w)
            for rd in self.readers.get(wres, ()):
                deps.add(rd)
        if sem is None:
            sem = ENG_SEM[eng]
        if amt is None:
            amt = 16 if sem in DMA_SEMS else 1
        self.ops.append({"i": idx, "eng": eng, "fn": fn, "deps": deps,
                         "sem": sem, "amt": amt, "waited": False})
        for wres in writes:
            self.last_write[wres] = idx
            self.readers[wres] = []
        for r in reads:
            self.readers.setdefault(r, []).append(idx)
        return idx

    def finalize(self, block, sems):
        ops = self.ops
        NL = 8
        lane_cnt = {}
        lane_prev = {}
        for op in ops:
            if op["sem"] in DMA_SEMS:
                q = op["sem"]
                i = lane_cnt.get(q, 0)
                lane_cnt[q] = i + 1
                lane = f"{q}{i % NL}"
                op["sem"] = lane
                prev = lane_prev.get(lane)
                if prev is not None:
                    op["lane_dep"] = prev
                lane_prev[lane] = op["i"]
        for op in ops:
            pruned = set()
            for d in op["deps"]:
                dop = ops[d]
                if dop["eng"] == "tensor" and op["eng"] == "tensor":
                    continue
                pruned.add(d)
            op["deps"] = pruned
            if "lane_dep" in op:
                op["deps"].add(op["lane_dep"])
        for op in ops:
            for d in op["deps"]:
                ops[d]["waited"] = True
            if op["sem"][:3] in DMA_SEMS or op["sem"].startswith("CC"):
                op["waited"] = True
        cnt = {}
        for op in ops:
            if op["waited"]:
                cnt[op["sem"]] = cnt.get(op["sem"], 0) + op["amt"]
                op["val"] = cnt[op["sem"]]
            else:
                op["val"] = None
        streams = {e: [] for e in ["sync", "scalar", "vector", "tensor",
                                   "gpsimd"]}
        for op in ops:
            streams[STREAM_OF[op["eng"]]].append(op)
        for sname, sops in streams.items():
            hw = {}
            for op in sops:
                byv = {}
                for d in op["deps"]:
                    dop = ops[d]
                    if dop["val"] is None:
                        continue
                    s, v = dop["sem"], dop["val"]
                    if hw.get(s, 0) >= v:
                        continue
                    byv[s] = max(byv.get(s, 0), v)
                op["waits"] = sorted(byv.items())
                for s, v in byv.items():
                    hw[s] = v

        def make(sops):
            def body(e):
                for op in sops:
                    for s, v in op["waits"]:
                        e.wait_ge(sems[s], v)
                    inst = op["fn"](e)
                    if op["waited"]:
                        inst.then_inc(sems[op["sem"]], op["amt"])
            return body
        for sname in ["sync", "scalar", "vector", "tensor", "gpsimd"]:
            getattr(block, sname)(make(streams[sname]))
        return len(ops), cnt


class Arena:

    def __init__(self, sched, tensor_ap, width_f32):
        self.S = sched
        self.t = tensor_ap
        self.tb = tensor_ap.bitcast(BF16)
        self.w = width_f32
        self.off = 0
        self.marks = []
        self.used = set()
        self.fence_idx = None

    def push(self):
        self.marks.append(self.off)

    def f32(self, parts, cols, key):
        assert self.off + cols <= self.w, f"arena overflow {self.off}+{cols}"
        v = self.t[0:parts, self.off:self.off + cols]
        self.off += cols
        self._reg(key)
        return v

    def bf16(self, parts, cols, key):
        colsf = (cols + 1) // 2
        assert self.off + colsf <= self.w, f"arena overflow {self.off}+{colsf}"
        v = self.tb[0:parts, 2 * self.off:2 * self.off + cols]
        self.off += colsf
        self._reg(key)
        return v

    def pairf32(self, cols, key_top, key_bot):
        assert self.off + cols <= self.w, "arena overflow (pair)"
        top = self.t[0:64, self.off:self.off + cols]
        bot = self.t[64:128, self.off:self.off + cols]
        self.off += cols
        self._reg(key_top)
        self._reg(key_bot)
        return top, bot

    def _reg(self, key):
        keys = key if isinstance(key, (tuple, list)) else (key,)
        for k in keys:
            self.used.add(k)
            if self.fence_idx is not None and k not in self.S.last_write:
                self.S.last_write[k] = self.fence_idx

    def pop_fence(self):
        self.off = self.marks.pop()
        used = sorted(self.used)
        idx = self.S.add("vector", lambda e: e.engine_nop(), reads=(),
                         writes=used)
        self.used = set()
        self.fence_idx = idx


def _ttiles(T):
    out, t = [], 0
    while t < T:
        w = min(512, T - t)
        out.append((t, w))
        t += w
    return out


def build_program(T, dbg=False):
    NQ = T // QC
    TT = _ttiles(T)
    T16 = T // 128

    nc = bass.Bass("TRN2", target_bir_lowering=False)

    def inp(name, shape, dt=F32):
        return nc.dram_tensor(name, shape, dt, kind="ExternalInput").ap()

    x_fm = inp("x_fm", [D, T])
    w_hbcdt = inp("w_hbcdt", [49, 128, NKC, 128])
    w_gate = inp("w_gate", [NKC, 128, I])
    w_out = inp("w_out", [NKC, 128, NKC, 128])
    w_ffn = inp("w_ffn", [NKC, 128, NKC, 128])
    w_la = inp("w_la", [128, NKC, 128])
    w_lb = inp("w_lb", [128, I])
    w_rt = inp("w_rt", [128, NKC * 8])
    diagw = inp("diagw", [NHBC, KCV, 128, 128])
    cwcols = inp("cwcols", [128, NHBC * KCV])
    cbias = inp("cbias", [128, NHBC])
    dtb = inp("dtb", [H, 1])
    alog = inp("alog", [H, 1])
    dvec_row = inp("dvec_row", [1, H])
    nw_cols = inp("nw_cols", [128, NKC])
    fb_cols = inp("fb_cols", [128, NKC])
    rb_row = inp("rb_row", [1, 8])
    identity = inp("identity", [128, 128])
    tri01 = inp("tri01", [128, 128])
    ind12 = inp("ind12", [12, 512])
    e816 = inp("e816", [8, 128])
    tokmask = inp("tokmask", [1, T])
    chmh = inp("chmh", [H, 64])
    chm1mh = inp("chm1mh", [H, 64])
    bndc = inp("bndc", [4, 8])
    bmask = inp("bmask", [2, 1])
    bsel_i = inp("bsel", [128, 64])

    out_fm = nc.dram_tensor("out_fm", [D, T], F32, kind="ExternalOutput").ap()

    xbc_s = nc.dram_tensor("xbc_s", [CONV, T], BF16).ap()
    gate_s = nc.dram_tensor("gate_s", [T, I], BF16).ap()
    xtm_s = nc.dram_tensor("xtm_s", [NQ, 128, I], BF16).ap()
    x2_s = nc.dram_tensor("x2_s", [D, T], F32).ap()
    lsp3 = nc.dram_tensor("lsp3_s", [3, H, T], BF16).ap()
    nrsp3 = nc.dram_tensor("nrsp3_s", [3, H, T], BF16).ap()
    lsp_s = [lsp3[i] for i in range(3)]
    nrsp_s = [nrsp3[i] for i in range(3)]
    elend_s = nc.dram_tensor("elend_s", [NQ, H], F32).ap()
    elt_s = nc.dram_tensor("elt_s", [NQ, 128, 64], BF16)
    coefb_s = nc.dram_tensor("coefb_s", [8, H], F32).ap()
    bnd_in = nc.dram_tensor("bnd_in", [4, CONV], BF16)
    bnd_out = nc.dram_tensor("bnd_out", [32, CONV], BF16)
    st_in = nc.dram_tensor("st_in", [128, H * P], BF16)
    st_out = nc.dram_tensor("st_out", [8 * 128, H * P], BF16)
    ca_in = nc.dram_tensor("ca_in", [H, 1], F32)
    ca_out = nc.dram_tensor("ca_out", [8 * H, 1], F32)
    q_in = nc.dram_tensor("q_in", [128, 64], F32)
    q_out = nc.dram_tensor("q_out", [128, 64], F32)

    from contextlib import ExitStack
    with ExitStack() as g:
        def sbt(name, shape, dt=F32):
            return g.enter_context(nc.sbuf_tensor(name, shape, dt))[:]

        panel = sbt("panel", [128, NKC * T], BF16)
        AR_W = 12544
        S_ = Sched()
        A = Arena(S_, sbt("arena", [128, AR_W], F32), AR_W)

        ident_bf = sbt("ident_bf", [128, 128], BF16)
        ident_f = sbt("ident_f", [128, 128], F32)
        tri_sb = sbt("tri_sb", [128, 128], F32)
        ind12_sb = sbt("ind12_sb", [12, 512], BF16)
        ones1f = sbt("ones1f", [1, 128], F32)
        ones1b = sbt("ones1b", [1, 128], BF16)
        ones3b = sbt("ones3b", [3, 128], BF16)
        onescol = sbt("onescol", [128, 1], F32)
        onescol_b = sbt("onescol_b", [128, 1], BF16)
        ones64x = sbt("ones64x", [H, QC], F32)
        cb_sb = sbt("cb_sb", [128, NHBC], F32)
        cw_sb = sbt("cw_sb", [128, NHBC * KCV], F32)
        dtb_sb = sbt("dtb_sb", [H, 1], F32)
        negA = sbt("negA", [H, 1], F32)
        nw_sb = sbt("nw_sb", [128, NKC], F32)
        fb_sb = sbt("fb_sb", [128, NKC], F32)
        dvrow_sb = sbt("dvrow_sb", [1, H], F32)
        rb_sb = sbt("rb_sb", [1, 8], F32)
        e816_sb = sbt("e816_sb", [8, 128], F32)
        bm_sb = sbt("bm_sb", [2, 1], F32)
        Sst = sbt("Sst", [128, H * P], F32)
        S_bf = sbt("S_bf", [128, H * P], BF16)
        D_b = sbt("D_b", [128, H], F32)
        pf_sb = sbt("pf_sb", [128, NHBC * 3], BF16)
        pxbc = sbt("pxbc", [128, NHBC * 3], BF16)
        qacc = sbt("qacc", [128, NKC], F32)

        PF = [g.enter_context(nc.psum_tensor(f"PF{i}", [128, 1024], F32))[:]
              for i in range(3)]
        PSA = [PF[i // 2][:, (i % 2) * 512:(i % 2) * 512 + 512]
               for i in range(6)]
        PBA = [g.enter_context(nc.psum_tensor(f"PBT{i}", [128, 1024],
                                              BF16))[:]
               for i in range(2)]

        sems = {}
        semnames = ["TEN", "VEC", "SCL", "POO", "CC1", "CC2a", "CC2b", "CC3"]
        for q in ["LDS", "LDA", "LDG"]:
            semnames += [f"{q}{i}" for i in range(8)]
        for s in semnames:
            sems[s] = g.enter_context(nc.semaphore(s))

        block = g.enter_context(nc.Block())

        def mm(out_ap, lhsT, rhs, reads, wkey, start=True, stop=True):
            S_.add("tensor",
                   lambda e, o=out_ap, l=lhsT, r=rhs, st=start, sp=stop:
                   e.matmul(o, l, r, start=st, stop=sp),
                   reads=reads, writes=[wkey])

        def tr(out_ap, in_ap, ident, reads, wkey):
            S_.add("tensor",
                   lambda e, o=out_ap, i_=in_ap, d=ident:
                   e.transpose(o, i_, d),
                   reads=reads, writes=[wkey])

        def vec(fn, reads, writes):
            S_.add("vector", fn, reads=reads, writes=writes)

        def pool(fn, reads, writes):
            S_.add("gpsimd", fn, reads=reads, writes=writes, sem="POO",
                   amt=1)

        def scl(fn, reads, writes):
            S_.add("scalar", fn, reads=reads, writes=writes)

        def dma(q, fn, reads, writes):
            sm = {"sync": "LDS", "scalq": "LDA", "gpsimd": "LDG"}[q]
            S_.add(q, fn, reads=reads, writes=writes, sem=sm, amt=16)

        def coll(name_sem, kind, op_, tin, tout, reads, writes):
            S_.add("gpsimd", lambda e, k=kind, o=op_, i_=tin, u=tout:
                   e.collective_compute(
                       k, o, replica_groups=[list(range(NCORES))],
                       ins=[i_.ap().opt()], outs=[u.ap().opt()]),
                   reads=reads, writes=writes, sem=name_sem, amt=1)

        ALLPAN = [f"pan{k}" for k in range(NKC)]
        ALLSG = [f"Sg{gg}" for gg in range(8)]

        for dst, src, rk in [(tri_sb, tri01, "tri"), (ident_f, identity, "idf"),
                             (cb_sb, cbias, "cb"), (cw_sb, cwcols, "cw"),
                             (dtb_sb, dtb, "dtbk"), (nw_sb, nw_cols, "nw"),
                             (fb_sb, fb_cols, "fb"),
                             (dvrow_sb, dvec_row, "dvr"),
                             (rb_sb, rb_row, "rbk"), (e816_sb, e816, "e816"),
                             (bm_sb, bmask, "bm"), (negA, alog, "negA")]:
            dma("sync", lambda e, d=dst, s=src: e.dma_start(out=d, in_=s),
                [], [rk])
        for dst, src, rk in [(ident_bf, identity, "idb"),
                             (ind12_sb, ind12, "ind12")]:
            dma("gpsimd", lambda e, d=dst, s=src: e.dma_start(out=d, in_=s),
                [], [rk])
        for tile, rk in [(ones1f, "o1f"), (ones1b, "o1b"), (ones3b, "o3b"),
                         (onescol, "oc"), (ones64x, "o64"),
                         (onescol_b, "ocb")]:
            vec(lambda e, t=tile: e.memset(t, 1.0), [], [rk])
        vec(lambda e: e.memset(Sst, 0.0), [], ALLSG)
        scl(lambda e: e.activation(out=negA, in_=negA, func=AF.Exp),
            ["negA"], ["negA"])
        vec(lambda e: e.tensor_scalar(out=negA, in0=negA, scalar1=-1.0,
                                      scalar2=None, op0=OP.mult),
            ["negA"], ["negA"])
        mm(PSA[4][:, 0:H], ones1f, dvrow_sb, ["o1f", "dvr"], "PS4")
        vec(lambda e: e.tensor_copy(D_b, PSA[4][:, 0:H]), ["PS4"], ["D_b"])

        def phase_ln(src, ph, router=False):
            A.push()
            a_b = A.bf16(128, T, f"ab{ph}")
            b_b = A.bf16(128, T, f"bb{ph}")
            xsq = [A.bf16(128, 512, f"xsq{i}{ph}") for i in range(2)]
            tmp_ln = [A.bf16(128, T, f"tmpln{i}{ph}") for i in range(2)]
            arow = A.f32(1, T, f"arow{ph}")
            brow = A.f32(1, T, f"brow{ph}")
            moff = A.off
            mrow = A.f32(1, T, f"mrow{ph}")
            for kc in range(NKC):
                if src == "x":
                    dma("gpsimd", lambda e, k=kc:
                        e.dma_start(out=panel[:, k * T:(k + 1) * T],
                                    in_=x_fm[k * 128:(k + 1) * 128, :]),
                        [], [f"pan{kc}"])
                else:
                    dma("gpsimd", lambda e, k=kc:
                        e.dma_start(out=panel[:, k * T:(k + 1) * T],
                                    in_=x2_s[k * 128:(k + 1) * 128, :]),
                        [f"x2d{kc}"], [f"pan{kc}"])
            for it, (t0, tw) in enumerate(TT):
                for kc in range(NKC):
                    ps = panel[:, kc * T + t0:kc * T + t0 + tw]
                    xb = kc % 2
                    scl(lambda e, o=xsq[xb][:, 0:tw], i_=ps:
                        e.activation(out=o, in_=i_, func=AF.Square),
                        [f"pan{kc}"], [f"xsq{xb}{ph}"])
                    mm(PSA[0][0:1, 0:tw], onescol_b, ps,
                       ["ocb", f"pan{kc}"], "PS0",
                       start=(kc == 0), stop=(kc == NKC - 1))
                    mm(PSA[1][0:1, 0:tw], onescol_b, xsq[xb][:, 0:tw],
                       ["ocb", f"xsq{xb}{ph}"], "PS1",
                       start=(kc == 0), stop=(kc == NKC - 1))
                vec(lambda e, t0=t0, tw=tw:
                    e.tensor_copy(mrow[:, t0:t0 + tw], PSA[0][0:1, 0:tw]),
                    ["PS0"], [f"mrow{ph}"])
                vec(lambda e, t0=t0, tw=tw:
                    e.tensor_copy(arow[:, t0:t0 + tw], PSA[1][0:1, 0:tw]),
                    ["PS1"], [f"arow{ph}"])
            vec(lambda e: e.tensor_scalar(out=mrow, in0=mrow,
                                          scalar1=1.0 / D, scalar2=None,
                                          op0=OP.mult),
                [f"mrow{ph}"], [f"mrow{ph}"])
            vec(lambda e: e.tensor_scalar(out=arow, in0=arow,
                                          scalar1=1.0 / D, scalar2=None,
                                          op0=OP.mult),
                [f"arow{ph}"], [f"arow{ph}"])
            vec(lambda e: e.scalar_tensor_tensor(out=brow, in0=mrow,
                                                 scalar=-1.0, in1=mrow,
                                                 op0=OP.mult, op1=OP.mult),
                [f"mrow{ph}"], [f"brow{ph}"])
            vec(lambda e: e.tensor_tensor(out=arow, in0=arow, in1=brow,
                                          op=OP.add),
                [f"arow{ph}", f"brow{ph}"], [f"arow{ph}"])
            vec(lambda e: e.tensor_scalar(out=arow, in0=arow, scalar1=EPS,
                                          scalar2=None, op0=OP.add),
                [f"arow{ph}"], [f"arow{ph}"])
            scl(lambda e: e.activation(out=arow, in_=arow, func=AF.Ln),
                [f"arow{ph}"], [f"arow{ph}"])
            scl(lambda e: e.activation(out=arow, in_=arow, func=AF.Exp,
                                       scale=-0.5),
                [f"arow{ph}"], [f"arow{ph}"])
            vec(lambda e: e.scalar_tensor_tensor(out=brow, in0=mrow,
                                                 scalar=-1.0, in1=arow,
                                                 op0=OP.mult, op1=OP.mult),
                [f"mrow{ph}", f"arow{ph}"], [f"brow{ph}"])
            for it, (t0, tw) in enumerate(TT):
                mm(PSA[2][:, 0:tw], ones1f, arow[:, t0:t0 + tw],
                   ["o1f", f"arow{ph}"], "PS2")
                mm(PSA[3][:, 0:tw], ones1f, brow[:, t0:t0 + tw],
                   ["o1f", f"brow{ph}"], "PS3")
                scl(lambda e, t0=t0, tw=tw:
                    e.copy(out=a_b[:, t0:t0 + tw], in_=PSA[2][:, 0:tw]),
                    ["PS2"], [f"ab{ph}"])
                scl(lambda e, t0=t0, tw=tw:
                    e.copy(out=b_b[:, t0:t0 + tw], in_=PSA[3][:, 0:tw]),
                    ["PS3"], [f"bb{ph}"])
            if router:
                tokm_sb = A.tb[0:1, 2 * moff:2 * moff + T]
                tkkey = f"mrow{ph}"
                mask_b = tmp_ln[1]
                mkkey = f"tmpln1{ph}"
                mab = A.bf16(128, T, "mab")
                sqs4 = tmp_ln[0]
                sqkey = f"tmpln0{ph}"
                qc_sb = A.f32(128, 64, "qck")
                bsel_sb = A.f32(128, 64, "bselk")
                mb1 = A.f32(1, 1, "mb1")
                mbcol = A.f32(128, 1, "mbcol")
                dma("gpsimd", lambda e: e.dma_start(out=tokm_sb, in_=tokmask),
                    [tkkey], [tkkey])
                dma("sync", lambda e: e.dma_start(out=bsel_sb, in_=bsel_i),
                    [], ["bselk"])
                for it, (t0, tw) in enumerate(TT):
                    mm(PSA[4][:, 0:tw], ones1b, tokm_sb[:, t0:t0 + tw],
                       ["o1b", tkkey], "PS4")
                    scl(lambda e, t0=t0, tw=tw:
                        e.copy(out=mask_b[:, t0:t0 + tw],
                               in_=PSA[4][:, 0:tw]),
                        ["PS4"], [mkkey])
                vec(lambda e: e.tensor_tensor(out=mab, in0=mask_b, in1=a_b,
                                              op=OP.mult),
                    [mkkey, f"ab{ph}"], ["mab"])
                for kc in range(NKC):
                    vec(lambda e, k=kc: e.scalar_tensor_tensor(
                        out=sqs4, in0=panel[:, k * T:(k + 1) * T],
                        scalar=1.0, in1=mab, op0=OP.mult, op1=OP.mult,
                        accum_out=qacc[:, k:k + 1]),
                        [f"pan{kc}", "mab"], [sqkey, f"qacc{kc}"])
                vec(lambda e: e.scalar_tensor_tensor(
                    out=mab[0:1, :], in0=brow, scalar=1.0, in1=tokm_sb,
                    op0=OP.mult, op1=OP.mult, accum_out=mb1),
                    [f"brow{ph}", tkkey], ["mab", "mb1"])
                mm(PSA[4][:, 0:1], ones1f, mb1, ["o1f", "mb1"], "PS4")
                vec(lambda e: e.tensor_copy(mbcol, PSA[4][:, 0:1]),
                    ["PS4"], ["mbcol"])
                QACCK = [f"qacc{k}" for k in range(NKC)]
                vec(lambda e: e.tensor_scalar(out=qacc, in0=qacc,
                                              scalar1=mbcol[:, 0:1],
                                              scalar2=None, op0=OP.add),
                    QACCK + ["mbcol"], QACCK)
                vec(lambda e: e.tensor_tensor(
                    out=qc_sb.rearrange("p (b k) -> p b k", b=2),
                    in0=qacc[:, None, :].broadcast_to([128, 2, NKC]),
                    in1=bsel_sb.rearrange("p (b k) -> p b k", b=2),
                    op=OP.mult),
                    QACCK + ["bselk"], ["qck"])
                dma("sync", lambda e: e.dma_start(out=q_in.ap(), in_=qc_sb),
                    ["qck"], ["q_in"])
                coll("CC3", "AllReduce", OP.add, q_in, q_out,
                     ["q_in"], ["q_out"])
            for kc in range(NKC):
                pr = kc % 2
                vec(lambda e, k=kc, p=pr:
                    e.tensor_tensor(out=tmp_ln[p],
                                    in0=panel[:, k * T:(k + 1) * T],
                                    in1=a_b, op=OP.mult),
                    [f"pan{kc}", f"ab{ph}"], [f"tmpln{pr}{ph}"])
                vec(lambda e, k=kc, p=pr:
                    e.tensor_tensor(out=panel[:, k * T:(k + 1) * T],
                                    in0=tmp_ln[p], in1=b_b, op=OP.add),
                    [f"tmpln{pr}{ph}", f"bb{ph}"], [f"pan{kc}"])
            A.pop_fence()

        phase_ln("x", "L1")

        A.push()
        dt_raw_full = A.f32(128, T, ("dt_raw", "dt_pad"))
        dt_raw = dt_raw_full[0:H, :]
        A.push()
        wsl = [A.bf16(128, NKC * 128, f"wsl{i}") for i in range(2)]
        dsl = [A.bf16(128, KCV * 128, f"dsl{i}") for i in range(2)]
        raw = [A.bf16(128, T + 4, f"raw{i}") for i in range(2)]
        xbct = [A.bf16(128, T, f"xbct{i}") for i in range(2)]
        bndb = A.bf16(4, 128, "bndb")
        NDT = 8
        TQ = T // NDT
        CPP = TQ // QC
        slotA = A.f32(64, TQ, "dZa")
        slotBq = A.f32(64, TQ, "dZb")
        sCq = A.f32(64, TQ, "dCq")
        hspB = A.bf16(64, TQ, "dHsp")
        eltwb = A.bf16(64, TQ, "eltw")
        eltb = A.bf16(128, CPP * 64, "eltb")
        eLendF = A.f32(64, NQ, "eLF")
        elendt = A.f32(NQ, H, "elendtk")
        caH = A.f32(64, NDT, "caH")

        vec(lambda e: e.memset(raw[0][:, 0:4], 0.0), [], ["raw0"])
        vec(lambda e: e.memset(raw[1][:, 0:4], 0.0), [], ["raw1"])
        vec(lambda e: e.memset(bndb, 0.0), [], ["bndb"])

        def emit_dt_half(half):
            hc = half * TQ
            X = dt_raw_full[0:64, hc:hc + TQ]
            sC = sCq
            slotB = slotBq
            vec(lambda e: e.tensor_scalar(out=slotA, in0=X,
                                          scalar1=dtb_sb[:, 0:1],
                                          scalar2=None, op0=OP.add),
                ["dt_raw", "dtbk"], ["dZa"])
            vec(lambda e: e.scalar_tensor_tensor(out=slotB, in0=slotA,
                                                 scalar=-1.0, in1=slotA,
                                                 op0=OP.mult, op1=OP.max),
                ["dZa"], ["dZb"])
            scl(lambda e: e.activation(out=slotB, in_=slotB, func=AF.Exp,
                                       scale=-1.0), ["dZb"], ["dZb"])
            scl(lambda e: e.activation(out=slotB, in_=slotB, func=AF.Ln,
                                       bias=onescol[0:64, 0:1]),
                ["dZb", "oc"], ["dZb"])
            vec(lambda e: e.tensor_scalar(out=sC, in0=slotA, scalar1=0.0,
                                          scalar2=None, op0=OP.max),
                ["dZa"], ["dCq"])
            vec(lambda e: e.tensor_tensor(out=sC, in0=sC, in1=slotB,
                                          op=OP.add),
                ["dCq", "dZb"], ["dCq"])
            vec(lambda e: e.tensor_scalar(out=X, in0=sC,
                                          scalar1=negA[:, 0:1],
                                          scalar2=None, op0=OP.mult),
                ["dCq", "negA", "dt_raw"], ["dt_raw"])
            for qq in range(CPP):
                vec(lambda e, qq=qq: e.tensor_tensor_scan(
                    out=slotA[:, qq * QC:(qq + 1) * QC], data0=ones64x,
                    data1=X[:, qq * QC:(qq + 1) * QC], initial=0.0,
                    op0=OP.mult, op1=OP.add), ["dt_raw", "o64"], ["dZa"])
            vec(lambda e, half=half:
                e.reduce_sum(caH[:, half:half + 1], X, axis=AX.X),
                ["dt_raw"], ["caH"])
            scl(lambda e, half=half: e.activation(
                out=eLendF[:, half * CPP:(half + 1) * CPP],
                in_=slotA[:, QC - 1::QC], func=AF.Exp), ["dZa"], ["eLF"])
            scl(lambda e: e.activation(out=eltwb, in_=slotA, func=AF.Exp),
                ["dZa"], ["eltw"])
            scl(lambda e: e.activation(out=slotB, in_=sC, func=AF.Ln),
                ["dCq"], ["dZb"])
            vec(lambda e: e.tensor_tensor(out=sC, in0=slotB, in1=slotA,
                                          op=OP.subtract),
                ["dZa", "dZb"], ["dCq"])
            for ti, (curv, ck, outs) in enumerate(
                    [(slotA, "dZa", lsp_s), (sCq, "dCq", nrsp_s)]):
                for si in range(3):
                    vec(lambda e, c=curv: e.tensor_copy(hspB, c),
                        [ck], ["dHsp"])
                    dma("sync", lambda e, o=outs[si], hc=hc:
                        e.dma_start(out=o[:, hc:hc + TQ], in_=hspB),
                        ["dHsp"], [f"sp{ti}{si}h{half}"])
                    if si < 2:
                        vec(lambda e, c=curv:
                            e.tensor_tensor(out=c, in0=c, in1=hspB,
                                            op=OP.subtract),
                            [ck, "dHsp"], [ck])

        def emit_dt_trans(half):
            for i in range(CPP):
                tr(PBA[1][:, i * 64:(i + 1) * 64],
                   eltwb[:, i * 128:(i + 1) * 128],
                   ident_bf[0:64, 0:64],
                   ["eltw", "idb"], "PBT1")
            vec(lambda e: e.tensor_copy(eltb, PBA[1][:, 0:CPP * 64]),
                ["PBT1"], ["eltb"])
            dma("sync", lambda e, half=half: e.dma_start(
                out=elt_s.ap()[half * CPP:(half + 1) * CPP]
                .rearrange("q p h -> p q h"),
                in_=eltb.rearrange("p (q h) -> p q h", q=CPP)),
                ["eltb"], [f"eltd{half}"])

        def emit_dt_fin():
            tr(PSA[5][0:NQ, 0:H], eLendF, ident_f[0:H, 0:H],
               ["eLF", "idf"], "PS5")
            vec(lambda e: e.tensor_copy(elendt, PSA[5][0:NQ, 0:H]),
                ["PS5"], ["elendtk"])
            dma("sync", lambda e: e.dma_start(out=elend_s, in_=elendt),
                ["elendtk"], ["elend_d"])
            vec(lambda e: e.tensor_tensor(out=caH[:, 0:4],
                                          in0=caH[:, 0:4],
                                          in1=caH[:, 4:8], op=OP.add),
                ["caH"], ["caH"])
            vec(lambda e: e.tensor_tensor(out=caH[:, 0:2],
                                          in0=caH[:, 0:2],
                                          in1=caH[:, 2:4], op=OP.add),
                ["caH"], ["caH"])
            vec(lambda e: e.tensor_tensor(out=caH[:, 0:1],
                                          in0=caH[:, 0:1],
                                          in1=caH[:, 1:2], op=OP.add),
                ["caH"], ["caH"])
            scl(lambda e: e.activation(out=caH[:, 0:1],
                                       in_=caH[:, 0:1], func=AF.Exp),
                ["caH"], ["caH"])
            dma("sync", lambda e: e.dma_start(out=ca_in.ap(),
                                              in_=caH[:, 0:1]),
                ["caH"], ["ca_in"])

        gidx = 0
        cidx = 0
        for oc in range(49):
            wb = oc % 2
            dma("gpsimd", lambda e, o=oc, w=wb:
                e.dma_start(out=wsl[w],
                            in_=w_hbcdt[o].rearrange("p kc m -> p (kc m)")),
                [], [f"wsl{wb}"])
            if oc >= 1:
                dma("gpsimd", lambda e, o=oc, w=wb:
                    e.dma_start(out=dsl[w].rearrange("p (k m) -> p k m",
                                                     k=KCV),
                                in_=diagw[o - 1]
                                .rearrange("k p m -> p k m")),
                    [], [f"dsl{wb}"])
            for it, (t0, tw) in enumerate(TT):
                bank = gidx % 4
                for kc in range(NKC):
                    mm(PSA[bank][:, 0:tw],
                       wsl[wb][:, kc * 128:(kc + 1) * 128],
                       panel[:, kc * T + t0:kc * T + t0 + tw],
                       [f"wsl{wb}", f"pan{kc}"], f"PS{bank}",
                       start=(kc == 0), stop=(kc == NKC - 1))
                scl(lambda e, w=wb, t0=t0, tw=tw, b=bank:
                    e.copy(out=raw[w][:, 4 + t0:4 + t0 + tw],
                           in_=PSA[b][:, 0:tw]),
                    [f"PS{bank}"], [f"raw{wb}"])
                gidx += 1
            if oc == 0:
                vec(lambda e: e.tensor_copy(dt_raw, raw[0][0:H, 4:4 + T]),
                    ["raw0"], ["dt_raw"])
            else:
                for it, (t0, tw) in enumerate(TT):
                    bank = 4 + (cidx % 2)
                    for k in range(KCV):
                        mm(PSA[bank][:, 0:tw],
                           dsl[wb][:, k * 128:(k + 1) * 128],
                           raw[wb][:, t0 + k + 1:t0 + k + 1 + tw],
                           [f"dsl{wb}", f"raw{wb}"], f"PS{bank}",
                           start=(k == 0), stop=(k == KCV - 1))
                    scl(lambda e, w=wb, t0=t0, tw=tw, b=bank, o=oc:
                        e.activation(out=xbct[w][:, t0:t0 + tw],
                                     in_=PSA[b][:, 0:tw], func=AF.Silu,
                                     bias=cb_sb[:, o - 1:o]),
                        [f"PS{bank}", "cb"], [f"xbct{wb}"])
                    cidx += 1
                vec(lambda e, w=wb, o=oc:
                    e.tensor_copy(pf_sb[:, (o - 1) * 3:(o - 1) * 3 + 3],
                                  raw[w][:, 4:7]),
                    [f"raw{wb}"], ["pf"])
                tr(PBA[0][0:3, 0:128], raw[wb][:, T + 1:T + 4], ident_bf,
                   [f"raw{wb}", "idb"], "PBT0")
                vec(lambda e: e.tensor_copy(bndb[0:3, :], PBA[0][0:3, 0:128]),
                    ["PBT0"], ["bndb"])
                dma("sync", lambda e, o=oc:
                    e.dma_start(out=bnd_in.ap()[:, (o - 1) * 128:o * 128],
                                in_=bndb),
                    ["bndb"], ["bnd_in"])
                dma("sync", lambda e, w=wb, o=oc:
                    e.dma_start(out=xbc_s[(o - 1) * 128:o * 128, :],
                                in_=xbct[w]),
                    [f"xbct{wb}"], [f"xbcd{oc - 1}"])
            if 2 <= oc <= 9:
                emit_dt_half(oc - 2)
            elif 11 <= oc <= 18:
                emit_dt_trans(oc - 11)
            elif oc == 20:
                emit_dt_fin()
        coll("CC1", "AllGather", OP.bypass, bnd_in, bnd_out,
             ["bnd_in"], ["bnd_out"])
        A.pop_fence()

        A.pop_fence()
        A.push()
        bnd_tm = A.bf16(4, CONV, "bnd_tm")
        rbuf = [A.bf16(4, CONV, f"rbuf{i}") for i in range(2)]
        bndc_sb = A.f32(4, 8, "bndck")
        rawx = A.bf16(128, 8, "rawx")
        pacc = A.f32(128, 3, "pacc")
        dma("sync", lambda e: e.dma_start(out=bndc_sb, in_=bndc),
            [], ["bndck"])
        vec(lambda e: e.memset(bnd_tm, 0.0), [], ["bnd_tm"])
        for r in range(8):
            rb_ = r % 2
            dma("scalq", lambda e, r=r, w=rb_:
                e.dma_start(out=rbuf[w],
                            in_=bnd_out.ap()[r * 4:(r + 1) * 4, :]),
                ["bnd_out"], [f"rbuf{rb_}"])
            vec(lambda e, r=r, w=rb_: e.scalar_tensor_tensor(
                out=bnd_tm, in0=rbuf[w], scalar=bndc_sb[:, r:r + 1],
                in1=bnd_tm, op0=OP.mult, op1=OP.add),
                [f"rbuf{rb_}", "bnd_tm", "bndck"], ["bnd_tm"])
        for c in range(NHBC):
            tr(PBA[1][:, 0:4], bnd_tm[0:4, c * 128:(c + 1) * 128],
               ident_bf[0:4, 0:4], ["bnd_tm", "idb"], "PBT1")
            vec(lambda e: e.tensor_copy(rawx[:, 0:3], PBA[1][:, 0:3]),
                ["PBT1"], ["rawx"])
            vec(lambda e, c=c: e.tensor_copy(rawx[:, 3:6],
                                             pf_sb[:, c * 3:c * 3 + 3]),
                ["pf"], ["rawx"])
            vec(lambda e, c=c: e.tensor_scalar(
                out=pacc, in0=rawx[:, 0:3],
                scalar1=cw_sb[:, c * KCV:c * KCV + 1], scalar2=None,
                op0=OP.mult), ["rawx", "cw"], ["pacc"])
            for k in range(1, KCV):
                vec(lambda e, c=c, k=k: e.scalar_tensor_tensor(
                    out=pacc, in0=rawx[:, k:k + 3],
                    scalar=cw_sb[:, c * KCV + k:c * KCV + k + 1],
                    in1=pacc, op0=OP.mult, op1=OP.add),
                    ["rawx", "cw", "pacc"], ["pacc"])
            scl(lambda e, c=c: e.activation(out=pxbc[:, c * 3:c * 3 + 3],
                                            in_=pacc, func=AF.Silu,
                                            bias=cb_sb[:, c:c + 1]),
                ["pacc", "cb"], ["pxbc"])
        A.pop_fence()

        def load_nr(NRq, q, tag):
            for si in range(3):
                dma("scalq", lambda e, si=si, q=q:
                    e.dma_start(out=NRq[si * 4:(si + 1) * 4, :]
                                .rearrange("k (hb t) -> k hb t", hb=NHB),
                                in_=nrsp_s[si]
                                .rearrange("(hb k) t -> k hb t", k=4)
                                [:, :, q * QC:(q + 1) * QC]),
                    [f"sp1{si}h{q // 2}"], [tag])

        def load_l3(Lr3, q, quarter, tag, nh=16):
            for si in range(3):
                dma("scalq", lambda e, si=si, q=q, hf=quarter, nh=nh:
                    e.dma_start(out=Lr3[si:si + 1]
                                .rearrange("p (h t) -> p h t", h=nh),
                                in_=lsp_s[si][hf * nh:(hf + 1) * nh,
                                              q * QC:(q + 1) * QC]
                                .unsqueeze(0)),
                    [f"sp0{si}h{q // 2}"], [tag])

        def load_elrow(dst, q, tag):
            dma("sync", lambda e, q=q:
                e.dma_start(out=dst, in_=elend_s[q:q + 1, :]),
                ["elend_d"], [tag])

        def overlay(buf, chunks, tag):
            for li, c in enumerate(chunks):
                vec(lambda e, li=li, c=c:
                    e.tensor_copy(buf[:, li * 128:li * 128 + 3],
                                  pxbc[:, c * 3:c * 3 + 3]),
                    ["pxbc", tag], [tag])

        A.push()
        xfm_q = A.bf16(128, I, "xfmq")
        xtm_q = A.bf16(128, I, "xtmq")
        bq = A.bf16(128, G * 128, "bq")
        btm = A.bf16(128, G * 128, "btm")
        NRq = A.bf16(12, NHB * QC, "NRqA")
        Lr3 = A.bf16(3, 32 * QC, "Lr3A")
        elr = A.f32(1, H, "elrA")
        cumAq = A.f32(128, H, "cumAqA")
        wcl = A.f32(128, H, "wclA")
        wcols = A.f32(128, H, "wcolsA")
        bw = A.bf16(128, 1024, "bwA")
        sttb = A.f32(128, 512, "sttA")

        for q in range(NQ):
            load_nr(NRq, q, "NRqA")
            load_elrow(elr, q, "elrA")
            dma("scalq", lambda e, q=q:
                e.dma_start(out=xfm_q.rearrange("p (c t) -> p c t", c=32),
                            in_=xbc_s[0:I, q * QC:(q + 1) * QC]
                            .rearrange("(c p) t -> p c t", p=128)),
                [f"xbcd{c}" for c in range(32)], ["xfmq"])
            dma("scalq", lambda e, q=q:
                e.dma_start(out=bq.rearrange("p (c t) -> p c t", c=8),
                            in_=xbc_s[I:I + G * 128, q * QC:(q + 1) * QC]
                            .rearrange("(c p) t -> p c t", p=128)),
                [f"xbcd{32 + c}" for c in range(8)], ["bq"])
            if q == 0:
                overlay(xfm_q, range(32), "xfmq")
                overlay(bq, range(32, 40), "bq")
            mm(PSA[5][:, 0:H], ones1f, elr, ["o1f", "elrA"], "PS5")
            vec(lambda e: e.tensor_copy(cumAq, PSA[5][:, 0:H]),
                ["PS5"], ["cumAqA"])
            for c8 in range(0, 32, 8):
                for c in range(c8, c8 + 8):
                    tr(PBA[0][:, (c - c8) * 128:(c - c8 + 1) * 128],
                       xfm_q[:, c * 128:(c + 1) * 128], ident_bf,
                       ["xfmq", "idb"], "PBT0")
                vec(lambda e, c8=c8:
                    e.tensor_copy(xtm_q[:, c8 * 128:(c8 + 8) * 128],
                                  PBA[0]),
                    ["PBT0"], ["xtmq"])
            dma("sync", lambda e, q=q: e.dma_start(out=xtm_s[q], in_=xtm_q),
                ["xtmq"], [f"xtmd{q}"])
            for gx in range(8):
                tr(PBA[1][:, gx * 128:(gx + 1) * 128],
                   bq[:, gx * 128:(gx + 1) * 128], ident_bf,
                   ["bq", "idb"], "PBT1")
            vec(lambda e: e.tensor_copy(btm, PBA[1]), ["PBT1"], ["btm"])
            for half in range(2):
                load_l3(Lr3, q, half, "Lr3A", nh=32)
                for hbl in range(8):
                    hb = half * 8 + hbl
                    mm(PSA[4][:, hb * 4:(hb + 1) * 4],
                       NRq[:, hb * QC:(hb + 1) * QC],
                       ind12_sb[:, QC - 1::QC],
                       ["NRqA", "ind12"], "PS4", start=True, stop=False)
                    mm(PSA[4][:, hb * 4:(hb + 1) * 4],
                       ones3b,
                       Lr3[:, hbl * 4 * QC + QC - 1:(hbl + 1) * 4 * QC:QC],
                       ["o3b", "Lr3A"], "PS4", start=False, stop=True)
            vec(lambda e: e.tensor_scalar(out=wcl, in0=PSA[4][:, 0:H],
                                          scalar1=30.0, scalar2=None,
                                          op0=OP.min), ["PS4"], ["wclA"])
            scl(lambda e: e.activation(out=wcols, in_=wcl, func=AF.Exp),
                ["wclA"], ["wcolsA"])
            for gx in range(8):
                ub = 2 + (gx % 2)
                pool(lambda e, gx=gx: e.tensor_tensor(
                    out=bw.rearrange("p (h n) -> p h n", h=8),
                    in0=btm[:, gx * 128:(gx + 1) * 128][:, None, :]
                    .broadcast_to([128, 8, 128]),
                    in1=wcols[:, gx * 8:(gx + 1) * 8][:, :, None]
                    .broadcast_to([128, 8, 128]),
                    op=OP.mult), ["btm", "wcolsA"], ["bwA"])
                for hl8 in range(8):
                    h = gx * 8 + hl8
                    mm(PSA[ub][:, hl8 * 64:(hl8 + 1) * 64],
                       bw[:, hl8 * 128:(hl8 + 1) * 128],
                       xtm_q[:, h * 64:(h + 1) * 64],
                       ["bwA", "xtmq"], f"PS{ub}")
                pool(lambda e, gx=gx: e.tensor_tensor(
                    out=sttb.rearrange("p (h x) -> p h x", h=8),
                    in0=Sst[:, gx * 512:(gx + 1) * 512]
                    .rearrange("p (h x) -> p h x", h=8),
                    in1=cumAq[:, gx * 8:(gx + 1) * 8][:, :, None]
                    .broadcast_to([128, 8, 64]),
                    op=OP.mult), [f"Sg{gx}", "cumAqA"], ["sttA"])
                vec(lambda e, gx=gx, b=ub: e.tensor_tensor(
                    out=Sst[:, gx * 512:(gx + 1) * 512], in0=sttb,
                    in1=PSA[b], op=OP.add),
                    ["sttA", f"PS{ub}"], [f"Sg{gx}"])
        dma("gpsimd", lambda e: e.dma_start(out=st_in.ap(), in_=Sst),
            ALLSG, ["st_in"])
        coll("CC2a", "AllGather", OP.bypass, st_in, st_out,
             ["st_in"], ["st_out"])
        coll("CC2b", "AllGather", OP.bypass, ca_in, ca_out,
             ["ca_in"], ["ca_out"])
        A.pop_fence()

        A.push()
        GS = 256
        NGS = I // GS
        gsl = [A.bf16(128, NKC * GS, f"gsl{i}") for i in range(2)]
        gout = [A.bf16(128, GS, f"gout{i}") for i in range(2)]
        gg = 0
        for s in range(NGS):
            wb2 = s % 2
            dma("gpsimd", lambda e, s=s, w=wb2:
                e.dma_start(out=gsl[w].rearrange("p (kc m) -> p kc m",
                                                 kc=NKC),
                            in_=w_gate[:, :, s * GS:(s + 1) * GS]
                            .rearrange("kc p m -> p kc m")),
                [], [f"gsl{wb2}"])
            for tt in range(T16):
                bank = gg % 4
                for kc in range(NKC):
                    mm(PSA[bank][:, 0:GS],
                       panel[:, kc * T + tt * 128:kc * T + (tt + 1) * 128],
                       gsl[wb2][:, kc * GS:(kc + 1) * GS],
                       [f"gsl{wb2}", f"pan{kc}"], f"PS{bank}",
                       start=(kc == 0), stop=(kc == NKC - 1))
                ob = gg % 2
                scl(lambda e, b=bank, o=ob:
                    e.activation(out=gout[o], in_=PSA[b][:, 0:GS],
                                 func=AF.Silu),
                    [f"PS{bank}"], [f"gout{ob}"])
                dma("sync", lambda e, s=s, tt=tt, o=ob:
                    e.dma_start(out=gate_s[tt * 128:(tt + 1) * 128,
                                           s * GS:(s + 1) * GS],
                                in_=gout[o]),
                    [f"gout{ob}"], [f"gated{s}_{tt}"])
                gg += 1
        A.pop_fence()

        A.push()
        cumAg = A.f32(H, 8, "cumAg")
        chm_sb = A.f32(H, 64, "chmk")
        chm1_sb = A.f32(H, 64, "chm1k")
        coefs = A.f32(H, 8, "coefs")
        ct1 = A.f32(H, 8, "ct1")
        ct2 = A.f32(H, 4, "ct2")
        ct3 = A.f32(H, 2, "ct3")
        coefT = A.f32(8, H, "coefT")
        coefrow = A.f32(1, 512, "coefrow")
        Gr = A.bf16(128, H * P, "Gr")
        cfb = A.f32(128, H, "cfb")
        tbuf = A.f32(128, H * P, "tbuf")
        dma("sync", lambda e: e.dma_start(out=chm_sb, in_=chmh), [], ["chmk"])
        dma("sync", lambda e: e.dma_start(out=chm1_sb, in_=chm1mh),
            [], ["chm1k"])
        dma("sync", lambda e: e.dma_start(
            out=cumAg,
            in_=ca_out.ap().rearrange("(r h) x -> h (r x)", r=8),
            allow_slow_non_contiguous=True), ["ca_out"], ["cumAg"])
        for r in range(8):
            vec(lambda e, r=r: e.tensor_tensor(
                out=ct1, in0=cumAg, in1=chm_sb[:, r * 8:(r + 1) * 8],
                op=OP.mult), ["cumAg", "chmk"], ["ct1"])
            vec(lambda e, r=r: e.tensor_tensor(
                out=ct1, in0=ct1, in1=chm1_sb[:, r * 8:(r + 1) * 8],
                op=OP.add), ["ct1", "chm1k"], ["ct1"])
            vec(lambda e: e.tensor_tensor(out=ct2, in0=ct1[:, 0:4],
                                          in1=ct1[:, 4:8], op=OP.mult),
                ["ct1"], ["ct2"])
            vec(lambda e: e.tensor_tensor(out=ct3, in0=ct2[:, 0:2],
                                          in1=ct2[:, 2:4], op=OP.mult),
                ["ct2"], ["ct3"])
            vec(lambda e, r=r: e.tensor_tensor(out=coefs[:, r:r + 1],
                                               in0=ct3[:, 0:1],
                                               in1=ct3[:, 1:2], op=OP.mult),
                ["ct3"], ["coefs"])
        tr(PSA[5][0:8, 0:H], coefs, ident_f[0:H, 0:H], ["coefs", "idf"], "PS5")
        vec(lambda e: e.tensor_copy(coefT, PSA[5][0:8, 0:H]),
            ["PS5"], ["coefT"])
        dma("sync", lambda e: e.dma_start(out=coefb_s, in_=coefT),
            ["coefT"], ["coefb_d"])
        dma("sync", lambda e: e.dma_start(
            out=coefrow.rearrange("p (r h) -> p r h", r=8),
            in_=coefb_s.unsqueeze(0)), ["coefb_d"], ["coefrow"])
        vec(lambda e: e.memset(Sst, 0.0), [], ALLSG)
        for r in range(8):
            dma("scalq", lambda e, r=r:
                e.dma_start(out=Gr,
                            in_=st_out.ap()[r * 128:(r + 1) * 128, :]),
                ["st_out"], ["Gr"])
            mm(PSA[4][:, 0:H], ones1f, coefrow[:, r * H:(r + 1) * H],
               ["o1f", "coefrow"], "PS4")
            vec(lambda e: e.tensor_copy(cfb, PSA[4][:, 0:H]),
                ["PS4"], ["cfb"])
            vec(lambda e: e.tensor_tensor(
                out=tbuf.rearrange("p (h x) -> p h x", h=H),
                in0=Gr.rearrange("p (h x) -> p h x", h=H),
                in1=cfb[:, :, None].broadcast_to([128, H, P]),
                op=OP.mult), ["Gr", "cfb"], ["tbuf"])
            for gx in range(8):
                vec(lambda e, gx=gx: e.tensor_tensor(
                    out=Sst[:, gx * 512:(gx + 1) * 512],
                    in0=tbuf[:, gx * 512:(gx + 1) * 512],
                    in1=Sst[:, gx * 512:(gx + 1) * 512], op=OP.add),
                    ["tbuf", f"Sg{gx}"], [f"Sg{gx}"])
        for gx in range(8):
            scl(lambda e, gx=gx: e.copy(
                out=S_bf[:, gx * 512:(gx + 1) * 512],
                in_=Sst[:, gx * 512:(gx + 1) * 512]),
                [f"Sg{gx}"], [f"Sbf{gx}"])
        A.pop_fence()

        A.push()
        xtmu = [A.bf16(128, 512, f"xtmu{i}") for i in range(2)]
        bq2x = [A.bf16(128, G * 128, f"bq2x{i}") for i in range(2)]
        cq2x = [A.bf16(128, G * 128, f"cq2x{i}") for i in range(2)]
        btm2 = A.bf16(128, G * 128, "btm2")
        Gm = A.bf16(128, G * 128, "Gm")
        NRL15 = [A.bf16(15, 4 * QC, f"NRq2x{i}") for i in range(2)]
        ILx = [A.bf16(15, 8 * QC, f"Lr32x{i}") for i in range(2)]
        elrx = [A.f32(1, H, f"elrx{i}") for i in range(2)]
        cumAq2 = [A.f32(128, H, f"cumAq2_{i}") for i in range(2)]
        dmy = A.f32(128, 1024, "dmy")
        wd = [A.bf16(128, 1024, f"wd{i}") for i in range(2)]
        wt = [A.bf16(128, 1024, (f"wtA{i}", f"wtB{i}")) for i in range(2)]
        xw = [A.bf16(128, 512, f"xw{i}") for i in range(2)]
        ysd = A.bf16(128, 512, "ysd")
        sdec = A.f32(128, 512, "sdec")
        eltq = [A.bf16(128, 64, f"eltq{i}") for i in range(3)]
        ysb = [A.f32(128, 512, f"ysb{i}") for i in range(2)]
        sqscr = A.bf16(128, 512, "sqx")
        ssq = [A.f32(128, 1, f"ssq{i}") for i in range(2)]
        rstd = [A.f32(128, 1, f"rstd{i}") for i in range(2)]
        yng = A.bf16(128, 512, "yng")
        ggt = [A.bf16(128, 512, f"ggt{i}") for i in range(2)]
        b30 = A.f32(128, 1, "b30")
        vec(lambda e: e.memset(b30, 30.0), [], ["b30"])
        E30 = float(np.exp(30.0))

        NU = NQ * 8

        def load_nrl(tu):
            q, gx = tu // 8, tu % 8
            b = tu % 2
            dma("sync", lambda e, q=q, gx=gx, b=b:
                e.dma_start(
                    out=ILx[b][12:15, :]
                    .rearrange("si (h t) -> si h t", h=8),
                    in_=lsp3[:, gx * 8:(gx + 1) * 8,
                             q * QC:(q + 1) * QC]),
                [f"sp0{si}h{q // 2}" for si in range(3)], [f"Lr32x{b}"])

        def load_nr2(g):
            tu = 2 * g
            q, gx = tu // 8, tu % 8
            b = g % 2
            for si in range(3):
                dma("sync", lambda e, si=si, q=q, gx=gx, b=b:
                    e.dma_start(
                        out=NRL15[b][si * 4:(si + 1) * 4, :]
                        .rearrange("k (hb t) -> k hb t", hb=4),
                        in_=nrsp3[si]
                        .rearrange("(hb k) t -> k hb t", k=4)
                        [:, gx * 2:gx * 2 + 4, q * QC:(q + 1) * QC]),
                    [f"sp1{si}h{q // 2}"], [f"NRq2x{b}"])

        def load_qgrp(q):
            b = q % 2
            dma("sync", lambda e, q=q, b=b:
                e.dma_start(out=bq2x[b].rearrange("p (c t) -> p c t", c=8),
                            in_=xbc_s[I:I + G * 128,
                                      q * QC:(q + 1) * QC]
                            .rearrange("(c p) t -> p c t", p=128)),
                [f"xbcd{32 + c}" for c in range(8)], [f"bq2x{b}"])
            dma("sync", lambda e, q=q, b=b:
                e.dma_start(out=cq2x[b].rearrange("p (c t) -> p c t", c=8),
                            in_=xbc_s[I + G * 128:CONV,
                                      q * QC:(q + 1) * QC]
                            .rearrange("(c p) t -> p c t", p=128)),
                [f"xbcd{40 + c}" for c in range(8)], [f"cq2x{b}"])
            dma("sync", lambda e, q=q, b3=q % 3:
                e.dma_start(out=eltq[b3], in_=elt_s.ap()[q]),
                [f"eltd{q // 2}"], [f"eltq{q % 3}"])
            dma("sync", lambda e, q=q, b=b:
                e.dma_start(out=elrx[b], in_=elend_s[q:q + 1, :]),
                ["elend_d"], [f"elrx{b}"])

        def b_front_d(ui):
            a2 = ui % 2
            g2 = (ui // 2) % 2
            aks = [f"PS{2 * a2}", f"PS{2 * a2 + 1}"]
            for sub in range(2):
                hbl = (ui % 2) * 2 + sub
                mm(PF[a2][:, sub * 512:sub * 512 + 512],
                   NRL15[g2][:, hbl * QC:(hbl + 1) * QC],
                   ILx[a2][:, sub * 512:(sub + 1) * 512],
                   [f"NRq2x{g2}", f"Lr32x{a2}"], aks[sub])

        def b_front(ui):
            q, gx = ui // 8, ui % 8
            u = ui % 2
            a2 = ui % 2
            qp = q % 2
            aks = [f"PS{2 * a2}", f"PS{2 * a2 + 1}"]
            if gx == 0:
                if q + 1 < NQ:
                    load_qgrp(q + 1)
                mm(PSA[4][:, 0:H], ones1f, elrx[qp],
                   ["o1f", f"elrx{qp}"], "PS4")
                vec(lambda e, qp=qp:
                    e.tensor_copy(cumAq2[qp], PSA[4][:, 0:H]),
                    ["PS4"], [f"cumAq2_{qp}"])
                for g2 in range(8):
                    half = g2 // 4
                    mm(PF[a2][:, half * 512 + (g2 % 4) * 128:
                       half * 512 + (g2 % 4 + 1) * 128],
                       bq2x[qp][:, g2 * 128:(g2 + 1) * 128],
                       cq2x[qp][:, g2 * 128:(g2 + 1) * 128],
                       [f"bq2x{qp}", f"cq2x{qp}"], aks[half])
                for half in range(2):
                    vec(lambda e, b=half, a_=a2: e.tensor_tensor(
                        out=Gm[:, b * 512:(b + 1) * 512]
                        .rearrange("p (g t) -> p g t", g=4),
                        in0=PF[a_][:, b * 512:(b + 1) * 512]
                        .rearrange("p (g t) -> p g t", g=4),
                        in1=tri_sb[:, None, :].broadcast_to([128, 4, 128]),
                        op=OP.mult), [aks[half], "tri"], ["Gm"])
                for g2 in range(8):
                    tr(PBA[1][:, g2 * 128:(g2 + 1) * 128],
                       bq2x[qp][:, g2 * 128:(g2 + 1) * 128], ident_bf,
                       [f"bq2x{qp}", "idb"], "PBT1")
                scl(lambda e: e.activation(out=btm2, in_=PBA[1],
                                           func=AF.Identity,
                                           scale=E30),
                    ["PBT1"], ["btm2"])
            if ui + 1 < NU:
                load_nrl(ui + 1)
            if ui % 2 == 1 and ui + 1 < NU:
                load_nr2((ui + 1) // 2)
            dma("sync", lambda e, q=q, gx=gx, u=u:
                e.dma_start(out=xtmu[u],
                            in_=xtm_s[q][:, gx * 512:(gx + 1) * 512]),
                [f"xtmd{q}"], [f"xtmu{u}"])

        def b_scal(ui):
            u = ui % 2
            a2 = ui % 2
            aks = [f"PS{2 * a2}", f"PS{2 * a2 + 1}"]
            scl(lambda e, a_=a2: e.activation(
                out=dmy, in_=PF[a_], func=AF.Relu, scale=-1.0,
                bias=b30[:, 0:1]), aks + ["b30"], ["dmyA"])
            scl(lambda e, u=u: e.activation(out=wd[u], in_=dmy,
                                            func=AF.Exp, scale=-1.0),
                ["dmyA"], [f"wdA{u}", f"wdB{u}"])

        def b_cde(ui):
            q, gx = ui // 8, ui % 8
            u = ui % 2
            a2 = ui % 2
            ib = ui % 2
            qp = q % 2
            aks = [f"PS{2 * a2}", f"PS{2 * a2 + 1}"]
            vec(lambda e, u=u: e.tensor_tensor(
                out=xw[u].rearrange("p (h x) -> p h x", h=8),
                in0=xtmu[u].rearrange("p (h x) -> p h x", h=8),
                in1=wd[u][:, QC - 1::QC][:, :, None]
                .broadcast_to([128, 8, 64]), op=OP.mult),
                [f"wdA{u}", f"wdB{u}", f"xtmu{u}"], [f"xw{u}"])
            pool(lambda e, u=u: e.tensor_tensor(
                out=ysd.rearrange("p (h x) -> p h x", h=8),
                in0=xtmu[u].rearrange("p (h x) -> p h x", h=8),
                in1=D_b[:, (ui % 8) * 8:(ui % 8) * 8 + 8][:, :, None]
                .broadcast_to([128, 8, 64]), op=OP.mult),
                [f"xtmu{u}", "D_b"], ["ysd"])
            vec(lambda e, gx=gx, u=u: e.tensor_tensor(
                out=wt[u].rearrange("p (h t) -> p h t", h=8)[:, 0:4, :],
                in0=wd[u].rearrange("p (h t) -> p h t", h=8)[:, 0:4, :],
                in1=Gm[:, gx * 128:(gx + 1) * 128][:, None, :]
                .broadcast_to([128, 4, 128]),
                op=OP.mult),
                [f"wdA{u}", "Gm"], [f"wtA{u}"])
            pool(lambda e, gx=gx, u=u: e.tensor_tensor(
                out=wt[u].rearrange("p (h t) -> p h t", h=8)[:, 4:8, :],
                in0=wd[u].rearrange("p (h t) -> p h t", h=8)[:, 4:8, :],
                in1=Gm[:, gx * 128:(gx + 1) * 128][:, None, :]
                .broadcast_to([128, 4, 128]),
                op=OP.mult),
                [f"wdB{u}", "Gm"], [f"wtB{u}"])
            pool(lambda e, gx=gx, qp=qp: e.tensor_tensor(
                out=sdec.rearrange("p (h x) -> p h x", h=8),
                in0=Sst[:, gx * 512:(gx + 1) * 512]
                .rearrange("p (h x) -> p h x", h=8),
                in1=cumAq2[qp][:, gx * 8:(gx + 1) * 8][:, :, None]
                .broadcast_to([128, 8, 64]), op=OP.mult),
                [f"Sg{gx}", f"cumAq2_{qp}"], ["sdec"])
            for hl8 in range(8):
                h = gx * 8 + hl8
                mm(PF[2][:, ib * 512 + hl8 * 64:
                   ib * 512 + (hl8 + 1) * 64],
                   cq2x[qp][:, gx * 128:(gx + 1) * 128],
                   S_bf[:, h * 64:(h + 1) * 64],
                   [f"cq2x{qp}", f"Sbf{gx}"], f"PS{4 + ib}")
            for hl8 in range(8):
                mm(PF[a2][:, 512 + hl8 * 64:512 + (hl8 + 1) * 64],
                   btm2[:, gx * 128:(gx + 1) * 128],
                   xw[u][:, hl8 * 64:(hl8 + 1) * 64],
                   ["btm2", f"xw{u}"], aks[1])
            for hl8 in range(8):
                wk = f"wtA{u}" if hl8 < 4 else f"wtB{u}"
                mm(PF[a2][:, hl8 * 64:(hl8 + 1) * 64],
                   wt[u][:, hl8 * 128:(hl8 + 1) * 128],
                   xtmu[u][:, hl8 * 64:(hl8 + 1) * 64],
                   [wk, f"xtmu{u}"], aks[0])

        def b_asm(ui):
            q, gx = ui // 8, ui % 8
            u = ui % 2
            a2 = ui % 2
            ib = ui % 2
            qp = q % 2
            aks = [f"PS{2 * a2}", f"PS{2 * a2 + 1}"]
            dma("sync", lambda e, q=q, gx=gx, u=u:
                e.dma_start(out=ggt[u],
                            in_=gate_s[q * QC:(q + 1) * QC,
                                       gx * 512:(gx + 1) * 512]),
                [f"gated{2 * gx}_{q}", f"gated{2 * gx + 1}_{q}"],
                [f"ggt{u}"])
            q3 = q % 3
            vec(lambda e, gx=gx, u=u, ib=ib, q3=q3: e.tensor_tensor(
                out=ysb[u].rearrange("p (h x) -> p h x", h=8),
                in0=PF[2][:, ib * 512:ib * 512 + 512]
                .rearrange("p (h x) -> p h x", h=8),
                in1=eltq[q3][:, gx * 8:(gx + 1) * 8][:, :, None]
                .broadcast_to([128, 8, 64]), op=OP.mult),
                [f"PS{4 + ib}", f"eltq{q3}"], [f"ysb{u}"])
            vec(lambda e, u=u, a_=a2: e.tensor_tensor(
                out=ysb[u], in0=ysb[u], in1=PF[a_][:, 0:512],
                op=OP.add), [f"ysb{u}", aks[0]], [f"ysb{u}"])
            pool(lambda e, u=u: e.tensor_tensor(
                out=ysb[u], in0=ysb[u], in1=ysd,
                op=OP.add), [f"ysb{u}", "ysd"], [f"ysb{u}"])
            vec(lambda e, gx=gx, a_=a2: e.tensor_tensor(
                out=Sst[:, gx * 512:(gx + 1) * 512],
                in0=sdec, in1=PF[a_][:, 512:1024],
                op=OP.add),
                ["sdec", aks[1]], [f"Sg{gx}"])

        def b_fin(ui):
            q, gx = ui // 8, ui % 8
            u = ui % 2
            scl(lambda e, u=u: e.activation(out=sqscr, in_=ysb[u],
                                            func=AF.Square,
                                            accum_out=ssq[u]),
                [f"ysb{u}"], ["sqx", f"ssq{u}"])
            vec(lambda e, u=u: e.tensor_scalar(
                out=rstd[u], in0=ssq[u], scalar1=1.0 / 512.0,
                scalar2=EPS, op0=OP.mult, op1=OP.add),
                [f"ssq{u}"], [f"rstd{u}"])
            scl(lambda e, u=u: e.activation(out=rstd[u], in_=rstd[u],
                                            func=AF.Ln),
                [f"rstd{u}"], [f"rstd{u}"])
            scl(lambda e, u=u: e.activation(out=rstd[u], in_=rstd[u],
                                            func=AF.Exp, scale=-0.5),
                [f"rstd{u}"], [f"rstd{u}"])
            vec(lambda e, u=u: e.scalar_tensor_tensor(
                out=yng, in0=ysb[u], scalar=rstd[u][:, 0:1],
                in1=ggt[u], op0=OP.mult, op1=OP.mult),
                [f"ysb{u}", f"rstd{u}", f"ggt{u}"], ["yng"])
            dma("gpsimd", lambda e, gx=gx: e.dma_start(
                out=S_bf[:, gx * 512:(gx + 1) * 512],
                in_=Sst[:, gx * 512:(gx + 1) * 512]),
                [f"Sg{gx}"], [f"Sbf{gx}"])
            for c4 in range(4):
                tr(PBA[1][:, 512 + c4 * 128:512 + (c4 + 1) * 128],
                   yng[:, c4 * 128:(c4 + 1) * 128],
                   ident_bf, ["yng", "idb"], "PBT1")
            vec(lambda e, gx=gx, q=q: e.tensor_copy(
                panel.rearrange("p (c t) -> p c t", c=NKC)
                [:, gx * 4:gx * 4 + 4, q * QC:(q + 1) * QC],
                PBA[1][:, 512:1024]
                .rearrange("p (c t) -> p c t", c=4)),
                ["PBT1"],
                [f"pan{gx * 4 + i}" for i in range(4)])

        for b in range(2):
            vec(lambda e, b=b: e.memset(NRL15[b], 1.0), [],
                [f"NRq2x{b}"])
            for hh in range(2):
                dma("gpsimd", lambda e, b=b, hh=hh:
                    e.dma_start(out=ILx[b][0:12, hh * 512:(hh + 1) * 512],
                                in_=ind12),
                    [f"Lr32x{b}"], [f"Lr32x{b}"])
        load_qgrp(0)
        load_nrl(0)
        load_nr2(0)
        overlay(bq2x[0], range(32, 40), "bq2x0")
        overlay(cq2x[0], range(40, 48), "cq2x0")

        for k in range(NU + 3):
            if 2 <= k <= NU + 1:
                b_asm(k - 2)
            if k < NU:
                b_front_d(k)
                b_scal(k)
            if 1 <= k <= NU:
                b_cde(k - 1)
            if k < NU:
                b_front(k)
            if 3 <= k <= NU + 2:
                b_fin(k - 3)
        A.pop_fence()

        if dbg:
            dbg1 = nc.dram_tensor("dbg1", [128, NKC * T], BF16).ap()
            for kc in range(NKC):
                dma("sync", lambda e, k=kc: e.dma_start(
                    out=dbg1[:, k * T:(k + 1) * T],
                    in_=panel[:, k * T:(k + 1) * T]),
                    [f"pan{kc}"], [f"dbg1_{kc}"])

        A.push()
        wsl3 = [A.bf16(128, NKC * 128, f"wsl3{i}") for i in range(2)]
        xr3 = [A.f32(128, T, f"xr3{i}") for i in range(2)]
        x2b = [A.f32(128, T, f"x2b{i}") for i in range(2)]
        gi3 = 0
        for oc in range(NKC):
            wb3 = oc % 2
            dma("gpsimd", lambda e, o=oc, w=wb3:
                e.dma_start(out=wsl3[w],
                            in_=w_out[o].rearrange("p kc m -> p (kc m)")),
                [], [f"wsl3{wb3}"])
            dma("scalq", lambda e, o=oc, w=wb3:
                e.dma_start(out=xr3[w],
                            in_=x_fm[o * 128:(o + 1) * 128, :]),
                [], [f"xr3{wb3}"])
            for it, (t0, tw) in enumerate(TT):
                bank = gi3 % 4
                for kc in range(NKC):
                    mm(PSA[bank][:, 0:tw],
                       wsl3[wb3][:, kc * 128:(kc + 1) * 128],
                       panel[:, kc * T + t0:kc * T + t0 + tw],
                       [f"wsl3{wb3}", f"pan{kc}"], f"PS{bank}",
                       start=(kc == 0), stop=(kc == NKC - 1))
                vec(lambda e, w=wb3, t0=t0, tw=tw, b=bank:
                    e.tensor_tensor(out=x2b[w][:, t0:t0 + tw],
                                    in0=PSA[b][:, 0:tw],
                                    in1=xr3[w][:, t0:t0 + tw], op=OP.add),
                    [f"PS{bank}", f"xr3{wb3}"], [f"x2b{wb3}"])
                gi3 += 1
            dma("sync", lambda e, o=oc, w=wb3:
                e.dma_start(out=x2_s[o * 128:(o + 1) * 128, :],
                            in_=x2b[w]),
                [f"x2b{wb3}"], [f"x2d{oc}"])
        A.pop_fence()

        phase_ln("x2", "L2", router=True)

        A.push()
        wslB = [A.bf16(128, NKC * 128, f"wslB{i}") for i in range(2)]
        blora = A.bf16(128, I, "blorak")
        wrt_sb = A.f32(128, NKC * 8, "wrtk")
        qred = A.f32(128, 64, "qredk")
        tbf = A.bf16(128, T, "tbf")
        xrB = A.f32(128, T, "xrB")
        obuf = A.f32(128, T, "obuf")
        ot1 = A.f32(128, 512, "ot1")
        lgt = A.f32(2, 8, ("lgt", "lgto"))
        mx2 = A.f32(2, 1, "mx2")
        rt8 = A.f32(8, 1, "rt8")
        rteff = A.f32(128, 1, "rteffk")

        dma("gpsimd", lambda e:
            e.dma_start(out=wslB[0],
                        in_=w_la.rearrange("p kc m -> p (kc m)")),
            [], ["wslB0"])
        for it, (t0, tw) in enumerate(TT):
            bank = it % 2
            for kc in range(NKC):
                mm(PSA[bank][:, 0:tw],
                   wslB[0][:, kc * 128:(kc + 1) * 128],
                   panel[:, kc * T + t0:kc * T + t0 + tw],
                   ["wslB0", f"pan{kc}"], f"PS{bank}",
                   start=(kc == 0), stop=(kc == NKC - 1))
            vec(lambda e, t0=t0, tw=tw, b=bank: e.tensor_copy(
                tbf[:, t0:t0 + tw], PSA[b][:, 0:tw]),
                [f"PS{bank}"], ["tbf"])
        dma("gpsimd", lambda e: e.dma_start(out=blora, in_=w_lb),
            [], ["blorak"])
        dma("sync", lambda e: e.dma_start(out=wrt_sb, in_=w_rt),
            [], ["wrtk"])
        dma("sync", lambda e: e.dma_start(out=qred, in_=q_out.ap()),
            ["q_out"], ["qredk"])
        for kc in range(NKC):
            mm(PSA[2][0:2, 0:8],
               qred[:, kc:64:32], wrt_sb[:, kc * 8:(kc + 1) * 8],
               ["qredk", "wrtk"], "PS2",
               start=(kc == 0), stop=False)
        mm(PSA[2][0:2, 0:8], ones1f[:, 0:2], rb_sb,
           ["o1f", "rbk"], "PS2", start=False, stop=True)
        vec(lambda e: e.tensor_copy(lgt, PSA[2][0:2, 0:8]), ["PS2"], ["lgt"])
        vec(lambda e: e.reduce_max(mx2, lgt, axis=AX.X), ["lgt"], ["mx2"])
        vec(lambda e: e.tensor_scalar(out=lgt, in0=lgt,
                                      scalar1=mx2[:, 0:1], scalar2=None,
                                      op0=OP.subtract), ["lgt", "mx2"], ["lgt"])
        scl(lambda e: e.activation(out=lgt, in_=lgt, func=AF.Exp),
            ["lgt"], ["lgt"])
        vec(lambda e: e.reduce_sum(mx2, lgt, axis=AX.X), ["lgt"], ["mx2"])
        vec(lambda e: e.reciprocal(mx2, mx2), ["mx2"], ["mx2"])
        vec(lambda e: e.tensor_scalar(out=lgt, in0=lgt,
                                      scalar1=mx2[:, 0:1], scalar2=None,
                                      op0=OP.mult), ["lgt", "mx2"], ["lgt"])
        mm(PSA[3][0:1, 0:8], bm_sb, lgt, ["bm", "lgt"], "PS3")
        vec(lambda e: e.tensor_copy(lgt[0:1, 0:8], PSA[3][0:1, 0:8]),
            ["PS3"], ["lgto"])
        tr(PSA[2][0:8, 0:1], lgt[0:1, 0:8], ident_f[0:1, 0:1],
           ["lgto", "idf"], "PS2")
        vec(lambda e: e.tensor_copy(rt8, PSA[2][0:8, 0:1]), ["PS2"], ["rt8"])
        mm(PSA[3][:, 0:1], e816_sb, rt8, ["e816", "rt8"], "PS3")
        vec(lambda e: e.tensor_copy(rteff, PSA[3][:, 0:1]), ["PS3"], ["rteffk"])

        blsc = [A.bf16(128, 128, f"blsc{i}") for i in range(2)]
        giB = 0
        for oc in range(NKC):
            wbB = oc % 2
            vec(lambda e, o=oc, w=wbB: e.tensor_scalar(
                out=blsc[w], in0=blora[:, o * 128:(o + 1) * 128],
                scalar1=rteff[:, 0:1], scalar2=None, op0=OP.mult),
                ["blorak", "rteffk"], [f"blsc{wbB}"])
            dma("gpsimd", lambda e, o=oc, w=wbB:
                e.dma_start(out=wslB[w],
                            in_=w_ffn[o].rearrange("p kc m -> p (kc m)")),
                [], [f"wslB{wbB}"])
            dma("scalq", lambda e, o=oc:
                e.dma_start(out=xrB, in_=x2_s[o * 128:(o + 1) * 128, :]),
                [f"x2d{oc}"], ["xrB"])
            for it, (t0, tw) in enumerate(TT):
                bank = giB % 4
                for kc in range(NKC):
                    mm(PSA[bank][:, 0:tw],
                       wslB[wbB][:, kc * 128:(kc + 1) * 128],
                       panel[:, kc * T + t0:kc * T + t0 + tw],
                       [f"wslB{wbB}", f"pan{kc}"], f"PS{bank}",
                       start=(kc == 0), stop=False)
                mm(PSA[bank][:, 0:tw],
                   blsc[wbB],
                   tbf[:, t0:t0 + tw],
                   [f"blsc{wbB}", "tbf"], f"PS{bank}", start=False, stop=True)
                scl(lambda e, t0=t0, tw=tw, b=bank, o=oc:
                    e.activation(out=ot1[:, 0:tw], in_=PSA[b][:, 0:tw],
                                 func=AF.Identity, bias=fb_sb[:, o:o + 1]),
                    [f"PS{bank}", "fb"], ["ot1"])
                vec(lambda e, t0=t0, tw=tw:
                    e.tensor_tensor(out=obuf[:, t0:t0 + tw],
                                    in0=ot1[:, 0:tw],
                                    in1=xrB[:, t0:t0 + tw], op=OP.add),
                    ["ot1", "xrB"], ["obuf"])
                giB += 1
            dma("sync", lambda e, o=oc:
                e.dma_start(out=out_fm[o * 128:(o + 1) * 128, :],
                            in_=obuf),
                ["obuf"], [f"outd{oc}"])
        A.pop_fence()

        S_.add("gpsimd", lambda e: e.memset(pxbc[0:1, 0:1], 0.0),
               reads=[f"outd{o}" for o in range(NKC)], writes=[],
               sem="POO", amt=1)

        nops, cnts = S_.finalize(block, sems)
    return nc, nops


def _prep_weights(inputs, T):
    ip = np.asarray(inputs["L0_attn_in_proj_w"])
    W_inT = np.ascontiguousarray(ip.T.astype(np.float32))
    slabs = np.zeros((49, 128, NKC, 128), np.float32)
    dtslab = np.zeros((D, 128), np.float32)
    dtslab[:, 0:H] = W_inT[:, I + CONV:I + CONV + H]
    slabs[0] = dtslab.reshape(NKC, 128, 128).transpose(1, 0, 2)
    for c in range(NHBC):
        blk = W_inT[:, I + c * 128:I + (c + 1) * 128]
        slabs[c + 1] = blk.reshape(NKC, 128, 128).transpose(1, 0, 2)
    w_hbcdt = np.ascontiguousarray(slabs)

    w_gate = np.ascontiguousarray(W_inT[:, 0:I].reshape(NKC, 128, I))

    op_ = np.asarray(inputs["L0_attn_out_proj_w"])
    nwv = np.asarray(inputs["L0_attn_norm_weight"]).astype(np.float32)
    OT = np.ascontiguousarray(op_.T.astype(np.float32)
                              * nwv[:, None])
    w_out = np.zeros((NKC, 128, NKC, 128), np.float32)
    for o in range(NKC):
        blk = OT[:, o * 128:(o + 1) * 128]
        w_out[o] = blk.reshape(NKC, 128, 128).transpose(1, 0, 2)

    Wf = np.asarray(inputs["L0_ffn_W"])
    FT = np.ascontiguousarray(Wf.T.astype(np.float32))
    w_ffn = np.zeros((NKC, 128, NKC, 128), np.float32)
    for o in range(NKC):
        blk = FT[:, o * 128:(o + 1) * 128]
        w_ffn[o] = blk.reshape(NKC, 128, 128).transpose(1, 0, 2)

    Am = np.asarray(inputs["L0_ffn_A"]).reshape(128, D)
    AT = Am.T
    w_la = np.ascontiguousarray(
        AT.reshape(NKC, 128, 128).transpose(1, 0, 2))
    Bm = np.asarray(inputs["L0_ffn_B"])
    w_lb = np.ascontiguousarray(
        Bm.transpose(0, 2, 1).reshape(128, I))
    rw = np.asarray(inputs["L0_ffn_router_W"])
    w_rt = np.ascontiguousarray(
        rw.T.reshape(NKC, 128, 8).transpose(1, 0, 2).reshape(128, NKC * 8))

    cw = np.asarray(inputs["L0_attn_conv_weight"])
    diag = np.zeros((NHBC, KCV, 128, 128), np.float32)
    idx = np.arange(128)
    for c in range(NHBC):
        for k in range(KCV):
            diag[c, k, idx, idx] = cw[c * 128:(c + 1) * 128, k]
    cwc = np.ascontiguousarray(
        cw.reshape(NHBC, 128, KCV).transpose(1, 0, 2).reshape(128,
                                                              NHBC * KCV))
    cb = np.asarray(inputs["L0_attn_conv_bias"])
    cbias = np.ascontiguousarray(cb.reshape(NHBC, 128).T)

    nw = np.asarray(inputs["L0_attn_norm_weight"])
    nwc = np.ascontiguousarray(nw.reshape(NKC, 128).T)
    fbv = np.asarray(inputs["L0_ffn_b"])
    fbc = np.ascontiguousarray(fbv.reshape(NKC, 128).T)

    return dict(
        w_hbcdt=w_hbcdt, w_gate=w_gate, w_out=w_out, w_ffn=w_ffn,
        w_la=w_la, w_lb=w_lb, w_rt=w_rt, diagw=diag, cwcols=cwc,
        cbias=cbias, nw_cols=nwc, fb_cols=fbc,
        dtb=np.asarray(inputs["L0_attn_dt_bias"]).reshape(H, 1),
        alog=np.asarray(inputs["L0_attn_A_log"]).reshape(H, 1),
        dvec_row=np.asarray(inputs["L0_attn_D"]).reshape(1, H),
        rb_row=np.asarray(inputs["L0_ffn_router_b"]).reshape(1, 8),
    )


def _consts(T, qcount):
    identity = np.eye(128, dtype=np.float32)
    tri = np.triu(np.full((128, 128), np.exp(30.0), np.float32))
    ind12 = np.zeros((12, 512), np.float32)
    for blk in range(3):
        for k in range(4):
            ind12[blk * 4 + k, k * 128:(k + 1) * 128] = 1.0
    e816m = np.zeros((8, 128), np.float32)
    for e in range(8):
        e816m[e, e * 16:(e + 1) * 16] = SCALING
    return identity, tri, ind12, e816m


def kernel(**inputs):
    x = np.asarray(inputs["L0_attn_hidden_states"])
    Bb, L, _ = x.shape
    T = (Bb * L) // NCORES
    QS, QE = 611, L - 1
    qcount = QE - QS

    shared = _prep_weights(inputs, T)
    identity, tri, ind12, e816m = _consts(T, qcount)
    shared.update(identity=identity, tri01=tri, ind12=ind12, e816=e816m)

    nc, nops = build_program(T)

    in_maps = []
    for c in range(NCORES):
        b = 0 if c < NCORES // 2 else 1
        ci = c % (NCORES // 2)
        t0, t1 = ci * T, (ci + 1) * T
        m = dict(shared)
        m["x_fm"] = np.ascontiguousarray(x[b, t0:t1, :].T)
        tm = np.zeros((1, T), np.float32)
        lo, hi = max(QS, t0), min(QE, t1)
        if hi > lo:
            tm[0, lo - t0:hi - t0] = 1.0 / qcount
        m["tokmask"] = tm
        chmv = np.zeros((H, 64), np.float32)
        chm1v = np.zeros((H, 64), np.float32)
        lo_r, hi_r = (0, 4) if b == 0 else (4, 8)
        for r in range(8):
            contributes = (lo_r <= r < hi_r) and (r < c)
            if contributes:
                for i in range(8):
                    mid = (r < i < c)
                    chmv[:, r * 8 + i] = 1.0 if mid else 0.0
                    chm1v[:, r * 8 + i] = 0.0 if mid else 1.0
        m["chmh"] = chmv
        m["chm1mh"] = chm1v
        bc = np.zeros((4, 8), np.float32)
        if ci > 0:
            bc[:, c - 1] = 1.0
        m["bndc"] = bc
        bmv = np.zeros((2, 1), np.float32)
        bmv[b, 0] = 1.0
        m["bmask"] = bmv
        bs = np.zeros((128, 64), np.float32)
        bs[:, b * 32:(b + 1) * 32] = 1.0
        m["bsel"] = bs
        in_maps.append(m)

    res = run_bass_kernel_spmd(nc, in_maps, core_ids=list(range(NCORES)))
    out = np.empty((Bb, L, D), np.float32)
    for c in range(NCORES):
        b = 0 if c < NCORES // 2 else 1
        ci = c % (NCORES // 2)
        out[b, ci * T:(ci + 1) * T, :] = res.results[c]["out_fm"].T
    return out
